# version 05
# baseline (speedup 1.0000x reference)
import numpy as np
import concourse.bass as bass
import concourse.mybir as mybir
from concourse.bass_utils import run_bass_kernel_spmd
from concourse import library_config

F32 = mybir.dt.float32
BF16 = mybir.dt.bfloat16
I16 = mybir.dt.int16
U8 = mybir.dt.uint8
U32 = mybir.dt.uint32
AF = mybir.ActivationFunctionType
ALU = mybir.AluOpType
AX = mybir.AxisListType

NC = 8
B, S, D = 16, 2048, 1024
T = B * S
TPC = T // NC
BPC = B // NC
E, I = 8, 2048
CAP = 4096
NT = TPC // 128
EPS = 1e-5
KB = 1024

_CACHE = {}


class Sem:
    def __init__(self, nc, name):
        self.h = nc.alloc_semaphore(name)
        self.n = 0

    def inc(self, inst, k=1):
        inst.then_inc(self.h, k)
        self.n += k
        return self.n


def build(upto=99):
    import os as _os
    S3T = int(_os.environ.get("K_S3T", NT))
    nc = bass.Bass("TRN2", target_bir_lowering=False, num_devices=NC)

    def din(name, shape, dt):
        return nc.dram_tensor(name, shape, dt, kind="ExternalInput").ap()

    xs_d = din("xs", [TPC, D], F32)
    wvT_d = din("wvT", [D, D], F32)
    woT_d = din("woT", [D, D], F32)
    embT_d = din("embT", [D, E], F32)
    w1T_d = din("w1T", [D, I], F32)
    w2T_d = din("w2T", [I, D], F32)
    iden_d = din("iden", [128, 128], F32)
    ones_d = din("onescol", [128, 1], F32)
    onesrow_d = din("onesrow", [1, 128], F32)
    triu16_d = din("triu16", [16, 16], F32)
    triu32_d = din("triu32", [32, 32], F32)
    maskc_d = din("maskc32", [E, 32], F32)
    iota8_d = din("iota8", [128, 8], F32)
    iotaE_d = din("iotaEB", [32, 8 * 128], F32)
    iotaG_d = din("iotaG", [32, 8 * 128], I16)
    idxid_d = din("idxid", [128, 256], I16)
    idx8_d = din("idx8", [128, 1], I16)
    iotasl_d = din("iotasl", [16, 256], F32)
    m1_16_d = din("m1c16", [16, 2048], I16)
    m1_32_d = din("m1c32", [32, 8 * 128], I16)
    z16_d = din("z16", [16, 256], I16)

    out_d = nc.dram_tensor("out", [TPC, D], F32, kind="ExternalOutput").ap()
    DBG = bool(_os.environ.get("K_DBG"))
    if DBG:
        o_clist = nc.dram_tensor("o_clist", [16, 256], I16, kind="ExternalOutput").ap()
        o_dots = nc.dram_tensor("o_dots", [8, TPC], F32, kind="ExternalOutput").ap()
        o_pack = nc.dram_tensor("o_pack", [96, 128], F32, kind="ExternalOutput").ap()
        o_gb = nc.dram_tensor("o_gb", [32, 2 * 128], I16, kind="ExternalOutput").ap()
        o_we = nc.dram_tensor("o_we", [128, 2 * NT], F32, kind="ExternalOutput").ap()
        o_con = nc.dram_tensor("o_con", [128, D], BF16, kind="ExternalOutput").ap()
        o_yrow = nc.dram_tensor("o_yrow", [8, D], BF16, kind="ExternalOutput").ap()
        o_sb = nc.dram_tensor("o_sb", [32, 2 * 128], F32, kind="ExternalOutput").ap()
        o_tot = nc.dram_tensor("o_tot", [32, 2 * 8], F32, kind="ExternalOutput").ap()
        o_craw = nc.dram_tensor("o_craw", [16, 2048], I16, kind="ExternalOutput").ap()
        o_ecand = nc.dram_tensor("o_ecand", [16, 2048], I16, kind="ExternalOutput").ap()
        o_nfs = nc.dram_tensor("o_nfs", [1, 8], U32, kind="ExternalOutput").ap()
        o_nff = nc.dram_tensor("o_nff", [1, 4], U32, kind="ExternalOutput").ap()
        o_grank = nc.dram_tensor("o_grank", [32, 8 * 128], F32, kind="ExternalOutput").ap()
        o_csum = nc.dram_tensor("o_csum", [32, 8 * 128], F32, kind="ExternalOutput").ap()
        o_egr = nc.dram_tensor("o_egr", [16, 2048], F32, kind="ExternalOutput").ap()

    def dint(name, shape, dt, shared=False):
        return nc.dram_tensor(name, shape, dt, kind="Internal",
                              addr_space="Shared" if shared else "Local").ap()

    x1_hbm = dint("x1_hbm", [TPC, D], F32)
    yhi_hbm = dint("yhi_hbm", [TPC, D], BF16)
    ylo_hbm = dint("ylo_hbm", [TPC, D], BF16)
    r_bounce = dint("r_bounce", [BPC * D], F32)
    t_bounce = dint("t_bounce", [BPC * D], F32)
    cnt_in = dint("cnt_in", [1, E], F32)
    cnt_all = dint("cnt_all", [NC, E], F32)
    cand_in = dint("cand_in", [E, TPC], I16)
    cand_all = dint("cand_all", [NC * E, TPC], I16)
    cand_mine = dint("cand_mine", [NC * TPC], I16)
    y_all = dint("y_all", [T, D], BF16)
    contrib_in = dint("contrib_in", [CAP, D], BF16)
    contrib_all = dint("contrib_all", [E * CAP, D], BF16)
    g1_dram = dint("g1_dram", [TPC], I16)
    g2_dram = dint("g2_dram", [TPC], I16)

    A = nc.alloc_sbuf_tensor
    sb_iden = A("c_iden", [128, 128], F32)
    sb_ones = A("c_ones", [128, 1], F32)
    sb_onesrow = A("c_onesrow", [1, 128], F32)
    sb_triu16 = A("c_triu16", [16, 16], F32)
    sb_triu32 = A("c_triu32", [32, 32], F32)
    sb_maskc = A("c_maskc", [E, 32], F32)
    sb_iota8 = A("c_iota8", [128, 8], F32)
    sb_iotaE = A("c_iotaE", [32, 8, 128], F32)
    sb_iotaG = A("c_iotaG", [32, 8, 128], I16)
    sb_idxid = A("c_idxid", [128, 256], I16)
    sb_idx8 = A("c_idx8", [128, 1], I16)
    sb_iotasl = A("c_iotasl", [16, 256], F32)
    sb_m116 = A("c_m116", [16, 2048], I16)
    sb_m132 = A("c_m132", [32, 8, 128], I16)
    sb_z16 = A("c_z16", [16, 256], I16)
    sb_embT = A("c_embT", [128, 8, E], F32)
    sb_ehi = A("c_ehi", [128, 8, E], BF16)
    sb_elo = A("c_elo", [128, 8, E], BF16)
    sb_elof = A("c_elof", [128, 8, E], F32)

    sc_y = A("s_y", [128, D], F32)
    sc_y2 = A("s_y2", [128, D], F32)
    sc_hi = A("s_hi", [128, D], BF16)
    sc_lo = A("s_lo", [128, D], BF16)
    sc_sx = A("s_sx", [128, 1], F32)
    sc_sx2 = A("s_sx2", [128, 1], F32)
    sc_m = A("s_m", [128, 1], F32)
    sc_var = A("s_var", [128, 1], F32)
    sc_rs = A("s_rs", [128, 1], F32)
    sc_nb = A("s_nb", [128, 1], F32)
    r_row = A("s_rrow", [1, BPC * D], F32)
    t_sb = A("s_t", [BPC, D], F32)
    a_sb = A("s_a", [BPC, D], F32)
    a_row = A("s_arow", [1, BPC * D], F32)
    rfm = A("s_rfm", [128, 8, BPC], F32)
    tfm = A("s_tfm", [128, 8, BPC], F32)
    wv_tile = A("s_wv", [128, 2, D], F32)
    dots_tm = A("s_dotstm", [128, NT, 8], F32)
    mx8 = A("r_mx8", [128, 8], F32)
    m1r = A("r_m1r", [128, 8], F32)
    m2r = A("r_m2r", [128, 8], F32)
    m1t = A("r_m1t", [128, 8], F32)
    m2t = A("r_m2t", [128, 8], F32)
    cc8 = A("r_cc8", [128, 8], F32)
    packA = A("r_packA", [128, 96], F32)
    packB = A("r_packB", [96, 128], F32)
    ix2B0 = A("r_ix2B0", [32, 128], F32)
    d12B0 = A("r_d12B0", [32, 128], F32)
    totB = A("r_totB", [32, 8], F32)
    pfxT = A("r_pfxT", [32, 8], F32)
    cntsb = A("r_cnt", [1, 8], F32)
    cntall_sb = A("r_cntall", [8, 8], F32)
    sB1 = A("r_sB1", [32, 128], F32)
    sB2 = A("r_sB2", [32, 128], F32)
    w1B = A("r_w1B", [32, 128], F32)
    w2B = A("r_w2B", [32, 128], F32)
    weB = A("r_weB", [32, 2, 128], F32)
    gB = A("r_gB", [32, 2, 128], I16)
    we_tm = A("r_wetm", [128, 2, NT], F32)
    clist = A("e_clist", [16, 256], I16)
    spA = A("e_spA", [16, 512], I16)
    spB = A("e_spB", [16, 512], I16)
    spC = A("e_spC", [16, 512], I16)
    nf_s = A("e_nfs", [1, 8], U32)
    nf_sf = A("e_nfsf", [1, 1], F32)
    nf_sb = A("e_nf", [1, 1], U32)
    nf_f = A("e_nff", [1, 1], F32)
    nfb = A("e_nfb", [16, 1], F32)
    evalid = A("e_valid", [16, 256], U8)
    clrep = A("e_clrep", [128, 256], I16)
    g1il = A("e_g1il", [128, 256], I16)
    g2il = A("e_g2il", [128, 256], I16)
    rowg = A("e_rowg", [128, TPC], I16)
    sb_idg = A("s_idg", [128, 64], I16)

    ARENA = 124 * KB
    _guard = A("arena_guard", [128, ARENA], U8)
    abase = nc.lookup_mloc(_guard).addr

    def AT(name, shape, dt, off):
        n = int(np.prod(shape[1:])) * mybir.dt.size(dt)
        assert off + n <= ARENA, f"{name}: {off}+{n} > {ARENA}"
        return nc.alloc_sbuf_tensor_at(name, shape, dt, offset=abase + off)

    xt2 = AT("p_xt2", [128, 2, D], F32, 0)
    attn_bc = AT("p_attnbc", [128, BPC, D], F32, 8 * KB)
    yfm_hi = AT("p_yfmhi", [128, 8, 512], BF16, 0)
    yfm_lo = AT("p_yfmlo", [128, 8, 512], BF16, 8 * KB)
    dots_em = AT("p_dotsem", [8, TPC], F32, 64 * KB)
    m1B = AT("p_m1B", [32, 8, 128], F32, 80 * KB)
    m2B = AT("p_m2B", [32, 8, 128], F32, 84 * KB)
    onesB = AT("p_onesB", [32, 8, 128], F32, 88 * KB)
    csumB = AT("p_csumB", [32, 8, 128], F32, 92 * KB)
    grankB = AT("p_grankB", [32, 8, 128], F32, 96 * KB)
    survB = AT("p_survB", [32, 8, 128], F32, 100 * KB)
    scrB = AT("p_scrB", [32, 8, 128], F32, 104 * KB)
    candB = AT("p_candB", [32, 8, 128], I16, 108 * KB)
    w1t = AT("p_w1t", [128, 8, I], BF16, 0)
    w2t = AT("p_w2t", [128, 16, D], BF16, 32 * KB)
    craw = AT("p_craw", [16, 8, 256], I16, 64 * KB)
    ecand = AT("p_ecand", [16, 2048], I16, 68 * KB)
    emask = AT("p_emask", [16, 2048], F32, 72 * KB)
    eX1 = AT("p_eX1", [16, 2048], F32, 80 * KB)
    eX2 = AT("p_eX2", [16, 2048], F32, 88 * KB)
    yfm2 = AT("p_yfm2", [128, 2, 8, 512], BF16, 96 * KB)
    inter = AT("p_inter", [128, 16, 512], BF16, 64 * KB)
    ctile = AT("p_ctile", [128, 2, D], BF16, 80 * KB)
    c1b = AT("p_c1b", [128, 2, 4, D], BF16, 0)
    c2b = AT("p_c2b", [128, 2, 4, D], BF16, 16 * KB)
    x1t = AT("p_x1t", [128, 2, D], F32, 32 * KB)
    o1t = AT("p_o1t", [128, 2, D], F32, 40 * KB)
    o2t = AT("p_o2t", [128, 2, D], F32, 48 * KB)

    pb = [nc.alloc_psum_tensor(f"pb{i}", [128, 512], F32) for i in range(8)]

    s_x = Sem(nc, "s_x")
    s_v = Sem(nc, "s_v")
    s_s = Sem(nc, "s_s")
    s_t = Sem(nc, "s_t")
    s_g = Sem(nc, "s_g")
    s_st = Sem(nc, "s_st")
    s_o = Sem(nc, "s_o")
    s_cc = Sem(nc, "s_cc")

    V, SC, TE, GP, SY = nc.vector, nc.scalar, nc.tensor, nc.gpsimd, nc.sync

    def cumsum(out_ap, in_ap):
        return V.tensor_tensor_scan(out=out_ap, data0=in_ap, data1=in_ap,
                                    initial=0.0, op0=ALU.add, op1=ALU.bypass)

    for dst, src in [
        (sb_iden[:], iden_d), (sb_ones[:], ones_d), (sb_onesrow[:], onesrow_d),
        (sb_triu16[:], triu16_d), (sb_triu32[:], triu32_d), (sb_maskc[:], maskc_d),
        (sb_iota8[:], iota8_d), (sb_iotasl[:], iotasl_d),
        (sb_iotaE[:], iotaE_d.rearrange("a (b c) -> a b c", b=8)),
        (sb_iotaG[:], iotaG_d.rearrange("a (b c) -> a b c", b=8)),
        (sb_idxid[:], idxid_d), (sb_idx8[:], idx8_d),
        (sb_m116[:], m1_16_d),
        (sb_m132[:], m1_32_d.rearrange("a (b c) -> a b c", b=8)),
        (sb_z16[:], z16_d),
        (sb_embT[:], embT_d.rearrange("(dc p) e -> p dc e", p=128)),
    ]:
        s_x.inc(SY.dma_start(out=dst, in_=src), 16)
    X0 = s_x.n
    if _os.environ.get("K_AGEARLY"):
        GP.wait_ge(s_x.h, X0)
        s_cc.inc(GP.collective_compute(
            "AllGather", ALU.bypass, replica_groups=[list(range(NC))],
            ins=[cnt_in], outs=[cnt_all]))
    V.wait_ge(s_x.h, X0)
    V.tensor_copy(sb_ehi[:], sb_embT[:])
    V.drain()
    V.tensor_copy(sb_elof[:], sb_ehi[:])
    V.drain()
    V.tensor_tensor(sb_elo[:], sb_embT[:], sb_elof[:], ALU.subtract)
    v_emb = s_v.inc(V.drain())

    if upto < 1:
        SY.wait_ge(s_st.h, s_st.n)
        SY.wait_ge(s_x.h, s_x.n)
        SY.wait_ge(s_v.h, s_v.n)
        SY.wait_ge(s_g.h, s_g.n)
        SY.wait_ge(s_cc.h, s_cc.n)
        return nc
    t_cons = []
    for t in range(NT):
        b = t // 16
        par = t % 2
        first = (t % 16 == 0)
        last = (t % 16 == 15)
        ld = SY.dma_start(out=xt2[:, par, :], in_=xs_d[128 * t:128 * (t + 1), :])
        if t >= 2:
            ld.wait_op(s_t.h, t_cons[t - 2], "sem-ge")
        x_t = s_x.inc(ld, 16)
        xt = xt2[:, par, :]
        SC.wait_ge(s_x.h, x_t)
        sq_t = s_s.inc(SC.activation(out=sc_y2[:], in_=xt, func=AF.Square, accum_out=sc_sx2[:]))
        V.wait_ge(s_x.h, x_t)
        V.reduce_sum(sc_sx[:], xt, axis=AX.X)
        V.drain()
        V.tensor_scalar_mul(sc_m[:], sc_sx[:], 1.0 / D)
        V.drain()
        V.tensor_tensor(sc_var[:], sc_m[:], sc_m[:], ALU.mult)
        V.wait_ge(s_s.h, sq_t)
        V.drain()
        V.scalar_tensor_tensor(sc_var[:], sc_sx2[:], 1.0 / D, sc_var[:], ALU.mult, ALU.subtract)
        V.drain()
        V.tensor_scalar_add(sc_var[:], sc_var[:], EPS)
        v_var = s_v.inc(V.drain())
        SC.wait_ge(s_v.h, v_var)
        SC.activation(out=sc_rs[:], in_=sc_var[:], func=AF.Sqrt, bias=0.0, scale=1.0)
        rs_t = s_s.inc(SC.drain())
        V.wait_ge(s_s.h, rs_t)
        V.reciprocal(sc_rs[:], sc_rs[:])
        V.drain()
        V.tensor_tensor(sc_nb[:], sc_m[:], sc_rs[:], ALU.mult)
        V.drain()
        v_nb = s_v.inc(V.tensor_scalar_mul(sc_nb[:], sc_nb[:], -1.0))
        SC.wait_ge(s_v.h, v_nb)
        y_t = s_s.inc(SC.activation(out=sc_y[:], in_=xt, func=AF.Identity, bias=sc_nb[:], scale=sc_rs[:]))
        TE.wait_ge(s_s.h, y_t)
        TE.matmul(pb[2 * b][0:1, :], sb_ones[:], sc_y[:, 0:512], start=first, stop=last)
        t_r = s_t.inc(TE.matmul(pb[2 * b + 1][0:1, :], sb_ones[:], sc_y[:, 512:1024], start=first, stop=last))
        t_cons.append(t_r)
        SC.wait_ge(s_t.h, t_r)
        V.wait_ge(s_t.h, t_r)

    V.wait_ge(s_t.h, s_t.n)
    for b in range(BPC):
        V.tensor_copy(r_row[0:1, D * b:D * b + 512], pb[2 * b][0:1, :])
        V.tensor_copy(r_row[0:1, D * b + 512:D * (b + 1)], pb[2 * b + 1][0:1, :])
    v_r = s_v.inc(V.drain())

    if upto < 2:
        SY.wait_ge(s_st.h, s_st.n)
        SY.wait_ge(s_x.h, s_x.n)
        SY.wait_ge(s_v.h, s_v.n)
        SY.wait_ge(s_g.h, s_g.n)
        SY.wait_ge(s_cc.h, s_cc.n)
        return nc
    def matvec_fm(src_rows, bounce, fm_tile, w_dram, ps_lo, ps_hi, prod_sem_val):
        SY.wait_ge(s_v.h, prod_sem_val)
        s_st.inc(SY.dma_start(out=bounce.rearrange("(o a) -> o a", o=1), in_=src_rows), 16)
        SY.wait_ge(s_st.h, s_st.n)
        for b_ in range(BPC):
            s_x.inc(SY.dma_start(out=fm_tile[:, :, b_],
                                 in_=bounce.rearrange("(b k p) -> b p k", p=128, k=8)[b_],
                                 allow_slow_non_contiguous=True), 16)
        t_base = s_t.n
        x_base = s_x.n
        for dc in range(8):
            ld = SY.dma_start(out=wv_tile[:, dc % 2, :], in_=w_dram[128 * dc:128 * (dc + 1), :])
            if dc >= 2:
                ld.wait_op(s_t.h, t_base + dc - 1, "sem-ge")
            s_x.inc(ld, 16)
        for dc in range(8):
            TE.wait_ge(s_x.h, x_base + 16 * (dc + 1))
            TE.matmul(ps_lo[0:BPC, :], fm_tile[:, dc, :], wv_tile[:, dc % 2, 0:512],
                      start=(dc == 0), stop=(dc == 7))
            s_t.inc(TE.matmul(ps_hi[0:BPC, :], fm_tile[:, dc, :], wv_tile[:, dc % 2, 512:1024],
                              start=(dc == 0), stop=(dc == 7)))

    matvec_fm(r_row[:], r_bounce, rfm, wvT_d, pb[4], pb[5], v_r)
    V.wait_ge(s_t.h, s_t.n)
    V.tensor_copy(t_sb[:, 0:512], pb[4][0:BPC, :])
    V.tensor_copy(t_sb[:, 512:1024], pb[5][0:BPC, :])
    v_t = s_v.inc(V.drain())
    matvec_fm(t_sb[:], t_bounce, tfm, woT_d, pb[6], pb[7], v_t)
    V.wait_ge(s_t.h, s_t.n)
    V.tensor_copy(a_sb[:, 0:512], pb[6][0:BPC, :])
    V.tensor_copy(a_sb[:, 512:1024], pb[7][0:BPC, :])
    v_a = s_v.inc(V.drain())
    SY.wait_ge(s_v.h, v_a)
    s_st.inc(SY.dma_start(out=t_bounce.rearrange("(b d) -> b d", b=BPC), in_=a_sb[:]), 16)
    SY.wait_ge(s_st.h, s_st.n)
    x_ar = s_x.inc(SY.dma_start(out=a_row[:], in_=t_bounce.rearrange("(o a) -> o a", o=1)), 16)
    TE.wait_ge(s_x.h, x_ar)
    for b in range(BPC):
        TE.matmul(pb[0][:], sb_onesrow[:], a_row[0:1, D * b:D * b + 512], start=True, stop=True)
        t_bc = s_t.inc(TE.matmul(pb[1][:], sb_onesrow[:], a_row[0:1, D * b + 512:D * b + 1024], start=True, stop=True))
        V.wait_ge(s_t.h, t_bc)
        V.tensor_copy(attn_bc[:, b, 0:512], pb[0][:])
        v_bc = s_v.inc(V.tensor_copy(attn_bc[:, b, 512:1024], pb[1][:]))
        TE.wait_ge(s_v.h, v_bc)
    v_abc = s_v.inc(V.drain())

    if upto < 3:
        SY.wait_ge(s_st.h, s_st.n)
        SY.wait_ge(s_x.h, s_x.n)
        SY.wait_ge(s_v.h, s_v.n)
        SY.wait_ge(s_g.h, s_g.n)
        SY.wait_ge(s_cc.h, s_cc.n)
        return nc
    V.wait_ge(s_v.h, v_abc)
    tile_done = []
    for t in range(S3T):
        b = t // 16
        par = t % 2
        ld = SY.dma_start(out=xt2[:, par, :], in_=xs_d[128 * t:128 * (t + 1), :])
        if t >= 2:
            ld.wait_op(s_st.h, tile_done[t - 2], "sem-ge")
        x_t = s_x.inc(ld, 16)
        xt = xt2[:, par, :]
        V.wait_ge(s_x.h, x_t)
        V.tensor_tensor(xt, xt, attn_bc[:, b, :], ALU.add)
        v_x1 = s_v.inc(V.drain())
        s_st.inc(SY.dma_start(out=x1_hbm[128 * t:128 * (t + 1), :], in_=xt).wait_op(s_v.h, v_x1, "sem-ge"), 16)
        SC.wait_ge(s_v.h, v_x1)
        sq_t = s_s.inc(SC.activation(out=sc_y2[:], in_=xt, func=AF.Square, accum_out=sc_sx2[:]))
        V.reduce_sum(sc_sx[:], xt, axis=AX.X)
        V.drain()
        V.tensor_scalar_mul(sc_m[:], sc_sx[:], 1.0 / D)
        V.drain()
        V.tensor_tensor(sc_var[:], sc_m[:], sc_m[:], ALU.mult)
        V.wait_ge(s_s.h, sq_t)
        V.drain()
        V.scalar_tensor_tensor(sc_var[:], sc_sx2[:], 1.0 / D, sc_var[:], ALU.mult, ALU.subtract)
        V.drain()
        V.tensor_scalar_add(sc_var[:], sc_var[:], EPS)
        v_var = s_v.inc(V.drain())
        SC.wait_ge(s_v.h, v_var)
        SC.activation(out=sc_rs[:], in_=sc_var[:], func=AF.Sqrt, bias=0.0, scale=1.0)
        rs_t = s_s.inc(SC.drain())
        V.wait_ge(s_s.h, rs_t)
        V.reciprocal(sc_rs[:], sc_rs[:])
        V.drain()
        V.tensor_tensor(sc_nb[:], sc_m[:], sc_rs[:], ALU.mult)
        V.drain()
        v_nb = s_v.inc(V.tensor_scalar_mul(sc_nb[:], sc_nb[:], -1.0))
        SC.wait_ge(s_v.h, v_nb)
        y_t = s_s.inc(SC.activation(out=sc_y[:], in_=xt, func=AF.Identity, bias=sc_nb[:], scale=sc_rs[:]))
        V.wait_ge(s_s.h, y_t)
        V.tensor_copy(sc_hi[:], sc_y[:])
        v_hi = s_v.inc(V.drain())
        V.tensor_tensor(sc_lo[:], sc_y[:], sc_hi[:], ALU.subtract)
        v_lo = s_v.inc(V.drain())
        s_st.inc(SY.dma_start(out=yhi_hbm[128 * t:128 * (t + 1), :], in_=sc_hi[:]).wait_op(s_v.h, v_hi, "sem-ge"), 16)
        st_t = s_st.inc(SY.dma_start(out=ylo_hbm[128 * t:128 * (t + 1), :], in_=sc_lo[:]).wait_op(s_v.h, v_lo, "sem-ge"), 16)
        tile_done.append(st_t)
        V.wait_ge(s_st.h, st_t)
        SC.wait_ge(s_st.h, st_t)
    ST3 = s_st.n

    if not _os.environ.get("K_NOGPW"):
        GP.wait_ge(s_st.h, ST3)
    if _os.environ.get("K_NOCC"):
        cc_y = 0
    elif _os.environ.get("K_TINYAG"):
        cc_y = s_cc.inc(GP.collective_compute(
            "AllGather", ALU.bypass, replica_groups=[list(range(NC))],
            ins=[cnt_in], outs=[cnt_all]))
    else:
        cc_y = s_cc.inc(GP.collective_compute(
            "AllGather", ALU.bypass, replica_groups=[list(range(NC))],
            ins=[yhi_hbm], outs=[y_all]))

    if upto < 4:
        SY.wait_ge(s_st.h, s_st.n)
        SY.wait_ge(s_x.h, s_x.n)
        SY.wait_ge(s_v.h, s_v.n)
        SY.wait_ge(s_g.h, s_g.n)
        SY.wait_ge(s_cc.h, s_cc.n)
        return nc
    GP.load_library(library_config.mlp)
    V.wait_ge(s_v.h, v_emb)
    dots_mm = []
    for g in range(0 if _os.environ.get("K_NODG") else 8):
        GP.wait_ge(s_st.h, ST3)
        if g >= 1:
            GP.wait_ge(s_t.h, dots_mm[-1])
        s_g.inc(GP.dma_gather(
            yfm_hi[:], yhi_hbm, sb_idxid[:, 32 * g:32 * (g + 1)], 512, 512, D,
            elem_step=D, transpose=True), 16)
        g_lo = s_g.inc(GP.dma_gather(
            yfm_lo[:], ylo_hbm, sb_idxid[:, 32 * g:32 * (g + 1)], 512, 512, D,
            elem_step=D, transpose=True), 16)
        TE.wait_ge(s_g.h, g_lo)
        ps = pb[2 + g % 2]
        TE.wait_ge(s_v.h, s_v.n)
        n_mm = 0
        t_d = None
        for (yt, et) in ((yfm_hi, sb_ehi), (yfm_hi, sb_elo), (yfm_lo, sb_ehi), (yfm_lo, sb_elo)):
            for dc in range(8):
                inst = TE.matmul(ps[0:8, :], et[:, dc, :], yt[:, dc, :],
                                 start=(n_mm == 0), stop=(n_mm == 31))
                n_mm += 1
                if n_mm == 32:
                    t_d = s_t.inc(inst)
        dots_mm.append(t_d)
        V.wait_ge(s_t.h, t_d)
        s_v.inc(V.tensor_copy(dots_em[:, 512 * g:512 * (g + 1)], ps[0:8, :]))

    if dots_mm:
        GP.wait_ge(s_t.h, dots_mm[-1])
    w_ffn = s_g.n
    if not _os.environ.get("K_NOW"):
        for icg in range(8):
            s_g.inc(GP.dma_start(out=w1t[:, icg, :],
                                 in_=w1T_d.rearrange("(dc p) i -> p dc i", p=128)[:, icg, :]), 16)
        for icg in range(16):
            s_g.inc(GP.dma_start(out=w2t[:, icg, :], in_=w2T_d[128 * icg:128 * (icg + 1), :]), 16)
        w_ffn = s_g.n

    if upto < 5:
        SY.wait_ge(s_st.h, s_st.n)
        SY.wait_ge(s_x.h, s_x.n)
        SY.wait_ge(s_v.h, s_v.n)
        SY.wait_ge(s_g.h, s_g.n)
        SY.wait_ge(s_cc.h, s_cc.n)
        return nc
    V.wait_ge(s_v.h, s_v.n)
    for t in range(NT):
        ps = pb[4 + t % 2]
        t_tr = s_t.inc(TE.transpose(ps[0:128, 0:8], dots_em[:, 128 * t:128 * (t + 1)], sb_iden[0:8, 0:8]))
        V.wait_ge(s_t.h, t_tr)
        v_tr = s_v.inc(V.tensor_copy(dots_tm[:, t, :], ps[0:128, 0:8]))
        TE.wait_ge(s_v.h, v_tr)
    for t in range(NT):
        dt_ = dots_tm[:, t, :]
        V.max(out=mx8[:], in_=dt_)
        V.drain()
        V.tensor_tensor(m1r[:], dt_, mx8[:, 0:1].to_broadcast([128, 8]), ALU.is_equal)
        V.drain()
        cumsum(cc8[:], m1r[:])
        V.drain()
        V.tensor_scalar(cc8[:], cc8[:], 1.0, None, ALU.is_le)
        V.drain()
        V.tensor_tensor(m1t[:], m1r[:], cc8[:], ALU.mult)
        V.tensor_tensor(m2r[:], dt_, mx8[:, 1:2].to_broadcast([128, 8]), ALU.is_equal)
        V.drain()
        V.tensor_tensor(m2r[:], m2r[:], m1t[:], ALU.subtract)
        V.drain()
        V.tensor_scalar(m2r[:], m2r[:], 0.0, None, ALU.max)
        V.drain()
        cumsum(cc8[:], m2r[:])
        V.drain()
        V.tensor_scalar(cc8[:], cc8[:], 1.0, None, ALU.is_le)
        V.drain()
        V.tensor_tensor(m2t[:], m2r[:], cc8[:], ALU.mult)
        V.drain()
        V.tensor_tensor(cc8[:], m1t[:], sb_iota8[:], ALU.mult)
        V.drain()
        V.reduce_sum(packA[:, t:t + 1], cc8[:], axis=AX.X)
        V.drain()
        V.tensor_tensor(cc8[:], m2t[:], sb_iota8[:], ALU.mult)
        V.drain()
        V.reduce_sum(packA[:, 32 + t:32 + t + 1], cc8[:], axis=AX.X)
        V.tensor_tensor(packA[:, 64 + t:64 + t + 1], mx8[:, 0:1], mx8[:, 1:2], ALU.subtract)
        V.drain()
    v_pack = s_v.inc(V.drain())
    TE.wait_ge(s_v.h, v_pack)
    t_pk = s_t.inc(TE.transpose(pb[6][0:96, 0:128], packA[:, 0:96], sb_iden[:]))
    V.wait_ge(s_t.h, t_pk)
    v_pk = s_v.inc(V.tensor_copy(packB[:], pb[6][0:96, 0:128]))
    SY.wait_ge(s_v.h, v_pk)
    s_x.inc(SY.dma_start(out=ix2B0[:], in_=packB[32:64, :]), 16)
    s_x.inc(SY.dma_start(out=d12B0[:], in_=packB[64:96, :]), 16)
    x_pk = s_x.n
    ix1B = packB[0:32, :]
    ix2B = ix2B0[:]
    d12B = d12B0[:]
    SC.wait_ge(s_x.h, x_pk)
    V.wait_ge(s_x.h, x_pk)
    SC.wait_ge(s_v.h, v_pk)
    SC.activation(out=w1B[:], in_=d12B, func=AF.Sigmoid, bias=0.0, scale=1.0)
    s_s.inc(SC.activation(out=w2B[:], in_=d12B, func=AF.Sigmoid, bias=0.0, scale=-1.0))
    V.wait_ge(s_v.h, v_pk)
    V.tensor_tensor(m1B[:], ix1B.unsqueeze(1).to_broadcast([32, 8, 128]), sb_iotaE[:], ALU.is_equal)
    V.tensor_tensor(m2B[:], ix2B.unsqueeze(1).to_broadcast([32, 8, 128]), sb_iotaE[:], ALU.is_equal)
    V.drain()
    V.tensor_tensor(onesB[:], m1B[:], m2B[:], ALU.add)
    V.drain()
    for e in range(E):
        cumsum(csumB[:, e, :], onesB[:, e, :])
        V.drain()
    V.tensor_copy(totB[:], csumB[:, :, 127])
    v_tot = s_v.inc(V.drain())
    TE.wait_ge(s_v.h, v_tot)
    t_cnt = s_t.inc(TE.matmul(pb[7][0:1, 0:8], sb_ones[0:32, :], totB[:], start=True, stop=True))
    V.wait_ge(s_t.h, t_cnt)
    v_cnt = s_v.inc(V.tensor_copy(cntsb[:], pb[7][0:1, 0:8]))
    SY.wait_ge(s_v.h, v_cnt)
    s_st.inc(SY.dma_start(out=cnt_in, in_=cntsb[:]), 16)
    GP.wait_ge(s_st.h, s_st.n)
    cc_cnt = s_cc.inc(GP.collective_compute(
        "AllGather", ALU.bypass, replica_groups=[list(range(NC))],
        ins=[cnt_in], outs=[cnt_all]))
    V.tensor_scalar(survB[:], onesB[:], 0.5, None, ALU.is_gt)
    v_su = s_v.inc(V.drain())
    sU8 = AT("p_sU8", [32, 8, 128], U8, 112 * KB)
    V.tensor_copy(sU8[:], survB[:])
    V.drain()
    V.select(candB[:], sU8[:], sb_iotaG[:], sb_m132[:])
    v_cand = s_v.inc(V.drain())
    SY.wait_ge(s_v.h, v_cand)
    s_st.inc(SY.dma_start(out=cand_in.rearrange("e (p f) -> p e f", p=32), in_=candB[:]), 16)
    GP.wait_ge(s_st.h, s_st.n)
    cc_cand = s_cc.inc(GP.collective_compute(
        "AllGather", ALU.bypass, replica_groups=[list(range(NC))],
        ins=[cand_in], outs=[cand_all]))
    SY.wait_ge(s_cc.h, cc_cnt)
    x_cnt = s_x.inc(SY.dma_start(out=cntall_sb[:], in_=cnt_all), 16)
    TE.wait_ge(s_x.h, x_cnt)
    TE.wait_ge(s_v.h, v_cnt)
    TE.matmul(pb[7][0:32, 0:8], sb_triu32[:], totB[:], start=True, stop=False)
    t_pfx = s_t.inc(TE.matmul(pb[7][0:32, 0:8], sb_maskc[:], cntall_sb[:], start=False, stop=True))
    V.wait_ge(s_t.h, t_pfx)
    v_pfx = s_v.inc(V.tensor_copy(pfxT[:], pb[7][0:32, 0:8]))
    V.wait_ge(s_v.h, v_pfx)
    V.tensor_tensor(grankB[:], csumB[:], pfxT[:].unsqueeze(2).to_broadcast([32, 8, 128]), ALU.add)
    v_gr_tap = s_v.inc(V.drain())
    if DBG:
        s_o.inc(SY.dma_start(out=o_grank, in_=grankB[:].rearrange("a b c -> a (b c)")).wait_op(s_v.h, v_gr_tap, "sem-ge"), 16)
        s_o.inc(SY.dma_start(out=o_csum, in_=csumB[:].rearrange("a b c -> a (b c)")).wait_op(s_v.h, v_gr_tap, "sem-ge"), 16)
    V.tensor_scalar(survB[:], grankB[:], float(CAP), None, ALU.is_le)
    V.drain()
    V.tensor_tensor(survB[:], survB[:], onesB[:], ALU.mult)
    V.drain()
    def sum_e(dst, msk, val):
        V.tensor_tensor(scrB[:], msk, val, ALU.mult)
        V.drain()
        V.tensor_tensor(dst, scrB[:, 0, :], scrB[:, 1, :], ALU.add)
        for e_ in range(2, E):
            V.drain()
            V.tensor_tensor(dst, dst, scrB[:, e_, :], ALU.add)
        V.drain()

    sum_e(sB1[:], m1B[:], grankB[:])
    sum_e(sB2[:], m2B[:], grankB[:])
    V.tensor_scalar(sB1[:], sB1[:], -1.0, None, ALU.add)
    V.tensor_scalar(sB2[:], sB2[:], -1.0, None, ALU.add)
    V.drain()
    V.tensor_scalar(sB1[:], sB1[:], float(CAP - 1), None, ALU.min)
    V.tensor_scalar(sB2[:], sB2[:], float(CAP - 1), None, ALU.min)
    V.drain()
    V.scalar_tensor_tensor(gB[:, 0, :], ix1B, float(CAP), sB1[:], ALU.mult, ALU.add)
    V.scalar_tensor_tensor(gB[:, 1, :], ix2B, float(CAP), sB2[:], ALU.mult, ALU.add)
    v_g = s_v.inc(V.drain())
    sum_e(sB1[:], m1B[:], survB[:])
    sum_e(sB2[:], m2B[:], survB[:])
    V.wait_ge(s_s.h, s_s.n)
    V.tensor_tensor(weB[:, 0, :], w1B[:], sB1[:], ALU.mult)
    V.tensor_tensor(weB[:, 1, :], w2B[:], sB2[:], ALU.mult)
    v_we = s_v.inc(V.drain())
    TE.wait_ge(s_v.h, v_we)
    for k in range(2):
        t_we = s_t.inc(TE.transpose(pb[6][0:128, 0:32], weB[:, k, :], sb_iden[0:32, 0:32]))
        V.wait_ge(s_t.h, t_we)
        v_wt = s_v.inc(V.tensor_copy(we_tm[:, k, :], pb[6][0:128, 0:32]))
        TE.wait_ge(s_v.h, v_wt)
    SY.wait_ge(s_v.h, v_g)
    s_st.inc(SY.dma_start(out=g1_dram.rearrange("(p f) -> p f", p=32), in_=gB[:, 0, :]), 16)
    st_g = s_st.inc(SY.dma_start(out=g2_dram.rearrange("(p f) -> p f", p=32), in_=gB[:, 1, :]), 16)
    SY.wait_ge(s_st.h, st_g)
    for k in range(8):
        s_x.inc(SY.dma_start(out=g1il[16 * k:16 * (k + 1), :],
                             in_=g1_dram.rearrange("(f p) -> p f", p=16),
                             allow_slow_non_contiguous=True), 16)
        s_x.inc(SY.dma_start(out=g2il[16 * k:16 * (k + 1), :],
                             in_=g2_dram.rearrange("(f p) -> p f", p=16),
                             allow_slow_non_contiguous=True), 16)
    x_gil = s_x.n

    if upto < 6:
        SY.wait_ge(s_st.h, s_st.n)
        SY.wait_ge(s_x.h, s_x.n)
        SY.wait_ge(s_v.h, s_v.n)
        SY.wait_ge(s_g.h, s_g.n)
        SY.wait_ge(s_cc.h, s_cc.n)
        return nc
    GP.wait_ge(s_cc.h, cc_cand)
    g_rows = s_g.inc(GP.dma_gather(
        rowg[:].rearrange("p (c n) -> p c n", c=1), cand_all, sb_idx8[:], 8, 8, TPC,
        elem_step=TPC, transpose=False), 16)
    SY.wait_ge(s_g.h, g_rows)
    st_cm = s_st.inc(SY.dma_start(out=cand_mine.rearrange("(r n) -> r n", r=8), in_=rowg[0:8, :]), 16)
    if _os.environ.get("K_E6") == "1":
        SY.wait_ge(s_st.h, s_st.n)
        SY.wait_ge(s_x.h, s_x.n)
        SY.wait_ge(s_v.h, s_v.n)
        SY.wait_ge(s_g.h, s_g.n)
        SY.wait_ge(s_cc.h, s_cc.n)
        return nc
    SY.wait_ge(s_st.h, st_cm)
    x_cr = s_x.inc(SY.dma_start(out=craw[:],
                                in_=cand_mine.rearrange("(blk fl p) -> p blk fl", p=16, fl=256),
                                allow_slow_non_contiguous=True).wait_op(s_v.h, v_we, "sem-ge"), 16)
    if _os.environ.get("K_E6") == "2":
        SY.wait_ge(s_st.h, s_st.n)
        SY.wait_ge(s_x.h, s_x.n)
        SY.wait_ge(s_v.h, s_v.n)
        SY.wait_ge(s_g.h, s_g.n)
        SY.wait_ge(s_cc.h, s_cc.n)
        return nc
    V.wait_ge(s_x.h, x_cr)
    V.wait_ge(s_v.h, v_we)
    crawf = craw[:].rearrange("p a b -> p (a b)")
    if DBG:
        s_o.inc(SY.dma_start(out=o_craw, in_=crawf).wait_op(s_x.h, x_cr, "sem-ge"), 16)
    V.tensor_scalar(emask[:], crawf, 0.0, None, ALU.is_ge)
    v_em2 = s_v.inc(V.drain())
    TE.wait_ge(s_v.h, v_em2)
    for q in range(4):
        s_t.inc(TE.matmul(pb[q][0:1, :], sb_ones[0:16, :], emask[:, 512 * q:512 * (q + 1)],
                          start=True, stop=True))
    V.wait_ge(s_t.h, s_t.n)
    for q in range(4):
        V.tensor_copy(eX1[0:1, 512 * q:512 * (q + 1)], pb[q][0:1, :])
    V.drain()
    cumsum(eX2[0:1, :], eX1[0:1, :])
    V.drain()
    v_col = s_v.inc(V.tensor_tensor(eX2[0:1, :], eX2[0:1, :], eX1[0:1, :], ALU.subtract))
    TE.wait_ge(s_v.h, v_col)
    for q in range(4):
        TE.matmul(pb[4 + (q % 2)][0:16, :], sb_triu16[:], emask[:, 512 * q:512 * (q + 1)],
                  start=True, stop=False)
        t_p = s_t.inc(TE.matmul(pb[4 + (q % 2)][0:16, :], sb_onesrow[0:1, 0:16],
                                eX2[0:1, 512 * q:512 * (q + 1)], start=False, stop=True))
        V.wait_ge(s_t.h, t_p)
        v_ep = s_v.inc(V.tensor_copy(eX1[:, 512 * q:512 * (q + 1)], pb[4 + (q % 2)][0:16, :]))
        TE.wait_ge(s_v.h, v_ep)
    V.tensor_tensor(eX1[:], eX1[:], emask[:], ALU.add)
    V.drain()
    V.tensor_scalar(eX2[:], eX1[:], float(CAP), None, ALU.is_le)
    V.drain()
    V.tensor_tensor(eX2[:], eX2[:], emask[:], ALU.mult)
    V.drain()
    eU8 = eX1[:].bitcast(U8)[:, 0:2048]
    V.tensor_copy(eU8, eX2[:])
    V.drain()
    v_ec = s_v.inc(V.select(ecand[:], eU8, crawf, sb_m116[:]))
    if DBG:
        s_o.inc(SY.dma_start(out=o_ecand, in_=ecand[:]).wait_op(s_v.h, v_ec, "sem-ge"), 16)
        s_o.inc(SY.dma_start(out=o_egr, in_=eX2[:]).wait_op(s_v.h, v_ec, "sem-ge"), 16)
    if _os.environ.get("K_E6") == "3":
        SY.wait_ge(s_st.h, s_st.n)
        SY.wait_ge(s_x.h, s_x.n)
        SY.wait_ge(s_v.h, s_v.n)
        SY.wait_ge(s_g.h, s_g.n)
        SY.wait_ge(s_cc.h, s_cc.n)
        return nc
    GP.load_library(library_config.sparse_gather)
    GP.wait_ge(s_v.h, v_ec)

    def spg(in_ap, out_ap, nf_cell, fill_ap):
        g_ = s_g.inc(GP.sparse_gather(out=out_ap, in_=in_ap, num_found=nf_cell))
        V.wait_ge(s_g.h, g_)
        v1 = s_v.inc(V.tensor_copy(nf_sf[:], nf_cell))
        TE.wait_ge(s_v.h, v1)
        t1 = s_t.inc(TE.matmul(pb[6][0:16, 0:1], sb_onesrow[0:1, 0:16], nf_sf[:], start=True, stop=True))
        V.wait_ge(s_t.h, t1)
        V.tensor_copy(nfb[:], pb[6][0:16, 0:1])
        V.drain()
        V.tensor_tensor(evalid[:], sb_iotasl[:], nfb[:].to_broadcast([16, 256]), ALU.is_ge)
        V.drain()
        v2 = s_v.inc(V.copy_predicated(out_ap, evalid[:], fill_ap))
        GP.wait_ge(s_v.h, v2)
        return v2

    for q in range(4):
        spg(ecand[:, 512 * q:512 * (q + 1)],
            (spA if q < 2 else spB)[:, 256 * (q % 2):256 * (q % 2 + 1)],
            nf_s[0:1, q:q + 1], sb_m116[:, 0:256])
    spg(spA[:], spC[:, 0:256], nf_s[0:1, 4:5], sb_m116[:, 0:256])
    spg(spB[:], spC[:, 256:512], nf_s[0:1, 5:6], sb_m116[:, 0:256])
    v_cl0 = spg(spC[:], clist[:], nf_sb[:], sb_z16[:])
    GP.load_library(library_config.mlp)
    SY.wait_ge(s_v.h, v_cl0)
    for k in range(8):
        s_x.inc(SY.dma_start(out=clrep[16 * k:16 * (k + 1), :], in_=clist[:]), 16)
    x_clr = s_x.n

    if _os.environ.get("K_E6") == "5":
        SY.wait_ge(s_st.h, s_st.n)
        SY.wait_ge(s_x.h, s_x.n)
        SY.wait_ge(s_v.h, s_v.n)
        SY.wait_ge(s_g.h, s_g.n)
        SY.wait_ge(s_cc.h, s_cc.n)
        return nc
    if DBG:
        GP.wait_ge(s_o.h, s_o.n)
    GP.wait_ge(s_cc.h, cc_y)
    GP.wait_ge(s_x.h, x_clr)
    relu_done = {}
    st_ct = []
    gath_y = []
    mm1_last = {}
    for tg in range(8):
        gpar = tg % 2
        if tg >= 2:
            GP.wait_ge(s_t.h, mm1_last[tg - 2])
        g_y = s_g.inc(GP.dma_gather(
            yfm2[:, gpar], y_all, clrep[:, 32 * tg:32 * (tg + 1)], 512, 512, D,
            elem_step=D, transpose=True), 16)
        gath_y.append(g_y)
        TE.wait_ge(s_g.h, max(g_y, w_ffn))
        for ic in range(16):
            ps = pb[ic % 2]
            if (tg, ic - 2) in relu_done:
                TE.wait_ge(s_s.h, relu_done[(tg, ic - 2)])
            elif tg > 0 and ic < 2:
                TE.wait_ge(s_s.h, relu_done[(tg - 1, 14 + ic)])
            t_i = None
            for dc in range(8):
                inst = TE.matmul(ps[:], w1t[:, dc, 128 * ic:128 * (ic + 1)], yfm2[:, gpar, dc, :],
                                 start=(dc == 0), stop=(dc == 7))
                if dc == 7:
                    t_i = s_t.inc(inst)
            if ic == 15:
                mm1_last[tg] = t_i
            SC.wait_ge(s_t.h, t_i)
            relu_done[(tg, ic)] = s_s.inc(
                SC.activation(out=inter[:, ic, :], in_=ps[:], func=AF.Relu, bias=0.0, scale=1.0))
        TE.wait_ge(s_s.h, relu_done[(tg, 15)])
        for sub in range(4):
            cpar = sub % 2
            for hh in range(2):
                ps = pb[4 + 2 * cpar + hh]
                t_o = None
                for ic in range(16):
                    inst = TE.matmul(ps[:], inter[:, ic, 128 * sub:128 * (sub + 1)],
                                     w2t[:, ic, 512 * hh:512 * (hh + 1)],
                                     start=(ic == 0), stop=(ic == 15))
                    if ic == 15:
                        t_o = s_t.inc(inst)
                V.wait_ge(s_t.h, t_o)
                if len(st_ct) >= 2:
                    V.wait_ge(s_st.h, st_ct[-2])
                v_o = s_v.inc(V.tensor_copy(ctile[:, cpar, 512 * hh:512 * (hh + 1)], ps[:]))
                TE.wait_ge(s_v.h, v_o)
            row0 = 512 * tg + 128 * sub
            st_ct.append(s_st.inc(SY.dma_start(
                out=contrib_in[row0:row0 + 128, :],
                in_=ctile[:, cpar, :]).wait_op(s_v.h, s_v.n, "sem-ge"), 16))
    GP.wait_ge(s_st.h, s_st.n)
    cc_con = s_cc.inc(GP.collective_compute(
        "AllGather", ALU.bypass, replica_groups=[list(range(NC))],
        ins=[contrib_in], outs=[contrib_all]))

    if upto < 7:
        SY.wait_ge(s_st.h, s_st.n)
        SY.wait_ge(s_x.h, s_x.n)
        SY.wait_ge(s_v.h, s_v.n)
        SY.wait_ge(s_g.h, s_g.n)
        SY.wait_ge(s_cc.h, s_cc.n)
        return nc
    GP.wait_ge(s_cc.h, cc_con)
    GP.wait_ge(s_x.h, x_gil)
    fin = []
    for g in range(8):
        gpar = g % 2
        if g >= 2:
            GP.wait_ge(s_v.h, fin[4 * (g - 2) + 3])
        g_c1 = s_g.inc(GP.dma_gather(
            c1b[:, gpar], contrib_all, g1il[:, 32 * g:32 * (g + 1)], 512, 512, D,
            elem_step=D, transpose=False), 16)
        g_c2 = s_g.inc(GP.dma_gather(
            c2b[:, gpar], contrib_all, g2il[:, 32 * g:32 * (g + 1)], 512, 512, D,
            elem_step=D, transpose=False), 16)
        V.wait_ge(s_g.h, g_c2)
        SC.wait_ge(s_g.h, g_c2)
        for tt in range(4):
            t = 4 * g + tt
            par = tt % 2
            ld = SY.dma_start(out=x1t[:, par, :], in_=x1_hbm[128 * t:128 * (t + 1), :])
            if len(fin) >= 2:
                ld.wait_op(s_v.h, fin[-2], "sem-ge")
            x_t = s_x.inc(ld, 16)
            V.wait_ge(s_x.h, x_t)
            if len(fin) >= 2:
                V.wait_ge(s_o.h, s_o.n)
                SC.wait_ge(s_v.h, fin[-2])
            V.tensor_tensor(o1t[:, par, :], c1b[:, gpar, tt, :],
                            we_tm[:, 0, t:t + 1].to_broadcast([128, D]), ALU.mult)
            V.drain()
            v_c1 = s_v.inc(V.tensor_tensor(o1t[:, par, :], o1t[:, par, :], x1t[:, par, :], ALU.add))
            SC.activation(out=o2t[:, par, :], in_=c2b[:, gpar, tt, :], func=AF.Identity, bias=0.0,
                          scale=we_tm[:, 1, t:t + 1])
            sc_c2 = s_s.inc(SC.drain())
            V.wait_ge(s_s.h, sc_c2)
            V.wait_ge(s_v.h, v_c1)
            v_f = s_v.inc(V.tensor_tensor(o1t[:, par, :], o1t[:, par, :], o2t[:, par, :], ALU.add))
            fin.append(v_f)
            s_o.inc(SY.dma_start(out=out_d[128 * t:128 * (t + 1), :],
                                 in_=o1t[:, par, :]).wait_op(s_v.h, v_f, "sem-ge"), 16)
    if DBG:
        SY.wait_ge(s_v.h, s_v.n)
        SY.wait_ge(s_g.h, s_g.n)
        s_o.inc(SY.dma_start(out=o_clist, in_=clist[:]), 16)
        s_o.inc(SY.dma_start(out=o_dots, in_=dots_em[:]), 16)
        s_o.inc(SY.dma_start(out=o_pack, in_=packB[:]), 16)
        s_o.inc(SY.dma_start(out=o_gb, in_=gB[:].rearrange("a b c -> a (b c)")), 16)
        s_o.inc(SY.dma_start(out=o_we, in_=we_tm[:].rearrange("a b c -> a (b c)")), 16)
        s_o.inc(SY.dma_start(out=o_con, in_=contrib_in[0:128, :]), 16)
        s_o.inc(SY.dma_start(out=o_yrow, in_=y_all[0:8, :]), 16)
        s_o.inc(SY.dma_start(out=o_sb[:, 0:128], in_=sB1[:]), 16)
        s_o.inc(SY.dma_start(out=o_sb[:, 128:256], in_=sB2[:]), 16)
        s_o.inc(SY.dma_start(out=o_tot[:, 0:8], in_=totB[:]), 16)
        s_o.inc(SY.dma_start(out=o_tot[:, 8:16], in_=pfxT[:]), 16)
        s_o.inc(SY.dma_start(out=o_nfs, in_=nf_s[:]), 16)
        s_o.inc(SY.dma_start(out=o_nff[0:1, 0:1], in_=nf_sb[:]), 16)
    SY.wait_ge(s_o.h, s_o.n)
    return nc


def _host_consts():
    c = {}
    c["iden"] = np.eye(128, dtype=np.float32)
    c["onescol"] = np.ones((128, 1), np.float32)
    c["onesrow"] = np.ones((1, 128), np.float32)
    c["triu16"] = np.triu(np.ones((16, 16), np.float32), 1)
    c["triu32"] = np.triu(np.ones((32, 32), np.float32), 1)
    c["iota8"] = np.tile(np.arange(8, dtype=np.float32), (128, 1))
    c["iotaEB"] = np.ascontiguousarray(
        np.tile(np.arange(8, dtype=np.float32)[None, :, None], (32, 1, 128)).reshape(32, -1))
    idxid = np.zeros((16, 256), np.int16)
    for i_ in range(4096):
        idxid[i_ % 16, i_ // 16] = i_
    c["idxid"] = np.tile(idxid, (8, 1))
    iotasl = np.zeros((16, 256), np.float32)
    for j in range(4096):
        iotasl[j % 16, j // 16] = j
    c["iotasl"] = iotasl
    c["m1c16"] = np.full((16, 2048), -1, np.int16)
    c["m1c32"] = np.full((32, 8 * 128), -1, np.int16)
    c["z16"] = np.zeros((16, 256), np.int16)
    return c


def kernel(L0_attn_x, L0_attn_w_qkv, L0_attn_w_o, L0_ffn_expert_embeddings,
           L0_ffn_first_linear, L0_ffn_second_linear):
    x = np.ascontiguousarray(np.asarray(L0_attn_x, dtype=np.float32))
    wqkv = np.asarray(L0_attn_w_qkv, dtype=np.float32)
    wo = np.asarray(L0_attn_w_o, dtype=np.float32)
    emb = np.asarray(L0_ffn_expert_embeddings, dtype=np.float32)
    w1 = np.asarray(L0_ffn_first_linear, dtype=np.float32)
    w2 = np.asarray(L0_ffn_second_linear, dtype=np.float32)

    wvT = np.ascontiguousarray(wqkv[2 * D:3 * D, :].T)
    woT = np.ascontiguousarray(wo.T)
    embT = np.ascontiguousarray(emb.T)
    consts = _host_consts()

    import os as _os
    _upto = int(_os.environ.get("K_UPTO", 99))
    if _CACHE.get("upto") != _upto:
        _CACHE.clear()
        _CACHE["upto"] = _upto
        _CACHE["nc"] = build(_upto)
        mybir.codegen_inst_isa_subclasses(_CACHE["nc"])
    nc = _CACHE["nc"]

    xr = x.reshape(T, D)
    in_maps = []
    for c in range(NC):
        iotaG = (4096 * c + 128 * np.arange(32)[:, None, None] +
                 np.arange(128)[None, None, :] +
                 np.zeros((1, 8, 1), np.int64)).astype(np.int16).reshape(32, -1)
        maskc = np.zeros((E, 32), np.float32)
        maskc[:c, :] = 1.0
        idx8 = np.full((128, 1), -1, np.int16)
        for k in range(8):
            for j in range(8):
                idx8[16 * k + j, 0] = 8 * j + c
        m = dict(
            xs=np.ascontiguousarray(xr[TPC * c:TPC * (c + 1)]),
            wvT=wvT, woT=woT, embT=embT,
            w1T=np.ascontiguousarray(w1[c].T), w2T=np.ascontiguousarray(w2[c].T),
            iotaG=np.ascontiguousarray(iotaG), maskc32=maskc, idx8=idx8,
            **consts,
        )
        in_maps.append(m)
    res = run_bass_kernel_spmd(nc, in_maps, core_ids=list(range(NC)))
    out = np.concatenate([np.asarray(r["out"]) for r in res.results], axis=0)
    return out.reshape(B, S, D)


# version 06
# speedup vs baseline: 1.0491x; 1.0491x over previous
import numpy as np
import concourse.bass as bass
import concourse.mybir as mybir
from concourse.bass_utils import run_bass_kernel_spmd
from concourse import library_config

F32 = mybir.dt.float32
BF16 = mybir.dt.bfloat16
I16 = mybir.dt.int16
U8 = mybir.dt.uint8
U32 = mybir.dt.uint32
AF = mybir.ActivationFunctionType
ALU = mybir.AluOpType
AX = mybir.AxisListType

NC = 8
B, S, D = 16, 2048, 1024
T = B * S
TPC = T // NC
BPC = B // NC
E, I = 8, 2048
CAP = 4096
NT = TPC // 128
EPS = 1e-5
KB = 1024

_CACHE = {}


class Sem:
    def __init__(self, nc, name):
        self.h = nc.alloc_semaphore(name)
        self.n = 0

    def inc(self, inst, k=1):
        inst.then_inc(self.h, k)
        self.n += k
        return self.n


def build(upto=99):
    import os as _os
    S3T = int(_os.environ.get("K_S3T", NT))
    nc = bass.Bass("TRN2", target_bir_lowering=False, num_devices=NC)

    def din(name, shape, dt):
        return nc.dram_tensor(name, shape, dt, kind="ExternalInput").ap()

    xs_d = din("xs", [TPC, D], F32)
    wvT_d = din("wvT", [D, D], F32)
    woT_d = din("woT", [D, D], F32)
    embT_d = din("embT", [D, E], F32)
    w1T_d = din("w1T", [D, I], F32)
    w2T_d = din("w2T", [I, D], F32)
    iden_d = din("iden", [128, 128], F32)
    ones_d = din("onescol", [128, 1], F32)
    onesrow_d = din("onesrow", [1, 128], F32)
    triu16_d = din("triu16", [16, 16], F32)
    triu32_d = din("triu32", [32, 32], F32)
    maskc_d = din("maskc32", [E, 32], F32)
    iota8_d = din("iota8", [128, 8], F32)
    iota8m_d = din("iota8m", [128, 8], F32)
    iotaE_d = din("iotaEB", [32, 8 * 128], F32)
    iotaG_d = din("iotaG", [32, 8 * 128], I16)
    idxid_d = din("idxid", [128, 256], I16)
    idx8_d = din("idx8", [128, 1], I16)
    iotasl_d = din("iotasl", [16, 256], F32)
    m1_16_d = din("m1c16", [16, 2048], I16)
    m1_32_d = din("m1c32", [32, 8 * 128], I16)
    z16_d = din("z16", [16, 256], I16)

    out_d = nc.dram_tensor("out", [TPC, D], F32, kind="ExternalOutput").ap()
    DBG = bool(_os.environ.get("K_DBG"))
    if DBG:
        o_clist = nc.dram_tensor("o_clist", [16, 256], I16, kind="ExternalOutput").ap()
        o_dots = nc.dram_tensor("o_dots", [8, TPC], F32, kind="ExternalOutput").ap()
        o_pack = nc.dram_tensor("o_pack", [96, 128], F32, kind="ExternalOutput").ap()
        o_gb = nc.dram_tensor("o_gb", [32, 2 * 128], I16, kind="ExternalOutput").ap()
        o_we = nc.dram_tensor("o_we", [128, 2 * NT], F32, kind="ExternalOutput").ap()
        o_con = nc.dram_tensor("o_con", [128, D], BF16, kind="ExternalOutput").ap()
        o_yrow = nc.dram_tensor("o_yrow", [8, D], BF16, kind="ExternalOutput").ap()
        o_sb = nc.dram_tensor("o_sb", [32, 2 * 128], F32, kind="ExternalOutput").ap()
        o_tot = nc.dram_tensor("o_tot", [32, 2 * 8], F32, kind="ExternalOutput").ap()
        o_craw = nc.dram_tensor("o_craw", [16, 2048], I16, kind="ExternalOutput").ap()
        o_ecand = nc.dram_tensor("o_ecand", [16, 2048], I16, kind="ExternalOutput").ap()
        o_nfs = nc.dram_tensor("o_nfs", [1, 8], U32, kind="ExternalOutput").ap()
        o_nff = nc.dram_tensor("o_nff", [1, 4], U32, kind="ExternalOutput").ap()
        o_grank = nc.dram_tensor("o_grank", [32, 8 * 128], F32, kind="ExternalOutput").ap()
        o_csum = nc.dram_tensor("o_csum", [32, 8 * 128], F32, kind="ExternalOutput").ap()
        o_egr = nc.dram_tensor("o_egr", [16, 2048], F32, kind="ExternalOutput").ap()
        o_x1 = nc.dram_tensor("o_x1", [128, D], F32, kind="ExternalOutput").ap()
        o_arow = nc.dram_tensor("o_arow", [1, BPC * D], F32, kind="ExternalOutput").ap()

    def dint(name, shape, dt, shared=False):
        return nc.dram_tensor(name, shape, dt, kind="Internal",
                              addr_space="Shared" if shared else "Local").ap()

    x1_hbm = dint("x1_hbm", [TPC, D], F32)
    yhi_hbm = dint("yhi_hbm", [TPC, D], BF16)
    ylo_hbm = dint("ylo_hbm", [TPC, D], BF16)
    r_bounce = dint("r_bounce", [BPC * D], F32)
    t_bounce = dint("t_bounce", [BPC * D], F32)
    cnt_in = dint("cnt_in", [1, E], F32)
    cnt_all = dint("cnt_all", [NC, E], F32)
    cand_in = dint("cand_in", [E, TPC], I16)
    cand_all = dint("cand_all", [NC * E, TPC], I16)
    cand_mine = dint("cand_mine", [NC * TPC], I16)
    y_all = dint("y_all", [T, D], BF16)
    contrib_in = dint("contrib_in", [CAP, D], BF16)
    contrib_all = dint("contrib_all", [E * CAP, D], BF16)
    g1_dram = dint("g1_dram", [TPC], I16)
    g2_dram = dint("g2_dram", [TPC], I16)

    A = nc.alloc_sbuf_tensor
    sb_iden = A("c_iden", [128, 128], F32)
    sb_ones = A("c_ones", [128, 1], F32)
    sb_onesrow = A("c_onesrow", [1, 128], F32)
    sb_triu16 = A("c_triu16", [16, 16], F32)
    sb_triu32 = A("c_triu32", [32, 32], F32)
    sb_maskc = A("c_maskc", [E, 32], F32)
    sb_iota8 = A("c_iota8", [128, 8], F32)
    sb_iota8m = A("c_iota8m", [128, 8], F32)
    sb_iotaE = A("c_iotaE", [32, 8, 128], F32)
    sb_iotaG = A("c_iotaG", [32, 8, 128], I16)
    sb_idxid = A("c_idxid", [128, 256], I16)
    sb_idx8 = A("c_idx8", [128, 1], I16)
    sb_iotasl = A("c_iotasl", [16, 256], F32)
    sb_m116 = A("c_m116", [16, 2048], I16)
    sb_m132 = A("c_m132", [32, 8, 128], I16)
    sb_z16 = A("c_z16", [16, 256], I16)
    sb_embT = A("c_embT", [128, 8, E], F32)
    sb_ehi = A("c_ehi", [128, 8, E], BF16)
    sb_elo = A("c_elo", [128, 8, E], BF16)
    sb_elof = A("c_elof", [128, 8, E], F32)

    sc_y = A("s_y", [128, D], F32)
    sc_y2 = A("s_y2", [128, D], F32)
    sc_hi = A("s_hi", [128, D], BF16)
    sc_lo = A("s_lo", [128, D], BF16)
    sc_lo2 = A("s_lo2", [128, D], BF16)
    sc_lo3 = A("s_lo3", [128, D], BF16)
    sxC = A("s_sxC", [128, NT], F32)
    sx2C = A("s_sx2C", [128, NT], F32)
    mC = A("s_mC", [128, NT], F32)
    varC = A("s_varC", [128, NT], F32)
    rsC = A("s_rsC", [128, NT], F32)
    nbC = A("s_nbC", [128, NT], F32)
    r3a = A("s_r3a", [128, NT, 8], F32)
    r3b = A("s_r3b", [128, NT, 8], F32)
    mx1C = A("s_mx1C", [128, NT], F32)
    mx2C = A("s_mx2C", [128, NT], F32)
    r_row = A("s_rrow", [1, BPC * D], F32)
    t_sb = A("s_t", [BPC, D], F32)
    a_sb = A("s_a", [BPC, D], F32)
    a_row = A("s_arow", [1, BPC * D], F32)
    rfm = A("s_rfm", [128, 8, BPC], F32)
    tfm = A("s_tfm", [128, 8, BPC], F32)
    wv_tile = A("s_wv", [128, 2, D], F32)
    dots_tm = A("s_dotstm", [128, NT, 8], F32)
    mx8 = A("r_mx8", [128, 8], F32)
    m1r = A("r_m1r", [128, 8], F32)
    m2r = A("r_m2r", [128, 8], F32)
    m1t = A("r_m1t", [128, 8], F32)
    m2t = A("r_m2t", [128, 8], F32)
    cc8 = A("r_cc8", [128, 8], F32)
    packA = A("r_packA", [128, 96], F32)
    packB = A("r_packB", [96, 128], F32)
    ix2B0 = A("r_ix2B0", [32, 128], F32)
    d12B0 = A("r_d12B0", [32, 128], F32)
    totB = A("r_totB", [32, 8], F32)
    pfxT = A("r_pfxT", [32, 8], F32)
    cntsb = A("r_cnt", [1, 8], F32)
    cntall_sb = A("r_cntall", [8, 8], F32)
    sB1 = A("r_sB1", [32, 128], F32)
    sB2 = A("r_sB2", [32, 128], F32)
    w1B = A("r_w1B", [32, 128], F32)
    w2B = A("r_w2B", [32, 128], F32)
    weB = A("r_weB", [32, 2, 128], F32)
    gB = A("r_gB", [32, 2, 128], I16)
    we_tm = A("r_wetm", [128, 2, NT], F32)
    clist = A("e_clist", [16, 256], I16)
    spA = A("e_spA", [16, 512], I16)
    spB = A("e_spB", [16, 512], I16)
    spC = A("e_spC", [16, 512], I16)
    nf_s = A("e_nfs", [1, 8], U32)
    nf_sf = A("e_nfsf", [1, 1], F32)
    nf_sb = A("e_nf", [1, 1], U32)
    nf_f = A("e_nff", [1, 1], F32)
    nfb = A("e_nfb", [16, 1], F32)
    evalid = A("e_valid", [16, 256], U8)
    clrep = A("e_clrep", [128, 256], I16)
    g1il = A("e_g1il", [128, 256], I16)
    g2il = A("e_g2il", [128, 256], I16)
    rowg = A("e_rowg", [128, TPC], I16)
    sb_idg = A("s_idg", [128, 64], I16)

    ARENA = 120 * KB
    _guard = A("arena_guard", [128, ARENA], U8)
    abase = nc.lookup_mloc(_guard).addr

    def AT(name, shape, dt, off):
        n = int(np.prod(shape[1:])) * mybir.dt.size(dt)
        assert off + n <= ARENA, f"{name}: {off}+{n} > {ARENA}"
        return nc.alloc_sbuf_tensor_at(name, shape, dt, offset=abase + off)

    xt2 = AT("p_xt2", [128, 2, D], F32, 0)
    attn_bc = AT("p_attnbc", [128, BPC, D], F32, 8 * KB)
    yfm_hi = AT("p_yfmhi", [128, 2, 8, 512], BF16, 0)
    yfm_lo = AT("p_yfmlo", [128, 2, 8, 512], BF16, 16 * KB)
    dots_em = AT("p_dotsem", [8, TPC], F32, 64 * KB)
    m1B = AT("p_m1B", [32, 8, 128], F32, 80 * KB)
    m2B = AT("p_m2B", [32, 8, 128], F32, 84 * KB)
    onesB = AT("p_onesB", [32, 8, 128], F32, 88 * KB)
    csumB = AT("p_csumB", [32, 8, 128], F32, 92 * KB)
    grankB = AT("p_grankB", [32, 8, 128], F32, 96 * KB)
    survB = AT("p_survB", [32, 8, 128], F32, 100 * KB)
    scrB = AT("p_scrB", [32, 8, 128], F32, 104 * KB)
    candB = AT("p_candB", [32, 8, 128], I16, 108 * KB)
    w1t = AT("p_w1t", [128, 8, I], BF16, 0)
    w2t = AT("p_w2t", [128, 16, D], BF16, 32 * KB)
    craw = AT("p_craw", [16, 8, 256], I16, 64 * KB)
    ecand = AT("p_ecand", [16, 2048], I16, 68 * KB)
    emask = AT("p_emask", [16, 2048], F32, 72 * KB)
    eX1 = AT("p_eX1", [16, 2048], F32, 80 * KB)
    eX2 = AT("p_eX2", [16, 2048], F32, 88 * KB)
    yfm2 = AT("p_yfm2", [128, 2, 8, 512], BF16, 96 * KB)
    inter = AT("p_inter", [128, 16, 512], BF16, 64 * KB)
    ctile = AT("p_ctile", [128, 2, D], BF16, 80 * KB)
    c1b = AT("p_c1b", [128, 2, 4, D], BF16, 0)
    c2b = AT("p_c2b", [128, 2, 4, D], BF16, 16 * KB)
    cxg = AT("p_cxg", [128, 4, D], F32, 32 * KB)
    o1g = AT("p_o1g", [128, 4, D], F32, 48 * KB)
    o2g = AT("p_o2g", [128, 4, D], F32, 64 * KB)

    pb = [nc.alloc_psum_tensor(f"pb{i}", [128, 512], F32) for i in range(8)]

    s_x = Sem(nc, "s_x")
    s_v = Sem(nc, "s_v")
    s_s = Sem(nc, "s_s")
    s_t = Sem(nc, "s_t")
    s_g = Sem(nc, "s_g")
    s_st = Sem(nc, "s_st")
    s_o = Sem(nc, "s_o")
    s_cc = Sem(nc, "s_cc")

    V, SC, TE, GP, SY = nc.vector, nc.scalar, nc.tensor, nc.gpsimd, nc.sync

    def cumsum(out_ap, in_ap):
        return V.tensor_tensor_scan(out=out_ap, data0=in_ap, data1=in_ap,
                                    initial=0.0, op0=ALU.add, op1=ALU.bypass)

    for dst, src in [
        (sb_iden[:], iden_d), (sb_ones[:], ones_d), (sb_onesrow[:], onesrow_d),
        (sb_triu16[:], triu16_d), (sb_triu32[:], triu32_d), (sb_maskc[:], maskc_d),
        (sb_iota8[:], iota8_d), (sb_iota8m[:], iota8m_d), (sb_iotasl[:], iotasl_d),
        (sb_iotaE[:], iotaE_d.rearrange("a (b c) -> a b c", b=8)),
        (sb_iotaG[:], iotaG_d.rearrange("a (b c) -> a b c", b=8)),
        (sb_idxid[:], idxid_d), (sb_idx8[:], idx8_d),
        (sb_m116[:], m1_16_d),
        (sb_m132[:], m1_32_d.rearrange("a (b c) -> a b c", b=8)),
        (sb_z16[:], z16_d),
        (sb_embT[:], embT_d.rearrange("(dc p) e -> p dc e", p=128)),
    ]:
        s_x.inc(SY.dma_start(out=dst, in_=src), 16)
    X0 = s_x.n
    if _os.environ.get("K_AGEARLY"):
        GP.wait_ge(s_x.h, X0)
        s_cc.inc(GP.collective_compute(
            "AllGather", ALU.bypass, replica_groups=[list(range(NC))],
            ins=[cnt_in], outs=[cnt_all]))
    V.wait_ge(s_x.h, X0)
    V.tensor_copy(sb_ehi[:], sb_embT[:])
    V.drain()
    V.tensor_copy(sb_elof[:], sb_ehi[:])
    V.drain()
    V.tensor_tensor(sb_elo[:], sb_embT[:], sb_elof[:], ALU.subtract)
    v_emb = s_v.inc(V.drain())

    if upto < 1:
        SY.wait_ge(s_st.h, s_st.n)
        SY.wait_ge(s_x.h, s_x.n)
        SY.wait_ge(s_v.h, s_v.n)
        SY.wait_ge(s_g.h, s_g.n)
        SY.wait_ge(s_cc.h, s_cc.n)
        return nc
    passA_done = []
    for t in range(NT):
        par = t % 2
        ld = SY.dma_start(out=xt2[:, par, :], in_=xs_d[128 * t:128 * (t + 1), :])
        if t >= 2:
            ld.wait_op(s_v.h, passA_done[t - 2], "sem-ge")
        x_t = s_x.inc(ld, 16)
        xt = xt2[:, par, :]
        SC.wait_ge(s_x.h, x_t)
        sq_t = s_s.inc(SC.activation(out=sc_y2[:], in_=xt, func=AF.Square, accum_out=sx2C[:, t:t + 1]))
        V.wait_ge(s_s.h, sq_t)
        passA_done.append(s_v.inc(V.reduce_sum(sxC[:, t:t + 1], xt, axis=AX.X)))
    V.wait_ge(s_s.h, s_s.n)
    V.drain()
    V.tensor_scalar_mul(mC[:], sxC[:], 1.0 / D)
    V.drain()
    V.tensor_tensor(varC[:], mC[:], mC[:], ALU.mult)
    V.drain()
    V.scalar_tensor_tensor(varC[:], sx2C[:], 1.0 / D, varC[:], ALU.mult, ALU.subtract)
    V.drain()
    v_var = s_v.inc(V.tensor_scalar_add(varC[:], varC[:], EPS))
    SC.wait_ge(s_v.h, v_var)
    sqr = s_s.inc(SC.activation(out=rsC[:], in_=varC[:], func=AF.Sqrt, bias=0.0, scale=1.0))
    V.wait_ge(s_s.h, sqr)
    V.reciprocal(rsC[:], rsC[:])
    V.drain()
    V.tensor_tensor(nbC[:], mC[:], rsC[:], ALU.mult)
    V.drain()
    v_nb = s_v.inc(V.tensor_scalar_mul(nbC[:], nbC[:], -1.0))
    SC.wait_ge(s_v.h, v_nb)
    t_cons = []
    for t in range(NT):
        b = t // 16
        par = t % 2
        first = (t % 16 == 0)
        last = (t % 16 == 15)
        ld = SY.dma_start(out=xt2[:, par, :], in_=xs_d[128 * t:128 * (t + 1), :])
        if t >= 2:
            ld.wait_op(s_s.h, t_cons[t - 2][0], "sem-ge")
        else:
            ld.wait_op(s_v.h, passA_done[-1], "sem-ge")
        x_t = s_x.inc(ld, 16)
        xt = xt2[:, par, :]
        SC.wait_ge(s_x.h, x_t)
        if t >= 2:
            SC.wait_ge(s_t.h, t_cons[t - 2][1])
        yslot = sc_y[:] if par == 0 else sc_y2[:]
        y_t = s_s.inc(SC.activation(out=yslot, in_=xt, func=AF.Identity,
                                    bias=nbC[:, t:t + 1], scale=rsC[:, t:t + 1]))
        TE.wait_ge(s_s.h, y_t)
        TE.matmul(pb[2 * b][0:1, :], sb_ones[:], yslot[:, 0:512] if False else (sc_y[:, 0:512] if par == 0 else sc_y2[:, 0:512]), start=first, stop=last)
        t_r = s_t.inc(TE.matmul(pb[2 * b + 1][0:1, :], sb_ones[:], sc_y[:, 512:1024] if par == 0 else sc_y2[:, 512:1024], start=first, stop=last))
        t_cons.append((y_t, t_r))

    V.wait_ge(s_t.h, s_t.n)
    for b in range(BPC):
        V.tensor_copy(r_row[0:1, D * b:D * b + 512], pb[2 * b][0:1, :])
        V.tensor_copy(r_row[0:1, D * b + 512:D * (b + 1)], pb[2 * b + 1][0:1, :])
    v_r = s_v.inc(V.drain())

    def matvec_fm(src_rows, bounce, fm_tile, w_dram, ps_lo, ps_hi, prod_sem_val):
        SY.wait_ge(s_v.h, prod_sem_val)
        s_st.inc(SY.dma_start(out=bounce.rearrange("(o a) -> o a", o=1), in_=src_rows), 16)
        SY.wait_ge(s_st.h, s_st.n)
        for b_ in range(BPC):
            s_x.inc(SY.dma_start(out=fm_tile[:, :, b_],
                                 in_=bounce.rearrange("(b k p) -> b p k", p=128, k=8)[b_],
                                 allow_slow_non_contiguous=True), 16)
        t_base = s_t.n
        x_base = s_x.n
        for dc in range(8):
            ld = SY.dma_start(out=wv_tile[:, dc % 2, :], in_=w_dram[128 * dc:128 * (dc + 1), :])
            if dc >= 2:
                ld.wait_op(s_t.h, t_base + dc - 1, "sem-ge")
            s_x.inc(ld, 16)
        for dc in range(8):
            TE.wait_ge(s_x.h, x_base + 16 * (dc + 1))
            TE.matmul(ps_lo[0:BPC, :], fm_tile[:, dc, :], wv_tile[:, dc % 2, 0:512],
                      start=(dc == 0), stop=(dc == 7))
            s_t.inc(TE.matmul(ps_hi[0:BPC, :], fm_tile[:, dc, :], wv_tile[:, dc % 2, 512:1024],
                              start=(dc == 0), stop=(dc == 7)))

    matvec_fm(r_row[:], r_bounce, rfm, wvT_d, pb[4], pb[5], v_r)
    V.wait_ge(s_t.h, s_t.n)
    V.tensor_copy(t_sb[:, 0:512], pb[4][0:BPC, :])
    V.tensor_copy(t_sb[:, 512:1024], pb[5][0:BPC, :])
    v_t = s_v.inc(V.drain())
    matvec_fm(t_sb[:], t_bounce, tfm, woT_d, pb[6], pb[7], v_t)
    V.wait_ge(s_t.h, s_t.n)
    V.tensor_copy(a_sb[:, 0:512], pb[6][0:BPC, :])
    V.tensor_copy(a_sb[:, 512:1024], pb[7][0:BPC, :])
    v_a = s_v.inc(V.drain())
    SY.wait_ge(s_v.h, v_a)
    s_st.inc(SY.dma_start(out=t_bounce.rearrange("(b d) -> b d", b=BPC), in_=a_sb[:]), 16)
    SY.wait_ge(s_st.h, s_st.n)
    x_ar = s_x.inc(SY.dma_start(out=a_row[:], in_=t_bounce.rearrange("(o a) -> o a", o=1)), 16)
    TE.wait_ge(s_x.h, x_ar)
    for b in range(BPC):
        TE.matmul(pb[0][:], sb_onesrow[:], a_row[0:1, D * b:D * b + 512], start=True, stop=True)
        t_bc = s_t.inc(TE.matmul(pb[1][:], sb_onesrow[:], a_row[0:1, D * b + 512:D * b + 1024], start=True, stop=True))
        V.wait_ge(s_t.h, t_bc)
        V.tensor_copy(attn_bc[:, b, 0:512], pb[0][:])
        v_bc = s_v.inc(V.tensor_copy(attn_bc[:, b, 512:1024], pb[1][:]))
        TE.wait_ge(s_v.h, v_bc)
    v_abc = s_v.inc(V.drain())

    if upto < 3:
        SY.wait_ge(s_st.h, s_st.n)
        SY.wait_ge(s_x.h, s_x.n)
        SY.wait_ge(s_v.h, s_v.n)
        SY.wait_ge(s_g.h, s_g.n)
        SY.wait_ge(s_cc.h, s_cc.n)
        return nc
    V.wait_ge(s_v.h, v_abc)
    pA = []
    for t in range(S3T):
        par = t % 2
        b = t // 16
        ld = SY.dma_start(out=xt2[:, par, :], in_=xs_d[128 * t:128 * (t + 1), :])
        if t >= 2:
            ld.wait_op(s_st.h, pA[t - 2], "sem-ge")
        x_t = s_x.inc(ld, 16)
        xt = xt2[:, par, :]
        V.wait_ge(s_x.h, x_t)
        v_x1 = s_v.inc(V.tensor_tensor(xt, xt, attn_bc[:, b, :], ALU.add))
        SC.wait_ge(s_v.h, v_x1)
        sq_t = s_s.inc(SC.activation(out=sc_y2[:], in_=xt, func=AF.Square, accum_out=sx2C[:, t:t + 1]))
        V.wait_ge(s_s.h, sq_t)
        s_v.inc(V.reduce_sum(sxC[:, t:t + 1], xt, axis=AX.X))
        pA.append(s_st.inc(SY.dma_start(out=x1_hbm[128 * t:128 * (t + 1), :],
                                        in_=xt).wait_op(s_v.h, s_v.n, "sem-ge"), 16))
    V.wait_ge(s_s.h, s_s.n)
    V.drain()
    V.tensor_scalar_mul(mC[:], sxC[:], 1.0 / D)
    V.drain()
    V.tensor_tensor(varC[:], mC[:], mC[:], ALU.mult)
    V.drain()
    V.scalar_tensor_tensor(varC[:], sx2C[:], 1.0 / D, varC[:], ALU.mult, ALU.subtract)
    V.drain()
    v_var = s_v.inc(V.tensor_scalar_add(varC[:], varC[:], EPS))
    SC.wait_ge(s_v.h, v_var)
    sqr = s_s.inc(SC.activation(out=rsC[:], in_=varC[:], func=AF.Sqrt, bias=0.0, scale=1.0))
    V.wait_ge(s_s.h, sqr)
    V.reciprocal(rsC[:], rsC[:])
    V.drain()
    V.tensor_tensor(nbC[:], mC[:], rsC[:], ALU.mult)
    V.drain()
    v_nb = s_v.inc(V.tensor_scalar_mul(nbC[:], nbC[:], -1.0))
    SC.wait_ge(s_v.h, v_nb)
    tile_done = []
    for t in range(S3T):
        par = t % 2
        ld = SY.dma_start(out=xt2[:, par, :], in_=x1_hbm[128 * t:128 * (t + 1), :])
        ld.wait_op(s_st.h, max(pA[-1], tile_done[t - 2]) if t >= 2 else pA[-1], "sem-ge")
        x_t = s_x.inc(ld, 16)
        xt = xt2[:, par, :]
        SC.wait_ge(s_x.h, x_t)
        if t >= 2:
            SC.wait_ge(s_st.h, tile_done[t - 2])
        yslot = sc_y[:] if par == 0 else sc_y2[:]
        hslot = sc_hi[:, 0:512].bitcast(BF16) if False else (sc_hi[:] if par == 0 else sc_lo[:])
        SC.activation(out=yslot, in_=xt, func=AF.Identity,
                      bias=nbC[:, t:t + 1], scale=rsC[:, t:t + 1])
        hi_t = s_s.inc(SC.activation(out=hslot, in_=xt, func=AF.Identity,
                                     bias=nbC[:, t:t + 1], scale=rsC[:, t:t + 1]))
        V.wait_ge(s_s.h, hi_t)
        lslot = sc_lo2[:] if par == 0 else sc_lo3[:]
        v_lo = s_v.inc(V.tensor_tensor(lslot, yslot, hslot, ALU.subtract))
        s_st.inc(SY.dma_start(out=yhi_hbm[128 * t:128 * (t + 1), :], in_=hslot).wait_op(s_s.h, hi_t, "sem-ge"), 16)
        st_t = s_st.inc(SY.dma_start(out=ylo_hbm[128 * t:128 * (t + 1), :], in_=lslot).wait_op(s_v.h, v_lo, "sem-ge"), 16)
        tile_done.append(st_t)
    ST3 = s_st.n

    if not _os.environ.get("K_NOGPW"):
        GP.wait_ge(s_st.h, ST3)
    if _os.environ.get("K_NOCC"):
        cc_y = 0
    elif _os.environ.get("K_TINYAG"):
        cc_y = s_cc.inc(GP.collective_compute(
            "AllGather", ALU.bypass, replica_groups=[list(range(NC))],
            ins=[cnt_in], outs=[cnt_all]))
    else:
        cc_y = s_cc.inc(GP.collective_compute(
            "AllGather", ALU.bypass, replica_groups=[list(range(NC))],
            ins=[yhi_hbm], outs=[y_all]))

    if upto < 4:
        SY.wait_ge(s_st.h, s_st.n)
        SY.wait_ge(s_x.h, s_x.n)
        SY.wait_ge(s_v.h, s_v.n)
        SY.wait_ge(s_g.h, s_g.n)
        SY.wait_ge(s_cc.h, s_cc.n)
        return nc
    GP.load_library(library_config.mlp)
    V.wait_ge(s_v.h, v_emb)
    dots_mm = []
    for g in range(0 if _os.environ.get("K_NODG") else 8):
        gp_ = g % 2
        GP.wait_ge(s_st.h, ST3)
        if g >= 2:
            GP.wait_ge(s_t.h, dots_mm[g - 2])
        s_g.inc(GP.dma_gather(
            yfm_hi[:, gp_], yhi_hbm, sb_idxid[:, 32 * g:32 * (g + 1)], 512, 512, D,
            elem_step=D, transpose=True), 16)
        g_lo = s_g.inc(GP.dma_gather(
            yfm_lo[:, gp_], ylo_hbm, sb_idxid[:, 32 * g:32 * (g + 1)], 512, 512, D,
            elem_step=D, transpose=True), 16)
        TE.wait_ge(s_g.h, g_lo)
        ps = pb[2 + gp_]
        if g >= 2:
            TE.wait_ge(s_v.h, s_v.n)
        n_mm = 0
        t_d = None
        for (yt, et) in ((yfm_hi, sb_ehi), (yfm_hi, sb_elo), (yfm_lo, sb_ehi), (yfm_lo, sb_elo)):
            for dc in range(8):
                inst = TE.matmul(ps[0:8, :], et[:, dc, :], yt[:, gp_, dc, :],
                                 start=(n_mm == 0), stop=(n_mm == 31))
                n_mm += 1
                if n_mm == 32:
                    t_d = s_t.inc(inst)
        dots_mm.append(t_d)
        V.wait_ge(s_t.h, t_d)
        s_v.inc(V.tensor_copy(dots_em[:, 512 * g:512 * (g + 1)], ps[0:8, :]))

    if dots_mm:
        GP.wait_ge(s_t.h, dots_mm[-1])
    w_ffn = s_g.n
    if not _os.environ.get("K_NOW"):
        for icg in range(8):
            s_g.inc(GP.dma_start(out=w1t[:, icg, :],
                                 in_=w1T_d.rearrange("(dc p) i -> p dc i", p=128)[:, icg, :]), 16)
        for icg in range(16):
            s_g.inc(GP.dma_start(out=w2t[:, icg, :], in_=w2T_d[128 * icg:128 * (icg + 1), :]), 16)
        w_ffn = s_g.n

    if upto < 5:
        SY.wait_ge(s_st.h, s_st.n)
        SY.wait_ge(s_x.h, s_x.n)
        SY.wait_ge(s_v.h, s_v.n)
        SY.wait_ge(s_g.h, s_g.n)
        SY.wait_ge(s_cc.h, s_cc.n)
        return nc
    V.wait_ge(s_v.h, s_v.n)
    for t in range(NT):
        ps = pb[4 + t % 2]
        t_tr = s_t.inc(TE.transpose(ps[0:128, 0:8], dots_em[:, 128 * t:128 * (t + 1)], sb_iden[0:8, 0:8]))
        V.wait_ge(s_t.h, t_tr)
        v_tr = s_v.inc(V.tensor_copy(dots_tm[:, t, :], ps[0:128, 0:8]))
        TE.wait_ge(s_v.h, v_tr)
    dots3 = dots_tm[:]
    iota3 = sb_iota8[:].unsqueeze(1).to_broadcast([128, NT, 8])
    iotam3 = sb_iota8m[:].unsqueeze(1).to_broadcast([128, NT, 8])
    V.wait_ge(s_v.h, s_v.n)
    V.drain()
    V.reduce_max(mx1C[:], dots3, axis=AX.X)
    V.drain()
    V.tensor_tensor(r3a[:], dots3, mx1C[:].unsqueeze(2).to_broadcast([128, NT, 8]), ALU.is_equal)
    V.drain()
    V.tensor_tensor(r3b[:], r3a[:], iotam3, ALU.mult)
    V.drain()
    V.tensor_reduce(packA[:, 0:32], r3b[:], axis=AX.X, op=ALU.min)
    V.drain()
    V.tensor_scalar_add(packA[:, 0:32], packA[:, 0:32], 1024.0)
    V.drain()
    V.tensor_tensor(r3a[:], iota3, packA[:, 0:32].unsqueeze(2).to_broadcast([128, NT, 8]), ALU.is_equal)
    V.drain()
    V.scalar_tensor_tensor(r3b[:], r3a[:], -1e30, dots3, ALU.mult, ALU.add)
    V.drain()
    V.reduce_max(mx2C[:], r3b[:], axis=AX.X)
    V.drain()
    V.tensor_tensor(r3a[:], r3b[:], mx2C[:].unsqueeze(2).to_broadcast([128, NT, 8]), ALU.is_equal)
    V.drain()
    V.tensor_tensor(r3b[:], r3a[:], iotam3, ALU.mult)
    V.drain()
    V.tensor_reduce(packA[:, 32:64], r3b[:], axis=AX.X, op=ALU.min)
    V.drain()
    V.tensor_scalar_add(packA[:, 32:64], packA[:, 32:64], 1024.0)
    V.drain()
    V.tensor_tensor(packA[:, 64:96], mx1C[:], mx2C[:], ALU.subtract)
    V.drain()
    v_pack = s_v.inc(V.drain())
    TE.wait_ge(s_v.h, v_pack)
    t_pk = s_t.inc(TE.transpose(pb[6][0:96, 0:128], packA[:, 0:96], sb_iden[:]))
    V.wait_ge(s_t.h, t_pk)
    v_pk = s_v.inc(V.tensor_copy(packB[:], pb[6][0:96, 0:128]))
    SY.wait_ge(s_v.h, v_pk)
    s_x.inc(SY.dma_start(out=ix2B0[:], in_=packB[32:64, :]), 16)
    s_x.inc(SY.dma_start(out=d12B0[:], in_=packB[64:96, :]), 16)
    x_pk = s_x.n
    ix1B = packB[0:32, :]
    ix2B = ix2B0[:]
    d12B = d12B0[:]
    SC.wait_ge(s_x.h, x_pk)
    V.wait_ge(s_x.h, x_pk)
    SC.wait_ge(s_v.h, v_pk)
    SC.activation(out=w1B[:], in_=d12B, func=AF.Sigmoid, bias=0.0, scale=1.0)
    s_s.inc(SC.activation(out=w2B[:], in_=d12B, func=AF.Sigmoid, bias=0.0, scale=-1.0))
    V.wait_ge(s_v.h, v_pk)
    V.tensor_tensor(m1B[:], ix1B.unsqueeze(1).to_broadcast([32, 8, 128]), sb_iotaE[:], ALU.is_equal)
    V.tensor_tensor(m2B[:], ix2B.unsqueeze(1).to_broadcast([32, 8, 128]), sb_iotaE[:], ALU.is_equal)
    V.drain()
    V.tensor_tensor(onesB[:], m1B[:], m2B[:], ALU.add)
    V.drain()
    for e in range(E):
        cumsum(csumB[:, e, :], onesB[:, e, :])
        V.drain()
    V.tensor_copy(totB[:], csumB[:, :, 127])
    v_tot = s_v.inc(V.drain())
    TE.wait_ge(s_v.h, v_tot)
    t_cnt = s_t.inc(TE.matmul(pb[7][0:1, 0:8], sb_ones[0:32, :], totB[:], start=True, stop=True))
    V.wait_ge(s_t.h, t_cnt)
    v_cnt = s_v.inc(V.tensor_copy(cntsb[:], pb[7][0:1, 0:8]))
    SY.wait_ge(s_v.h, v_cnt)
    s_st.inc(SY.dma_start(out=cnt_in, in_=cntsb[:]), 16)
    GP.wait_ge(s_st.h, s_st.n)
    cc_cnt = s_cc.inc(GP.collective_compute(
        "AllGather", ALU.bypass, replica_groups=[list(range(NC))],
        ins=[cnt_in], outs=[cnt_all]))
    V.tensor_scalar(survB[:], onesB[:], 0.5, None, ALU.is_gt)
    v_su = s_v.inc(V.drain())
    sU8 = AT("p_sU8", [32, 8, 128], U8, 112 * KB)
    V.tensor_copy(sU8[:], survB[:])
    V.drain()
    V.select(candB[:], sU8[:], sb_iotaG[:], sb_m132[:])
    v_cand = s_v.inc(V.drain())
    SY.wait_ge(s_v.h, v_cand)
    s_st.inc(SY.dma_start(out=cand_in.rearrange("e (p f) -> p e f", p=32), in_=candB[:]), 16)
    GP.wait_ge(s_st.h, s_st.n)
    cc_cand = s_cc.inc(GP.collective_compute(
        "AllGather", ALU.bypass, replica_groups=[list(range(NC))],
        ins=[cand_in], outs=[cand_all]))
    SY.wait_ge(s_cc.h, cc_cnt)
    x_cnt = s_x.inc(SY.dma_start(out=cntall_sb[:], in_=cnt_all), 16)
    TE.wait_ge(s_x.h, x_cnt)
    TE.wait_ge(s_v.h, v_cnt)
    TE.matmul(pb[7][0:32, 0:8], sb_triu32[:], totB[:], start=True, stop=False)
    t_pfx = s_t.inc(TE.matmul(pb[7][0:32, 0:8], sb_maskc[:], cntall_sb[:], start=False, stop=True))
    V.wait_ge(s_t.h, t_pfx)
    v_pfx = s_v.inc(V.tensor_copy(pfxT[:], pb[7][0:32, 0:8]))
    V.wait_ge(s_v.h, v_pfx)
    V.tensor_tensor(grankB[:], csumB[:], pfxT[:].unsqueeze(2).to_broadcast([32, 8, 128]), ALU.add)
    v_gr_tap = s_v.inc(V.drain())
    if DBG:
        s_o.inc(SY.dma_start(out=o_grank, in_=grankB[:].rearrange("a b c -> a (b c)")).wait_op(s_v.h, v_gr_tap, "sem-ge"), 16)
        s_o.inc(SY.dma_start(out=o_csum, in_=csumB[:].rearrange("a b c -> a (b c)")).wait_op(s_v.h, v_gr_tap, "sem-ge"), 16)
    V.tensor_scalar(survB[:], grankB[:], float(CAP), None, ALU.is_le)
    V.drain()
    V.tensor_tensor(survB[:], survB[:], onesB[:], ALU.mult)
    V.drain()
    def sum_e(dst, msk, val):
        V.tensor_tensor(scrB[:], msk, val, ALU.mult)
        V.drain()
        V.tensor_tensor(dst, scrB[:, 0, :], scrB[:, 1, :], ALU.add)
        for e_ in range(2, E):
            V.drain()
            V.tensor_tensor(dst, dst, scrB[:, e_, :], ALU.add)
        V.drain()

    sum_e(sB1[:], m1B[:], grankB[:])
    sum_e(sB2[:], m2B[:], grankB[:])
    V.tensor_scalar(sB1[:], sB1[:], -1.0, None, ALU.add)
    V.tensor_scalar(sB2[:], sB2[:], -1.0, None, ALU.add)
    V.drain()
    V.tensor_scalar(sB1[:], sB1[:], float(CAP - 1), None, ALU.min)
    V.tensor_scalar(sB2[:], sB2[:], float(CAP - 1), None, ALU.min)
    V.drain()
    for k, (sBk, ixk) in enumerate(((sB1, ix1B), (sB2, ix2B))):
        V.tensor_scalar(scrB[:, 0, :], sBk[:], 1024.0 - 0.5, None, ALU.is_ge)
        V.tensor_scalar(scrB[:, 1, :], sBk[:], 2048.0 - 0.5, None, ALU.is_ge)
        V.tensor_scalar(scrB[:, 2, :], sBk[:], 3072.0 - 0.5, None, ALU.is_ge)
        V.drain()
        V.tensor_tensor(scrB[:, 0, :], scrB[:, 0, :], scrB[:, 1, :], ALU.add)
        V.drain()
        V.tensor_tensor(scrB[:, 0, :], scrB[:, 0, :], scrB[:, 2, :], ALU.add)
        V.drain()
        V.scalar_tensor_tensor(scrB[:, 2, :], scrB[:, 0, :], 7168.0, sBk[:], ALU.mult, ALU.add)
        V.drain()
        V.scalar_tensor_tensor(gB[:, k, :], ixk, 1024.0, scrB[:, 2, :], ALU.mult, ALU.add)
        V.drain()
    v_g = s_v.inc(V.drain())
    sum_e(sB1[:], m1B[:], survB[:])
    sum_e(sB2[:], m2B[:], survB[:])
    V.wait_ge(s_s.h, s_s.n)
    V.tensor_tensor(weB[:, 0, :], w1B[:], sB1[:], ALU.mult)
    V.tensor_tensor(weB[:, 1, :], w2B[:], sB2[:], ALU.mult)
    v_we = s_v.inc(V.drain())
    TE.wait_ge(s_v.h, v_we)
    for k in range(2):
        t_we = s_t.inc(TE.transpose(pb[6][0:128, 0:32], weB[:, k, :], sb_iden[0:32, 0:32]))
        V.wait_ge(s_t.h, t_we)
        v_wt = s_v.inc(V.tensor_copy(we_tm[:, k, :], pb[6][0:128, 0:32]))
        TE.wait_ge(s_v.h, v_wt)
    SY.wait_ge(s_v.h, v_g)
    s_st.inc(SY.dma_start(out=g1_dram.rearrange("(p f) -> p f", p=32), in_=gB[:, 0, :]), 16)
    st_g = s_st.inc(SY.dma_start(out=g2_dram.rearrange("(p f) -> p f", p=32), in_=gB[:, 1, :]), 16)
    SY.wait_ge(s_st.h, st_g)
    for k in range(8):
        s_x.inc(SY.dma_start(out=g1il[16 * k:16 * (k + 1), :],
                             in_=g1_dram.rearrange("(f p) -> p f", p=16),
                             allow_slow_non_contiguous=True), 16)
        s_x.inc(SY.dma_start(out=g2il[16 * k:16 * (k + 1), :],
                             in_=g2_dram.rearrange("(f p) -> p f", p=16),
                             allow_slow_non_contiguous=True), 16)
    x_gil = s_x.n

    if upto < 6:
        SY.wait_ge(s_st.h, s_st.n)
        SY.wait_ge(s_x.h, s_x.n)
        SY.wait_ge(s_v.h, s_v.n)
        SY.wait_ge(s_g.h, s_g.n)
        SY.wait_ge(s_cc.h, s_cc.n)
        return nc
    GP.wait_ge(s_cc.h, cc_cand)
    g_rows = s_g.inc(GP.dma_gather(
        rowg[:].rearrange("p (c n) -> p c n", c=1), cand_all, sb_idx8[:], 8, 8, TPC,
        elem_step=TPC, transpose=False), 16)
    SY.wait_ge(s_g.h, g_rows)
    st_cm = s_st.inc(SY.dma_start(out=cand_mine.rearrange("(r n) -> r n", r=8), in_=rowg[0:8, :]), 16)
    if _os.environ.get("K_E6") == "1":
        SY.wait_ge(s_st.h, s_st.n)
        SY.wait_ge(s_x.h, s_x.n)
        SY.wait_ge(s_v.h, s_v.n)
        SY.wait_ge(s_g.h, s_g.n)
        SY.wait_ge(s_cc.h, s_cc.n)
        return nc
    SY.wait_ge(s_st.h, st_cm)
    x_cr = s_x.inc(SY.dma_start(out=craw[:],
                                in_=cand_mine.rearrange("(blk fl p) -> p blk fl", p=16, fl=256),
                                allow_slow_non_contiguous=True).wait_op(s_v.h, v_we, "sem-ge"), 16)
    if _os.environ.get("K_E6") == "2":
        SY.wait_ge(s_st.h, s_st.n)
        SY.wait_ge(s_x.h, s_x.n)
        SY.wait_ge(s_v.h, s_v.n)
        SY.wait_ge(s_g.h, s_g.n)
        SY.wait_ge(s_cc.h, s_cc.n)
        return nc
    V.wait_ge(s_x.h, x_cr)
    V.wait_ge(s_v.h, v_we)
    crawf = craw[:].rearrange("p a b -> p (a b)")
    if DBG:
        s_o.inc(SY.dma_start(out=o_craw, in_=crawf).wait_op(s_x.h, x_cr, "sem-ge"), 16)
    V.tensor_scalar(emask[:], crawf, 0.0, None, ALU.is_ge)
    v_em2 = s_v.inc(V.drain())
    TE.wait_ge(s_v.h, v_em2)
    for q in range(4):
        s_t.inc(TE.matmul(pb[q][0:1, :], sb_ones[0:16, :], emask[:, 512 * q:512 * (q + 1)],
                          start=True, stop=True))
    V.wait_ge(s_t.h, s_t.n)
    for q in range(4):
        V.tensor_copy(eX1[0:1, 512 * q:512 * (q + 1)], pb[q][0:1, :])
    V.drain()
    cumsum(eX2[0:1, :], eX1[0:1, :])
    V.drain()
    v_col = s_v.inc(V.tensor_tensor(eX2[0:1, :], eX2[0:1, :], eX1[0:1, :], ALU.subtract))
    TE.wait_ge(s_v.h, v_col)
    for q in range(4):
        TE.matmul(pb[4 + (q % 2)][0:16, :], sb_triu16[:], emask[:, 512 * q:512 * (q + 1)],
                  start=True, stop=False)
        t_p = s_t.inc(TE.matmul(pb[4 + (q % 2)][0:16, :], sb_onesrow[0:1, 0:16],
                                eX2[0:1, 512 * q:512 * (q + 1)], start=False, stop=True))
        V.wait_ge(s_t.h, t_p)
        v_ep = s_v.inc(V.tensor_copy(eX1[:, 512 * q:512 * (q + 1)], pb[4 + (q % 2)][0:16, :]))
        TE.wait_ge(s_v.h, v_ep)
    V.tensor_tensor(eX1[:], eX1[:], emask[:], ALU.add)
    V.drain()
    V.tensor_scalar(eX2[:], eX1[:], float(CAP), None, ALU.is_le)
    V.drain()
    V.tensor_tensor(eX2[:], eX2[:], emask[:], ALU.mult)
    V.drain()
    eU8 = eX1[:].bitcast(U8)[:, 0:2048]
    V.tensor_copy(eU8, eX2[:])
    V.drain()
    v_ec = s_v.inc(V.select(ecand[:], eU8, crawf, sb_m116[:]))
    if DBG:
        s_o.inc(SY.dma_start(out=o_ecand, in_=ecand[:]).wait_op(s_v.h, v_ec, "sem-ge"), 16)
        s_o.inc(SY.dma_start(out=o_egr, in_=eX2[:]).wait_op(s_v.h, v_ec, "sem-ge"), 16)
    if _os.environ.get("K_E6") == "3":
        SY.wait_ge(s_st.h, s_st.n)
        SY.wait_ge(s_x.h, s_x.n)
        SY.wait_ge(s_v.h, s_v.n)
        SY.wait_ge(s_g.h, s_g.n)
        SY.wait_ge(s_cc.h, s_cc.n)
        return nc
    GP.load_library(library_config.sparse_gather)
    GP.wait_ge(s_v.h, v_ec)

    def spg(in_ap, out_ap, nf_cell, fill_ap):
        g_ = s_g.inc(GP.sparse_gather(out=out_ap, in_=in_ap, num_found=nf_cell))
        V.wait_ge(s_g.h, g_)
        v1 = s_v.inc(V.tensor_copy(nf_sf[:], nf_cell))
        TE.wait_ge(s_v.h, v1)
        t1 = s_t.inc(TE.matmul(pb[6][0:16, 0:1], sb_onesrow[0:1, 0:16], nf_sf[:], start=True, stop=True))
        V.wait_ge(s_t.h, t1)
        V.tensor_copy(nfb[:], pb[6][0:16, 0:1])
        V.drain()
        V.tensor_tensor(evalid[:], sb_iotasl[:], nfb[:].to_broadcast([16, 256]), ALU.is_ge)
        V.drain()
        v2 = s_v.inc(V.copy_predicated(out_ap, evalid[:], fill_ap))
        GP.wait_ge(s_v.h, v2)
        return v2

    for q in range(4):
        spg(ecand[:, 512 * q:512 * (q + 1)],
            (spA if q < 2 else spB)[:, 256 * (q % 2):256 * (q % 2 + 1)],
            nf_s[0:1, q:q + 1], sb_m116[:, 0:256])
    spg(spA[:], spC[:, 0:256], nf_s[0:1, 4:5], sb_m116[:, 0:256])
    spg(spB[:], spC[:, 256:512], nf_s[0:1, 5:6], sb_m116[:, 0:256])
    v_cl0 = spg(spC[:], clist[:], nf_sb[:], sb_z16[:])
    GP.load_library(library_config.mlp)
    SY.wait_ge(s_v.h, v_cl0)
    for k in range(8):
        s_x.inc(SY.dma_start(out=clrep[16 * k:16 * (k + 1), :], in_=clist[:]), 16)
    x_clr = s_x.n

    if _os.environ.get("K_E6") == "5":
        SY.wait_ge(s_st.h, s_st.n)
        SY.wait_ge(s_x.h, s_x.n)
        SY.wait_ge(s_v.h, s_v.n)
        SY.wait_ge(s_g.h, s_g.n)
        SY.wait_ge(s_cc.h, s_cc.n)
        return nc
    if DBG:
        GP.wait_ge(s_o.h, s_o.n)
    GP.wait_ge(s_cc.h, cc_y)
    GP.wait_ge(s_x.h, x_clr)
    relu_done = {}
    st_ct = []
    gath_y = []
    mm1_last = {}
    for tg in range(8):
        gpar = tg % 2
        if tg >= 2:
            GP.wait_ge(s_t.h, mm1_last[tg - 2])
        g_y = s_g.inc(GP.dma_gather(
            yfm2[:, gpar], y_all, clrep[:, 32 * tg:32 * (tg + 1)], 512, 512, D,
            elem_step=D, transpose=True), 16)
        gath_y.append(g_y)
        TE.wait_ge(s_g.h, max(g_y, w_ffn))
        for ic in range(16):
            ps = pb[ic % 2]
            if (tg, ic - 2) in relu_done:
                TE.wait_ge(s_s.h, relu_done[(tg, ic - 2)])
            elif tg > 0 and ic < 2:
                TE.wait_ge(s_s.h, relu_done[(tg - 1, 14 + ic)])
            t_i = None
            for dc in range(8):
                inst = TE.matmul(ps[:], w1t[:, dc, 128 * ic:128 * (ic + 1)], yfm2[:, gpar, dc, :],
                                 start=(dc == 0), stop=(dc == 7))
                if dc == 7:
                    t_i = s_t.inc(inst)
            if ic == 15:
                mm1_last[tg] = t_i
            SC.wait_ge(s_t.h, t_i)
            relu_done[(tg, ic)] = s_s.inc(
                SC.activation(out=inter[:, ic, :], in_=ps[:], func=AF.Relu, bias=0.0, scale=1.0))
        TE.wait_ge(s_s.h, relu_done[(tg, 15)])
        for sub in range(4):
            cpar = sub % 2
            for hh in range(2):
                ps = pb[4 + 2 * cpar + hh]
                t_o = None
                for ic in range(16):
                    inst = TE.matmul(ps[:], inter[:, ic, 128 * sub:128 * (sub + 1)],
                                     w2t[:, ic, 512 * hh:512 * (hh + 1)],
                                     start=(ic == 0), stop=(ic == 15))
                    if ic == 15:
                        t_o = s_t.inc(inst)
                V.wait_ge(s_t.h, t_o)
                if len(st_ct) >= 2:
                    V.wait_ge(s_st.h, st_ct[-2])
                v_o = s_v.inc(V.tensor_copy(ctile[:, cpar, 512 * hh:512 * (hh + 1)], ps[:]))
                TE.wait_ge(s_v.h, v_o)
            row0 = 512 * tg + 128 * sub
            st_ct.append(s_st.inc(SY.dma_start(
                out=contrib_in[row0:row0 + 128, :],
                in_=ctile[:, cpar, :]).wait_op(s_v.h, s_v.n, "sem-ge"), 16))
    cc_con = None
    for k in range(4):
        GP.wait_ge(s_st.h, st_ct[4 * (2 * k + 1) + 3])
        cc_con = s_cc.inc(GP.collective_compute(
            "AllGather", ALU.bypass, replica_groups=[list(range(NC))],
            ins=[contrib_in[1024 * k:1024 * (k + 1), :]],
            outs=[contrib_all[8192 * k:8192 * (k + 1), :]]))

    if upto < 7:
        SY.wait_ge(s_st.h, s_st.n)
        SY.wait_ge(s_x.h, s_x.n)
        SY.wait_ge(s_v.h, s_v.n)
        SY.wait_ge(s_g.h, s_g.n)
        SY.wait_ge(s_cc.h, s_cc.n)
        return nc
    GP.wait_ge(s_cc.h, cc_con)
    GP.wait_ge(s_x.h, x_gil)
    fin = []
    for g in range(8):
        gpar = g % 2
        if g >= 2:
            GP.wait_ge(s_v.h, fin[g - 2])
        g_c1 = s_g.inc(GP.dma_gather(
            c1b[:, gpar], contrib_all, g1il[:, 32 * g:32 * (g + 1)], 512, 512, D,
            elem_step=D, transpose=False), 16)
        g_c2 = s_g.inc(GP.dma_gather(
            c2b[:, gpar], contrib_all, g2il[:, 32 * g:32 * (g + 1)], 512, 512, D,
            elem_step=D, transpose=False), 16)
        ld = SY.dma_start(out=cxg[:], in_=x1_hbm[512 * g:512 * (g + 1), :].rearrange("(a p) d -> p a d", p=128))
        if g >= 1:
            ld.wait_op(s_v.h, fin[g - 1], "sem-ge")
        x_t = s_x.inc(ld, 16)
        we1g = we_tm[:, 0, 4 * g:4 * (g + 1)].unsqueeze(2).to_broadcast([128, 4, D])
        we2g = we_tm[:, 1, 4 * g:4 * (g + 1)].unsqueeze(2).to_broadcast([128, 4, D])
        V.wait_ge(s_g.h, g_c2)
        V.wait_ge(s_x.h, x_t)
        if g >= 1:
            V.wait_ge(s_o.h, 16 * g)
        V.tensor_tensor(o1g[:], c1b[:, gpar], we1g, ALU.mult)
        V.tensor_tensor(o2g[:], c2b[:, gpar], we2g, ALU.mult)
        V.drain()
        V.tensor_tensor(o1g[:], o1g[:], cxg[:], ALU.add)
        V.drain()
        v_f = s_v.inc(V.tensor_tensor(o1g[:], o1g[:], o2g[:], ALU.add))
        fin.append(v_f)
        s_o.inc(SY.dma_start(out=out_d[512 * g:512 * (g + 1), :].rearrange("(a p) d -> p a d", p=128),
                             in_=o1g[:]).wait_op(s_v.h, v_f, "sem-ge"), 16)
    if DBG:
        SY.wait_ge(s_v.h, s_v.n)
        SY.wait_ge(s_g.h, s_g.n)
        s_o.inc(SY.dma_start(out=o_clist, in_=clist[:]), 16)
        s_o.inc(SY.dma_start(out=o_dots, in_=dots_em[:]), 16)
        s_o.inc(SY.dma_start(out=o_pack, in_=packB[:]), 16)
        s_o.inc(SY.dma_start(out=o_gb, in_=gB[:].rearrange("a b c -> a (b c)")), 16)
        s_o.inc(SY.dma_start(out=o_we, in_=we_tm[:].rearrange("a b c -> a (b c)")), 16)
        s_o.inc(SY.dma_start(out=o_con, in_=contrib_in[0:128, :]), 16)
        s_o.inc(SY.dma_start(out=o_yrow, in_=y_all[0:8, :]), 16)
        s_o.inc(SY.dma_start(out=o_sb[:, 0:128], in_=sB1[:]), 16)
        s_o.inc(SY.dma_start(out=o_sb[:, 128:256], in_=sB2[:]), 16)
        s_o.inc(SY.dma_start(out=o_tot[:, 0:8], in_=totB[:]), 16)
        s_o.inc(SY.dma_start(out=o_tot[:, 8:16], in_=pfxT[:]), 16)
        s_o.inc(SY.dma_start(out=o_x1, in_=x1_hbm[2048:2176, :]), 16)
        s_o.inc(SY.dma_start(out=o_arow, in_=a_row[:]), 16)
        s_o.inc(SY.dma_start(out=o_nfs, in_=nf_s[:]), 16)
        s_o.inc(SY.dma_start(out=o_nff[0:1, 0:1], in_=nf_sb[:]), 16)
    SY.wait_ge(s_o.h, s_o.n)
    return nc


def _host_consts():
    c = {}
    c["iden"] = np.eye(128, dtype=np.float32)
    c["onescol"] = np.ones((128, 1), np.float32)
    c["onesrow"] = np.ones((1, 128), np.float32)
    c["triu16"] = np.triu(np.ones((16, 16), np.float32), 1)
    c["triu32"] = np.triu(np.ones((32, 32), np.float32), 1)
    c["iota8"] = np.tile(np.arange(8, dtype=np.float32), (128, 1))
    c["iota8m"] = np.tile(np.arange(8, dtype=np.float32) - 1024.0, (128, 1))
    c["iotaEB"] = np.ascontiguousarray(
        np.tile(np.arange(8, dtype=np.float32)[None, :, None], (32, 1, 128)).reshape(32, -1))
    idxid = np.zeros((16, 256), np.int16)
    for i_ in range(4096):
        idxid[i_ % 16, i_ // 16] = i_
    c["idxid"] = np.tile(idxid, (8, 1))
    iotasl = np.zeros((16, 256), np.float32)
    for j in range(4096):
        iotasl[j % 16, j // 16] = j
    c["iotasl"] = iotasl
    c["m1c16"] = np.full((16, 2048), -1, np.int16)
    c["m1c32"] = np.full((32, 8 * 128), -1, np.int16)
    c["z16"] = np.zeros((16, 256), np.int16)
    return c


def kernel(L0_attn_x, L0_attn_w_qkv, L0_attn_w_o, L0_ffn_expert_embeddings,
           L0_ffn_first_linear, L0_ffn_second_linear):
    x = np.ascontiguousarray(np.asarray(L0_attn_x, dtype=np.float32))
    wqkv = np.asarray(L0_attn_w_qkv, dtype=np.float32)
    wo = np.asarray(L0_attn_w_o, dtype=np.float32)
    emb = np.asarray(L0_ffn_expert_embeddings, dtype=np.float32)
    w1 = np.asarray(L0_ffn_first_linear, dtype=np.float32)
    w2 = np.asarray(L0_ffn_second_linear, dtype=np.float32)

    wvT = np.ascontiguousarray(wqkv[2 * D:3 * D, :].T)
    woT = np.ascontiguousarray(wo.T)
    embT = np.ascontiguousarray(emb.T)
    consts = _host_consts()

    import os as _os
    _upto = int(_os.environ.get("K_UPTO", 99))
    if _CACHE.get("upto") != _upto:
        _CACHE.clear()
        _CACHE["upto"] = _upto
        _CACHE["nc"] = build(_upto)
        mybir.codegen_inst_isa_subclasses(_CACHE["nc"])
    nc = _CACHE["nc"]

    xr = x.reshape(T, D)
    in_maps = []
    for c in range(NC):
        iotaG = (4096 * c + 128 * np.arange(32)[:, None, None] +
                 np.arange(128)[None, None, :] +
                 np.zeros((1, 8, 1), np.int64)).astype(np.int16).reshape(32, -1)
        maskc = np.zeros((E, 32), np.float32)
        maskc[:c, :] = 1.0
        idx8 = np.full((128, 1), -1, np.int16)
        for k in range(8):
            for j in range(8):
                idx8[16 * k + j, 0] = 8 * j + c
        m = dict(
            xs=np.ascontiguousarray(xr[TPC * c:TPC * (c + 1)]),
            wvT=wvT, woT=woT, embT=embT,
            w1T=np.ascontiguousarray(w1[c].T), w2T=np.ascontiguousarray(w2[c].T),
            iotaG=np.ascontiguousarray(iotaG), maskc32=maskc, idx8=idx8,
            **consts,
        )
        in_maps.append(m)
    res = run_bass_kernel_spmd(nc, in_maps, core_ids=list(range(NC)))
    out = np.concatenate([np.asarray(r["out"]) for r in res.results], axis=0)
    return out.reshape(B, S, D)


# version 08
# speedup vs baseline: 1.3368x; 1.2743x over previous
import numpy as np
import concourse.bass as bass
import concourse.mybir as mybir
from concourse.bass_utils import run_bass_kernel_spmd
from concourse import library_config

F32 = mybir.dt.float32
BF16 = mybir.dt.bfloat16
I16 = mybir.dt.int16
U8 = mybir.dt.uint8
U32 = mybir.dt.uint32
AF = mybir.ActivationFunctionType
ALU = mybir.AluOpType
AX = mybir.AxisListType

NC = 8
B, S, D = 16, 2048, 1024
T = B * S
TPC = T // NC
BPC = B // NC
E, I = 8, 2048
CAP = 4096
NT = TPC // 128
EPS = 1e-5
KB = 1024

_CACHE = {}


class Sem:
    def __init__(self, nc, name):
        self.h = nc.alloc_semaphore(name)
        self.n = 0

    def inc(self, inst, k=1):
        inst.then_inc(self.h, k)
        self.n += k
        return self.n


def build(upto=99):
    import os as _os
    S3T = int(_os.environ.get("K_S3T", NT))
    nc = bass.Bass("TRN2", target_bir_lowering=False, num_devices=NC)

    def din(name, shape, dt):
        return nc.dram_tensor(name, shape, dt, kind="ExternalInput").ap()

    xs_d = din("xs", [TPC, D], F32)
    wvT_d = din("wvT", [D, D], F32)
    woT_d = din("woT", [D, D], F32)
    embT_d = din("embT", [D, E], F32)
    w1T_d = din("w1T", [D, I], F32)
    w2T_d = din("w2T", [I, D], F32)
    iden_d = din("iden", [128, 128], F32)
    ones_d = din("onescol", [128, 1], F32)
    onesrow_d = din("onesrow", [1, 128], F32)
    triu16_d = din("triu16", [16, 16], F32)
    triu32_d = din("triu32", [32, 32], F32)
    maskc_d = din("maskc32", [E, 32], F32)
    iota8_d = din("iota8", [128, 8], F32)
    iota8m_d = din("iota8m", [128, 8], F32)
    iotaE_d = din("iotaEB", [32, 8 * 128], F32)
    iotaG_d = din("iotaG", [32, 128], F32)
    idxid_d = din("idxid", [128, 256], I16)
    idx8_d = din("idx8", [128, 1], I16)
    iotasl_d = din("iotasl", [16, 256], F32)
    m1_16_d = din("m1c16", [16, 2048], I16)
    m1_32_d = din("m1c32", [32, 128], F32)
    z16_d = din("z16", [16, 256], I16)

    out_d = nc.dram_tensor("out", [TPC, D], F32, kind="ExternalOutput").ap()
    DBG = bool(_os.environ.get("K_DBG"))
    if DBG:
        o_clist = nc.dram_tensor("o_clist", [16, 256], I16, kind="ExternalOutput").ap()
        o_dots = nc.dram_tensor("o_dots", [8, TPC], F32, kind="ExternalOutput").ap()
        o_pack = nc.dram_tensor("o_pack", [96, 128], F32, kind="ExternalOutput").ap()
        o_gb = nc.dram_tensor("o_gb", [32, 2 * 128], F32, kind="ExternalOutput").ap()
        o_we = nc.dram_tensor("o_we", [128, 2 * NT], F32, kind="ExternalOutput").ap()
        o_con = nc.dram_tensor("o_con", [128, D], BF16, kind="ExternalOutput").ap()
        o_yrow = nc.dram_tensor("o_yrow", [8, D], BF16, kind="ExternalOutput").ap()
        o_sb = nc.dram_tensor("o_sb", [32, 2 * 128], F32, kind="ExternalOutput").ap()
        o_tot = nc.dram_tensor("o_tot", [32, 2 * 8], F32, kind="ExternalOutput").ap()
        o_craw = nc.dram_tensor("o_craw", [16, 2048], I16, kind="ExternalOutput").ap()
        o_ecand = nc.dram_tensor("o_ecand", [16, 2048], I16, kind="ExternalOutput").ap()
        o_nfs = nc.dram_tensor("o_nfs", [1, 8], U32, kind="ExternalOutput").ap()
        o_nff = nc.dram_tensor("o_nff", [1, 4], U32, kind="ExternalOutput").ap()
        o_grank = nc.dram_tensor("o_grank", [32, 8 * 128], F32, kind="ExternalOutput").ap()
        o_csum = nc.dram_tensor("o_csum", [32, 8 * 128], F32, kind="ExternalOutput").ap()
        o_egr = nc.dram_tensor("o_egr", [16, 2048], F32, kind="ExternalOutput").ap()
        o_x1 = nc.dram_tensor("o_x1", [128, D], F32, kind="ExternalOutput").ap()
        o_arow = nc.dram_tensor("o_arow", [1, BPC * D], F32, kind="ExternalOutput").ap()

    def dint(name, shape, dt, shared=False):
        return nc.dram_tensor(name, shape, dt, kind="Internal",
                              addr_space="Shared" if shared else "Local").ap()

    x1_hbm = dint("x1_hbm", [TPC, D], F32)
    yhi_hbm = dint("yhi_hbm", [TPC, D], BF16)
    ylo_hbm = dint("ylo_hbm", [TPC, D], BF16)
    r_bounce = dint("r_bounce", [BPC * D], F32)
    t_bounce = dint("t_bounce", [BPC * D], F32)
    cnt_in = dint("cnt_in", [1, E], F32)
    cnt_all = dint("cnt_all", [NC, E], F32, shared=True)
    cand_in = dint("cand_in", [E, TPC], I16)
    cand_all = dint("cand_all", [NC * E, TPC], I16, shared=True)
    cand_mine = dint("cand_mine", [NC * TPC], I16)
    y_all = dint("y_all", [T, D], BF16, shared=True)
    contrib_in = dint("contrib_in", [CAP, D], BF16)
    contrib_all = dint("contrib_all", [E * CAP, D], BF16, shared=True)
    g1_dram = dint("g1_dram", [TPC], I16)
    g2_dram = dint("g2_dram", [TPC], I16)

    A = nc.alloc_sbuf_tensor
    sb_iden = A("c_iden", [128, 128], F32)
    sb_ones = A("c_ones", [128, 1], F32)
    sb_onesrow = A("c_onesrow", [1, 128], F32)
    sb_triu16 = A("c_triu16", [16, 16], F32)
    sb_triu32 = A("c_triu32", [32, 32], F32)
    sb_maskc = A("c_maskc", [E, 32], F32)
    sb_iota8 = A("c_iota8", [128, 8], F32)
    sb_iota8m = A("c_iota8m", [128, 8], F32)
    sb_iotaE = A("c_iotaE", [32, 8, 128], F32)
    sb_iotaG = A("c_iotaG", [32, 128], F32)
    sb_idxid = A("c_idxid", [128, 256], I16)
    sb_idx8 = A("c_idx8", [128, 1], I16)
    sb_iotasl = A("c_iotasl", [16, 256], F32)
    sb_m116 = A("c_m116", [16, 2048], I16)
    sb_m132 = A("c_m132", [32, 128], F32)
    canI2 = A("r_canI2", [128, 32, 8], I16)
    g_i16 = A("r_gi16", [16, 2, 256], I16)
    sb_z16 = A("c_z16", [16, 256], I16)
    sb_embT = A("c_embT", [128, 8, E], F32)
    sb_ehi = A("c_ehi", [128, 8, E], BF16)
    sb_elo = A("c_elo", [128, 8, E], BF16)
    sb_elof = A("c_elof", [128, 8, E], F32)

    sc_y = A("s_y", [128, D], F32)
    sc_y2 = A("s_y2", [128, D], F32)
    sc_hi = A("s_hi", [128, D], BF16)
    sc_lo = A("s_lo", [128, D], BF16)
    sc_lo2 = A("s_lo2", [128, D], BF16)
    sc_lo3 = A("s_lo3", [128, D], BF16)
    sxC = A("s_sxC", [128, NT], F32)
    sx2C = A("s_sx2C", [128, NT], F32)
    mC = A("s_mC", [128, NT], F32)
    varC = A("s_varC", [128, NT], F32)
    rsC = A("s_rsC", [128, NT], F32)
    nbC = A("s_nbC", [128, NT], F32)
    r3a = A("s_r3a", [128, NT, 8], F32)
    r3b = A("s_r3b", [128, NT, 8], F32)
    mx1C = A("s_mx1C", [128, NT], F32)
    mx2C = A("s_mx2C", [128, NT], F32)
    r_row = A("s_rrow", [1, BPC * D], F32)
    t_sb = A("s_t", [BPC, D], F32)
    a_sb = A("s_a", [BPC, D], F32)
    a_row = A("s_arow", [1, BPC * D], F32)
    rfm = A("s_rfm", [128, 8, BPC], F32)
    tfm = A("s_tfm", [128, 8, BPC], F32)
    wv_tile = A("s_wv", [128, 2, D], F32)
    dots_tm = A("s_dotstm", [128, NT, 8], F32)
    mx8 = A("r_mx8", [128, 8], F32)
    m1r = A("r_m1r", [128, 8], F32)
    m2r = A("r_m2r", [128, 8], F32)
    m1t = A("r_m1t", [128, 8], F32)
    m2t = A("r_m2t", [128, 8], F32)
    cc8 = A("r_cc8", [128, 8], F32)
    packA = A("r_packA", [128, 96], F32)
    packB = A("r_packB", [96, 128], F32)
    ix2B0 = A("r_ix2B0", [32, 128], F32)
    d12B0 = A("r_d12B0", [32, 128], F32)
    totB = A("r_totB", [32, 8], F32)
    pfxT = A("r_pfxT", [32, 8], F32)
    cntsb = A("r_cnt", [1, 8], F32)
    cntall_sb = A("r_cntall", [8, 8], F32)
    sB1 = A("r_sB1", [32, 128], F32)
    sB2 = A("r_sB2", [32, 128], F32)
    w1B = A("r_w1B", [32, 128], F32)
    w2B = A("r_w2B", [32, 128], F32)
    weB = A("r_weB", [32, 2, 128], F32)
    gB = A("r_gB", [32, 2, 128], F32)
    we_tm = A("r_wetm", [128, 2, NT], F32)
    clist = A("e_clist", [16, 256], I16)
    spA = A("e_spA", [16, 512], I16)
    spB = A("e_spB", [16, 512], I16)
    spC = A("e_spC", [16, 512], I16)
    nf_s = A("e_nfs", [1, 8], U32)
    nf_sf = A("e_nfsf", [1, 1], F32)
    nf_sb = A("e_nf", [1, 1], U32)
    nf_f = A("e_nff", [1, 1], F32)
    nfb = A("e_nfb", [16, 1], F32)
    evalid = A("e_valid", [16, 256], U8)
    clrep = A("e_clrep", [128, 256], I16)
    g1il = A("e_g1il", [128, 256], I16)
    g2il = A("e_g2il", [128, 256], I16)
    rowg = A("e_rowg", [128, TPC], I16)
    sb_idg = A("s_idg", [128, 64], I16)

    ARENA = 120 * KB
    _guard = A("arena_guard", [128, ARENA], U8)
    abase = nc.lookup_mloc(_guard).addr

    def AT(name, shape, dt, off):
        n = int(np.prod(shape[1:])) * mybir.dt.size(dt)
        assert off + n <= ARENA, f"{name}: {off}+{n} > {ARENA}"
        return nc.alloc_sbuf_tensor_at(name, shape, dt, offset=abase + off)

    xt2 = AT("p_xt2", [128, 2, D], F32, 0)
    attn_bc = AT("p_attnbc", [128, BPC, D], F32, 8 * KB)
    yfm_hi = AT("p_yfmhi", [128, 2, 8, 512], BF16, 0)
    yfm_lo = AT("p_yfmlo", [128, 2, 8, 512], BF16, 16 * KB)
    dots_em = AT("p_dotsem", [8, TPC], F32, 64 * KB)
    m1B = AT("p_m1B", [32, 8, 128], F32, 80 * KB)
    m2B = AT("p_m2B", [32, 8, 128], F32, 84 * KB)
    onesB = AT("p_onesB", [32, 8, 128], F32, 88 * KB)
    csumB = AT("p_csumB", [32, 8, 128], F32, 92 * KB)
    grankB = AT("p_grankB", [32, 8, 128], F32, 96 * KB)
    survB = AT("p_survB", [32, 8, 128], F32, 100 * KB)
    scrB = AT("p_scrB", [32, 8, 128], F32, 104 * KB)
    candB = AT("p_candB", [32, 8, 128], F32, 108 * KB)
    w1t = AT("p_w1t", [128, 8, I], BF16, 0)
    w2t = AT("p_w2t", [128, 16, D], BF16, 32 * KB)
    craw = AT("p_craw", [16, 8, 256], I16, 64 * KB)
    ecand = AT("p_ecand", [16, 2048], I16, 68 * KB)
    emask = AT("p_emask", [16, 2048], F32, 72 * KB)
    eX1 = AT("p_eX1", [16, 2048], F32, 80 * KB)
    eX2 = AT("p_eX2", [16, 2048], F32, 88 * KB)
    yfm2 = AT("p_yfm2", [128, 2, 8, 512], BF16, 96 * KB)
    inter = AT("p_inter", [128, 16, 512], BF16, 64 * KB)
    ctile = AT("p_ctile", [128, 2, D], BF16, 80 * KB)
    c1b = AT("p_c1b", [128, 2, 4, D], BF16, 0)
    c2b = AT("p_c2b", [128, 2, 4, D], BF16, 16 * KB)
    cxg = AT("p_cxg", [128, 2, 4, D], F32, 32 * KB)
    o1g = AT("p_o1g", [128, 2, 4, D], F32, 64 * KB)
    o2g = AT("p_o2g", [128, 4, D], F32, 96 * KB)

    pb = [nc.alloc_psum_tensor(f"pb{i}", [128, 512], F32) for i in range(8)]

    s_x = Sem(nc, "s_x")
    s_v = Sem(nc, "s_v")
    s_s = Sem(nc, "s_s")
    s_t = Sem(nc, "s_t")
    s_g = Sem(nc, "s_g")
    s_st = Sem(nc, "s_st")
    s_o = Sem(nc, "s_o")
    s_cc = Sem(nc, "s_cc")

    V, SC, TE, GP, SY = nc.vector, nc.scalar, nc.tensor, nc.gpsimd, nc.sync

    def cumsum(out_ap, in_ap):
        return V.tensor_tensor_scan(out=out_ap, data0=in_ap, data1=in_ap,
                                    initial=0.0, op0=ALU.add, op1=ALU.bypass)

    for dst, src in [
        (sb_iden[:], iden_d), (sb_ones[:], ones_d), (sb_onesrow[:], onesrow_d),
        (sb_triu16[:], triu16_d), (sb_triu32[:], triu32_d), (sb_maskc[:], maskc_d),
        (sb_iota8[:], iota8_d), (sb_iota8m[:], iota8m_d), (sb_iotasl[:], iotasl_d),
        (sb_iotaE[:], iotaE_d.rearrange("a (b c) -> a b c", b=8)),
        (sb_iotaG[:], iotaG_d),
        (sb_idxid[:], idxid_d), (sb_idx8[:], idx8_d),
        (sb_m116[:], m1_16_d),
        (sb_m132[:], m1_32_d),
        (sb_z16[:], z16_d),
        (sb_embT[:], embT_d.rearrange("(dc p) e -> p dc e", p=128)),
    ]:
        s_x.inc(SY.dma_start(out=dst, in_=src), 16)
    X0 = s_x.n
    if _os.environ.get("K_AGEARLY"):
        GP.wait_ge(s_x.h, X0)
        s_cc.inc(GP.collective_compute(
            "AllGather", ALU.bypass, replica_groups=[list(range(NC))],
            ins=[cnt_in], outs=[cnt_all]))
    V.wait_ge(s_x.h, X0)
    V.tensor_copy(sb_ehi[:], sb_embT[:])
    V.drain()
    V.tensor_copy(sb_elof[:], sb_ehi[:])
    V.drain()
    V.tensor_tensor(sb_elo[:], sb_embT[:], sb_elof[:], ALU.subtract)
    v_emb = s_v.inc(V.drain())

    if upto < 1:
        SY.wait_ge(s_st.h, s_st.n)
        SY.wait_ge(s_x.h, s_x.n)
        SY.wait_ge(s_v.h, s_v.n)
        SY.wait_ge(s_g.h, s_g.n)
        SY.wait_ge(s_cc.h, s_cc.n)
        return nc
    passA_done = []
    for t in range(NT):
        par = t % 2
        ld = SY.dma_start(out=xt2[:, par, :], in_=xs_d[128 * t:128 * (t + 1), :])
        if t >= 2:
            ld.wait_op(s_v.h, passA_done[t - 2], "sem-ge")
        x_t = s_x.inc(ld, 16)
        xt = xt2[:, par, :]
        SC.wait_ge(s_x.h, x_t)
        sq_t = s_s.inc(SC.activation(out=sc_y2[:], in_=xt, func=AF.Square, accum_out=sx2C[:, t:t + 1]))
        V.wait_ge(s_s.h, sq_t)
        passA_done.append(s_v.inc(V.reduce_sum(sxC[:, t:t + 1], xt, axis=AX.X)))
    V.wait_ge(s_s.h, s_s.n)
    V.drain()
    V.tensor_scalar_mul(mC[:], sxC[:], 1.0 / D)
    V.drain()
    V.tensor_tensor(varC[:], mC[:], mC[:], ALU.mult)
    V.drain()
    V.scalar_tensor_tensor(varC[:], sx2C[:], 1.0 / D, varC[:], ALU.mult, ALU.subtract)
    V.drain()
    v_var = s_v.inc(V.tensor_scalar_add(varC[:], varC[:], EPS))
    SC.wait_ge(s_v.h, v_var)
    sqr = s_s.inc(SC.activation(out=rsC[:], in_=varC[:], func=AF.Sqrt, bias=0.0, scale=1.0))
    V.wait_ge(s_s.h, sqr)
    V.reciprocal(rsC[:], rsC[:])
    V.drain()
    V.tensor_tensor(nbC[:], mC[:], rsC[:], ALU.mult)
    V.drain()
    v_nb = s_v.inc(V.tensor_scalar_mul(nbC[:], nbC[:], -1.0))
    SC.wait_ge(s_v.h, v_nb)
    t_cons = []
    for t in range(NT):
        b = t // 16
        par = t % 2
        first = (t % 16 == 0)
        last = (t % 16 == 15)
        ld = SY.dma_start(out=xt2[:, par, :], in_=xs_d[128 * t:128 * (t + 1), :])
        if t >= 2:
            ld.wait_op(s_s.h, t_cons[t - 2][0], "sem-ge")
        else:
            ld.wait_op(s_v.h, passA_done[-1], "sem-ge")
        x_t = s_x.inc(ld, 16)
        xt = xt2[:, par, :]
        SC.wait_ge(s_x.h, x_t)
        if t >= 2:
            SC.wait_ge(s_t.h, t_cons[t - 2][1])
        yslot = sc_y[:] if par == 0 else sc_y2[:]
        y_t = s_s.inc(SC.activation(out=yslot, in_=xt, func=AF.Identity,
                                    bias=nbC[:, t:t + 1], scale=rsC[:, t:t + 1]))
        TE.wait_ge(s_s.h, y_t)
        TE.matmul(pb[2 * b][0:1, :], sb_ones[:], yslot[:, 0:512] if False else (sc_y[:, 0:512] if par == 0 else sc_y2[:, 0:512]), start=first, stop=last)
        t_r = s_t.inc(TE.matmul(pb[2 * b + 1][0:1, :], sb_ones[:], sc_y[:, 512:1024] if par == 0 else sc_y2[:, 512:1024], start=first, stop=last))
        t_cons.append((y_t, t_r))

    V.wait_ge(s_t.h, s_t.n)
    for b in range(BPC):
        V.tensor_copy(r_row[0:1, D * b:D * b + 512], pb[2 * b][0:1, :])
        V.tensor_copy(r_row[0:1, D * b + 512:D * (b + 1)], pb[2 * b + 1][0:1, :])
    v_r = s_v.inc(V.drain())

    def matvec_fm(src_rows, bounce, fm_tile, w_dram, ps_lo, ps_hi, prod_sem_val):
        SY.wait_ge(s_v.h, prod_sem_val)
        s_st.inc(SY.dma_start(out=bounce.rearrange("(o a) -> o a", o=1), in_=src_rows), 16)
        SY.wait_ge(s_st.h, s_st.n)
        for b_ in range(BPC):
            s_x.inc(SY.dma_start(out=fm_tile[:, :, b_],
                                 in_=bounce.rearrange("(b k p) -> b p k", p=128, k=8)[b_],
                                 allow_slow_non_contiguous=True), 16)
        t_base = s_t.n
        x_base = s_x.n
        for dc in range(8):
            ld = SY.dma_start(out=wv_tile[:, dc % 2, :], in_=w_dram[128 * dc:128 * (dc + 1), :])
            if dc >= 2:
                ld.wait_op(s_t.h, t_base + dc - 1, "sem-ge")
            s_x.inc(ld, 16)
        for dc in range(8):
            TE.wait_ge(s_x.h, x_base + 16 * (dc + 1))
            TE.matmul(ps_lo[0:BPC, :], fm_tile[:, dc, :], wv_tile[:, dc % 2, 0:512],
                      start=(dc == 0), stop=(dc == 7))
            s_t.inc(TE.matmul(ps_hi[0:BPC, :], fm_tile[:, dc, :], wv_tile[:, dc % 2, 512:1024],
                              start=(dc == 0), stop=(dc == 7)))

    matvec_fm(r_row[:], r_bounce, rfm, wvT_d, pb[4], pb[5], v_r)
    V.wait_ge(s_t.h, s_t.n)
    V.tensor_copy(t_sb[:, 0:512], pb[4][0:BPC, :])
    V.tensor_copy(t_sb[:, 512:1024], pb[5][0:BPC, :])
    v_t = s_v.inc(V.drain())
    matvec_fm(t_sb[:], t_bounce, tfm, woT_d, pb[6], pb[7], v_t)
    V.wait_ge(s_t.h, s_t.n)
    V.tensor_copy(a_sb[:, 0:512], pb[6][0:BPC, :])
    V.tensor_copy(a_sb[:, 512:1024], pb[7][0:BPC, :])
    v_a = s_v.inc(V.drain())
    SY.wait_ge(s_v.h, v_a)
    s_st.inc(SY.dma_start(out=t_bounce.rearrange("(b d) -> b d", b=BPC), in_=a_sb[:]), 16)
    SY.wait_ge(s_st.h, s_st.n)
    x_ar = s_x.inc(SY.dma_start(out=a_row[:], in_=t_bounce.rearrange("(o a) -> o a", o=1)), 16)
    TE.wait_ge(s_x.h, x_ar)
    for b in range(BPC):
        TE.matmul(pb[0][:], sb_onesrow[:], a_row[0:1, D * b:D * b + 512], start=True, stop=True)
        t_bc = s_t.inc(TE.matmul(pb[1][:], sb_onesrow[:], a_row[0:1, D * b + 512:D * b + 1024], start=True, stop=True))
        V.wait_ge(s_t.h, t_bc)
        V.tensor_copy(attn_bc[:, b, 0:512], pb[0][:])
        v_bc = s_v.inc(V.tensor_copy(attn_bc[:, b, 512:1024], pb[1][:]))
        TE.wait_ge(s_v.h, v_bc)
    v_abc = s_v.inc(V.drain())

    if upto < 3:
        SY.wait_ge(s_st.h, s_st.n)
        SY.wait_ge(s_x.h, s_x.n)
        SY.wait_ge(s_v.h, s_v.n)
        SY.wait_ge(s_g.h, s_g.n)
        SY.wait_ge(s_cc.h, s_cc.n)
        return nc
    V.wait_ge(s_v.h, v_abc)
    pA = []
    for t in range(S3T):
        par = t % 2
        b = t // 16
        ld = SY.dma_start(out=xt2[:, par, :], in_=xs_d[128 * t:128 * (t + 1), :])
        if t >= 2:
            ld.wait_op(s_st.h, pA[t - 2], "sem-ge")
        x_t = s_x.inc(ld, 16)
        xt = xt2[:, par, :]
        V.wait_ge(s_x.h, x_t)
        v_x1 = s_v.inc(V.tensor_tensor(xt, xt, attn_bc[:, b, :], ALU.add))
        SC.wait_ge(s_v.h, v_x1)
        sq_t = s_s.inc(SC.activation(out=sc_y2[:], in_=xt, func=AF.Square, accum_out=sx2C[:, t:t + 1]))
        V.wait_ge(s_s.h, sq_t)
        s_v.inc(V.reduce_sum(sxC[:, t:t + 1], xt, axis=AX.X))
        pA.append(s_st.inc(SY.dma_start(out=x1_hbm[128 * t:128 * (t + 1), :],
                                        in_=xt).wait_op(s_v.h, s_v.n, "sem-ge"), 16))
    V.wait_ge(s_s.h, s_s.n)
    V.drain()
    V.tensor_scalar_mul(mC[:], sxC[:], 1.0 / D)
    V.drain()
    V.tensor_tensor(varC[:], mC[:], mC[:], ALU.mult)
    V.drain()
    V.scalar_tensor_tensor(varC[:], sx2C[:], 1.0 / D, varC[:], ALU.mult, ALU.subtract)
    V.drain()
    v_var = s_v.inc(V.tensor_scalar_add(varC[:], varC[:], EPS))
    SC.wait_ge(s_v.h, v_var)
    sqr = s_s.inc(SC.activation(out=rsC[:], in_=varC[:], func=AF.Sqrt, bias=0.0, scale=1.0))
    V.wait_ge(s_s.h, sqr)
    V.reciprocal(rsC[:], rsC[:])
    V.drain()
    V.tensor_tensor(nbC[:], mC[:], rsC[:], ALU.mult)
    V.drain()
    v_nb = s_v.inc(V.tensor_scalar_mul(nbC[:], nbC[:], -1.0))
    SC.wait_ge(s_v.h, v_nb)
    tile_done = []
    for t in range(S3T):
        par = t % 2
        ld = SY.dma_start(out=xt2[:, par, :], in_=x1_hbm[128 * t:128 * (t + 1), :])
        ld.wait_op(s_st.h, max(pA[-1], tile_done[t - 2]) if t >= 2 else pA[-1], "sem-ge")
        x_t = s_x.inc(ld, 16)
        xt = xt2[:, par, :]
        SC.wait_ge(s_x.h, x_t)
        if t >= 2:
            SC.wait_ge(s_st.h, tile_done[t - 2])
        yslot = sc_y[:] if par == 0 else sc_y2[:]
        hslot = sc_hi[:, 0:512].bitcast(BF16) if False else (sc_hi[:] if par == 0 else sc_lo[:])
        SC.activation(out=yslot, in_=xt, func=AF.Identity,
                      bias=nbC[:, t:t + 1], scale=rsC[:, t:t + 1])
        hi_t = s_s.inc(SC.activation(out=hslot, in_=xt, func=AF.Identity,
                                     bias=nbC[:, t:t + 1], scale=rsC[:, t:t + 1]))
        V.wait_ge(s_s.h, hi_t)
        lslot = sc_lo2[:] if par == 0 else sc_lo3[:]
        v_lo = s_v.inc(V.tensor_tensor(lslot, yslot, hslot, ALU.subtract))
        s_st.inc(SY.dma_start(out=yhi_hbm[128 * t:128 * (t + 1), :], in_=hslot).wait_op(s_s.h, hi_t, "sem-ge"), 16)
        st_t = s_st.inc(SY.dma_start(out=ylo_hbm[128 * t:128 * (t + 1), :], in_=lslot).wait_op(s_v.h, v_lo, "sem-ge"), 16)
        tile_done.append(st_t)
    ST3 = s_st.n

    if not _os.environ.get("K_NOGPW"):
        GP.wait_ge(s_st.h, ST3)
    if _os.environ.get("K_NOCC"):
        cc_y = 0
    elif _os.environ.get("K_TINYAG"):
        cc_y = s_cc.inc(GP.collective_compute(
            "AllGather", ALU.bypass, replica_groups=[list(range(NC))],
            ins=[cnt_in], outs=[cnt_all]))
    else:
        cc_y = s_cc.inc(GP.collective_compute(
            "AllGather", ALU.bypass, replica_groups=[list(range(NC))],
            ins=[yhi_hbm], outs=[y_all]))

    if upto < 4:
        SY.wait_ge(s_st.h, s_st.n)
        SY.wait_ge(s_x.h, s_x.n)
        SY.wait_ge(s_v.h, s_v.n)
        SY.wait_ge(s_g.h, s_g.n)
        SY.wait_ge(s_cc.h, s_cc.n)
        return nc
    GP.load_library(library_config.mlp)
    V.wait_ge(s_v.h, v_emb)
    dots_mm = []
    for g in range(0 if _os.environ.get("K_NODG") else 8):
        gp_ = g % 2
        GP.wait_ge(s_st.h, ST3)
        if g >= 2:
            GP.wait_ge(s_t.h, dots_mm[g - 2])
        s_g.inc(GP.dma_gather(
            yfm_hi[:, gp_], yhi_hbm, sb_idxid[:, 32 * g:32 * (g + 1)], 512, 512, D,
            elem_step=D, transpose=True), 16)
        g_lo = s_g.inc(GP.dma_gather(
            yfm_lo[:, gp_], ylo_hbm, sb_idxid[:, 32 * g:32 * (g + 1)], 512, 512, D,
            elem_step=D, transpose=True), 16)
        TE.wait_ge(s_g.h, g_lo)
        ps = pb[2 + gp_]
        if g >= 2:
            TE.wait_ge(s_v.h, s_v.n)
        n_mm = 0
        t_d = None
        for (yt, et) in ((yfm_hi, sb_ehi), (yfm_hi, sb_elo), (yfm_lo, sb_ehi), (yfm_lo, sb_elo)):
            for dc in range(8):
                inst = TE.matmul(ps[0:8, :], et[:, dc, :], yt[:, gp_, dc, :],
                                 start=(n_mm == 0), stop=(n_mm == 31))
                n_mm += 1
                if n_mm == 32:
                    t_d = s_t.inc(inst)
        dots_mm.append(t_d)
        V.wait_ge(s_t.h, t_d)
        s_v.inc(V.tensor_copy(dots_em[:, 512 * g:512 * (g + 1)], ps[0:8, :]))

    if dots_mm:
        GP.wait_ge(s_t.h, dots_mm[-1])
    w_ffn = s_g.n
    if not _os.environ.get("K_NOW"):
        for icg in range(8):
            s_g.inc(GP.dma_start(out=w1t[:, icg, :],
                                 in_=w1T_d.rearrange("(dc p) i -> p dc i", p=128)[:, icg, :]), 16)
        for icg in range(16):
            s_g.inc(GP.dma_start(out=w2t[:, icg, :], in_=w2T_d[128 * icg:128 * (icg + 1), :]), 16)
        w_ffn = s_g.n

    if upto < 5:
        SY.wait_ge(s_st.h, s_st.n)
        SY.wait_ge(s_x.h, s_x.n)
        SY.wait_ge(s_v.h, s_v.n)
        SY.wait_ge(s_g.h, s_g.n)
        SY.wait_ge(s_cc.h, s_cc.n)
        return nc
    V.wait_ge(s_v.h, s_v.n)
    for t in range(NT):
        ps = pb[4 + t % 2]
        t_tr = s_t.inc(TE.transpose(ps[0:128, 0:8], dots_em[:, 128 * t:128 * (t + 1)], sb_iden[0:8, 0:8]))
        V.wait_ge(s_t.h, t_tr)
        v_tr = s_v.inc(V.tensor_copy(dots_tm[:, t, :], ps[0:128, 0:8]))
        TE.wait_ge(s_v.h, v_tr)
    dots3 = dots_tm[:]
    iota3 = sb_iota8[:].unsqueeze(1).to_broadcast([128, NT, 8])
    iotam3 = sb_iota8m[:].unsqueeze(1).to_broadcast([128, NT, 8])
    V.wait_ge(s_v.h, s_v.n)
    V.drain()
    V.reduce_max(mx1C[:], dots3, axis=AX.X)
    V.drain()
    V.tensor_tensor(r3a[:], dots3, mx1C[:].unsqueeze(2).to_broadcast([128, NT, 8]), ALU.is_equal)
    V.drain()
    V.tensor_tensor(r3b[:], r3a[:], iotam3, ALU.mult)
    V.drain()
    V.tensor_reduce(packA[:, 0:32], r3b[:], axis=AX.X, op=ALU.min)
    V.drain()
    V.tensor_scalar_add(packA[:, 0:32], packA[:, 0:32], 1024.0)
    V.drain()
    V.tensor_tensor(r3a[:], iota3, packA[:, 0:32].unsqueeze(2).to_broadcast([128, NT, 8]), ALU.is_equal)
    V.drain()
    V.scalar_tensor_tensor(r3b[:], r3a[:], -1e30, dots3, ALU.mult, ALU.add)
    V.drain()
    V.reduce_max(mx2C[:], r3b[:], axis=AX.X)
    V.drain()
    V.tensor_tensor(r3a[:], r3b[:], mx2C[:].unsqueeze(2).to_broadcast([128, NT, 8]), ALU.is_equal)
    V.drain()
    V.tensor_tensor(r3b[:], r3a[:], iotam3, ALU.mult)
    V.drain()
    V.tensor_reduce(packA[:, 32:64], r3b[:], axis=AX.X, op=ALU.min)
    V.drain()
    V.tensor_scalar_add(packA[:, 32:64], packA[:, 32:64], 1024.0)
    V.drain()
    V.tensor_tensor(packA[:, 64:96], mx1C[:], mx2C[:], ALU.subtract)
    V.drain()
    v_pack = s_v.inc(V.drain())
    TE.wait_ge(s_v.h, v_pack)
    t_pk = s_t.inc(TE.transpose(pb[6][0:96, 0:128], packA[:, 0:96], sb_iden[:]))
    V.wait_ge(s_t.h, t_pk)
    v_pk = s_v.inc(V.tensor_copy(packB[:], pb[6][0:96, 0:128]))
    SY.wait_ge(s_v.h, v_pk)
    s_x.inc(SY.dma_start(out=ix2B0[:], in_=packB[32:64, :]), 16)
    s_x.inc(SY.dma_start(out=d12B0[:], in_=packB[64:96, :]), 16)
    x_pk = s_x.n
    ix1B = packB[0:32, :]
    ix2B = ix2B0[:]
    d12B = d12B0[:]
    SC.wait_ge(s_x.h, x_pk)
    V.wait_ge(s_x.h, x_pk)
    SC.wait_ge(s_v.h, v_pk)
    SC.activation(out=w1B[:], in_=d12B, func=AF.Sigmoid, bias=0.0, scale=1.0)
    s_s.inc(SC.activation(out=w2B[:], in_=d12B, func=AF.Sigmoid, bias=0.0, scale=-1.0))
    V.wait_ge(s_v.h, v_pk)
    V.tensor_tensor(m1B[:], ix1B.unsqueeze(1).to_broadcast([32, 8, 128]), sb_iotaE[:], ALU.is_equal)
    V.tensor_tensor(m2B[:], ix2B.unsqueeze(1).to_broadcast([32, 8, 128]), sb_iotaE[:], ALU.is_equal)
    V.drain()
    V.tensor_tensor(onesB[:], m1B[:], m2B[:], ALU.add)
    V.drain()
    for e in range(E):
        cumsum(csumB[:, e, :], onesB[:, e, :])
        V.drain()
    V.tensor_copy(totB[:], csumB[:, :, 127])
    v_tot = s_v.inc(V.drain())
    TE.wait_ge(s_v.h, v_tot)
    t_cnt = s_t.inc(TE.matmul(pb[7][0:1, 0:8], sb_ones[0:32, :], totB[:], start=True, stop=True))
    V.wait_ge(s_t.h, t_cnt)
    v_cnt = s_v.inc(V.tensor_copy(cntsb[:], pb[7][0:1, 0:8]))
    SY.wait_ge(s_v.h, v_cnt)
    s_st.inc(SY.dma_start(out=cnt_in, in_=cntsb[:]), 16)
    GP.wait_ge(s_st.h, s_st.n)
    cc_cnt = s_cc.inc(GP.collective_compute(
        "AllGather", ALU.bypass, replica_groups=[list(range(NC))],
        ins=[cnt_in], outs=[cnt_all]))
    V.tensor_scalar(survB[:], onesB[:], 0.5, None, ALU.is_gt)
    v_su = s_v.inc(V.drain())
    sU8 = AT("p_sU8", [32, 8, 128], U8, 112 * KB)
    V.tensor_copy(sU8[:], survB[:])
    V.drain()
    V.select(candB[:], sU8[:], sb_iotaG[:].unsqueeze(1).to_broadcast([32, 8, 128]),
             sb_m132[:].unsqueeze(1).to_broadcast([32, 8, 128]))
    V.drain()
    cand4 = survB[:].rearrange("p a (b c) -> p b a c", b=8, c=16)
    v_cand = s_v.inc(V.tensor_copy(cand4, candB[:]))
    TE.wait_ge(s_v.h, v_cand)
    t_ct = None
    for fc in range(8):
        t_ct = s_t.inc(TE.transpose(
            pb[0][0:128, 32 * fc:32 * (fc + 1)],
            survB[:, fc, :], sb_iden[0:32, 0:32]))
    V.wait_ge(s_t.h, t_ct)
    v_ci = s_v.inc(V.tensor_copy(canI2[:].rearrange("p a b -> p (a b)"),
                                 pb[0][0:128, 0:256].rearrange("p (a b) -> p b a", a=8)))
    SY.wait_ge(s_v.h, v_ci)
    s_st.inc(SY.dma_start(out=cand_in.rearrange("e (q c) -> (e q) c", q=16),
                          in_=canI2[:].rearrange("p a b -> p (a b)")), 16)
    GP.wait_ge(s_st.h, s_st.n)
    cc_cand = s_cc.inc(GP.collective_compute(
        "AllGather", ALU.bypass, replica_groups=[list(range(NC))],
        ins=[cand_in], outs=[cand_all]))
    SY.wait_ge(s_cc.h, cc_cnt)
    x_cnt = s_x.inc(SY.dma_start(out=cntall_sb[:], in_=cnt_all), 16)
    TE.wait_ge(s_x.h, x_cnt)
    TE.wait_ge(s_v.h, v_cnt)
    TE.matmul(pb[7][0:32, 0:8], sb_triu32[:], totB[:], start=True, stop=False)
    t_pfx = s_t.inc(TE.matmul(pb[7][0:32, 0:8], sb_maskc[:], cntall_sb[:], start=False, stop=True))
    V.wait_ge(s_t.h, t_pfx)
    v_pfx = s_v.inc(V.tensor_copy(pfxT[:], pb[7][0:32, 0:8]))
    V.wait_ge(s_v.h, v_pfx)
    V.tensor_tensor(grankB[:], csumB[:], pfxT[:].unsqueeze(2).to_broadcast([32, 8, 128]), ALU.add)
    v_gr_tap = s_v.inc(V.drain())
    if DBG:
        s_o.inc(SY.dma_start(out=o_grank, in_=grankB[:].rearrange("a b c -> a (b c)")).wait_op(s_v.h, v_gr_tap, "sem-ge"), 16)
        s_o.inc(SY.dma_start(out=o_csum, in_=csumB[:].rearrange("a b c -> a (b c)")).wait_op(s_v.h, v_gr_tap, "sem-ge"), 16)
    V.tensor_scalar(survB[:], grankB[:], float(CAP), None, ALU.is_le)
    V.drain()
    V.tensor_tensor(survB[:], survB[:], onesB[:], ALU.mult)
    V.drain()
    def sum_e(dst, msk, val):
        V.tensor_tensor(scrB[:], msk, val, ALU.mult)
        V.drain()
        V.tensor_tensor(dst, scrB[:, 0, :], scrB[:, 1, :], ALU.add)
        for e_ in range(2, E):
            V.drain()
            V.tensor_tensor(dst, dst, scrB[:, e_, :], ALU.add)
        V.drain()

    sum_e(sB1[:], m1B[:], grankB[:])
    sum_e(sB2[:], m2B[:], grankB[:])
    V.tensor_scalar(sB1[:], sB1[:], -1.0, None, ALU.add)
    V.tensor_scalar(sB2[:], sB2[:], -1.0, None, ALU.add)
    V.drain()
    V.tensor_scalar(sB1[:], sB1[:], float(CAP - 1), None, ALU.min)
    V.tensor_scalar(sB2[:], sB2[:], float(CAP - 1), None, ALU.min)
    V.drain()
    for k, (sBk, ixk) in enumerate(((sB1, ix1B), (sB2, ix2B))):
        V.tensor_scalar(scrB[:, 0, :], sBk[:], 1024.0 - 0.5, None, ALU.is_ge)
        V.tensor_scalar(scrB[:, 1, :], sBk[:], 2048.0 - 0.5, None, ALU.is_ge)
        V.tensor_scalar(scrB[:, 2, :], sBk[:], 3072.0 - 0.5, None, ALU.is_ge)
        V.drain()
        V.tensor_tensor(scrB[:, 0, :], scrB[:, 0, :], scrB[:, 1, :], ALU.add)
        V.drain()
        V.tensor_tensor(scrB[:, 0, :], scrB[:, 0, :], scrB[:, 2, :], ALU.add)
        V.drain()
        V.scalar_tensor_tensor(scrB[:, 2, :], scrB[:, 0, :], 7168.0, sBk[:], ALU.mult, ALU.add)
        V.drain()
        V.scalar_tensor_tensor(gB[:, k, :], ixk, 1024.0, scrB[:, 2, :], ALU.mult, ALU.add)
        V.drain()
    v_g = s_v.inc(V.drain())
    sum_e(sB1[:], m1B[:], survB[:])
    sum_e(sB2[:], m2B[:], survB[:])
    V.wait_ge(s_s.h, s_s.n)
    V.tensor_tensor(weB[:, 0, :], w1B[:], sB1[:], ALU.mult)
    V.tensor_tensor(weB[:, 1, :], w2B[:], sB2[:], ALU.mult)
    v_we = s_v.inc(V.drain())
    TE.wait_ge(s_v.h, v_we)
    for k in range(2):
        t_we = s_t.inc(TE.transpose(pb[6][0:128, 0:32], weB[:, k, :], sb_iden[0:32, 0:32]))
        V.wait_ge(s_t.h, t_we)
        v_wt = s_v.inc(V.tensor_copy(we_tm[:, k, :], pb[6][0:128, 0:32]))
        TE.wait_ge(s_v.h, v_wt)
    TE.wait_ge(s_v.h, v_g)
    t_gt = None
    for k in range(2):
        for fc in range(8):
            t_gt = s_t.inc(TE.transpose(
                pb[1][0:16, 256 * k + 32 * fc:256 * k + 32 * (fc + 1)],
                gB[:, k, 16 * fc:16 * (fc + 1)], sb_iden[0:32, 0:32]))
    V.wait_ge(s_t.h, t_gt)
    V.tensor_copy(g_i16[:, 0, :], pb[1][0:16, 0:256].rearrange("p (a b) -> p b a", a=8))
    v_gi = s_v.inc(V.tensor_copy(g_i16[:, 1, :], pb[1][0:16, 256:512].rearrange("p (a b) -> p b a", a=8)))
    SY.wait_ge(s_v.h, v_gi)
    for k in range(8):
        s_x.inc(SY.dma_start(out=g1il[16 * k:16 * (k + 1), :], in_=g_i16[:, 0, :]), 16)
        s_x.inc(SY.dma_start(out=g2il[16 * k:16 * (k + 1), :], in_=g_i16[:, 1, :]), 16)
    x_gil = s_x.n

    if upto < 6:
        SY.wait_ge(s_st.h, s_st.n)
        SY.wait_ge(s_x.h, s_x.n)
        SY.wait_ge(s_v.h, s_v.n)
        SY.wait_ge(s_g.h, s_g.n)
        SY.wait_ge(s_cc.h, s_cc.n)
        return nc
    GP.wait_ge(s_cc.h, cc_cand)
    g_rows = s_g.inc(GP.dma_gather(
        rowg[:].rearrange("p (c n) -> p c n", c=1), cand_all, sb_idx8[:], 8, 8, TPC,
        elem_step=TPC, transpose=False), 16)
    SY.wait_ge(s_g.h, g_rows)
    st_cm = s_st.inc(SY.dma_start(out=cand_mine.rearrange("(r n) -> r n", r=8), in_=rowg[0:8, :]), 16)
    if _os.environ.get("K_E6") == "1":
        SY.wait_ge(s_st.h, s_st.n)
        SY.wait_ge(s_x.h, s_x.n)
        SY.wait_ge(s_v.h, s_v.n)
        SY.wait_ge(s_g.h, s_g.n)
        SY.wait_ge(s_cc.h, s_cc.n)
        return nc
    SY.wait_ge(s_st.h, st_cm)
    x_cr = s_x.inc(SY.dma_start(out=craw[:],
                                in_=cand_mine.rearrange("(blk q c) -> q blk c", q=16, c=256)
                                ).wait_op(s_v.h, v_we, "sem-ge"), 16)
    if _os.environ.get("K_E6") == "2":
        SY.wait_ge(s_st.h, s_st.n)
        SY.wait_ge(s_x.h, s_x.n)
        SY.wait_ge(s_v.h, s_v.n)
        SY.wait_ge(s_g.h, s_g.n)
        SY.wait_ge(s_cc.h, s_cc.n)
        return nc
    V.wait_ge(s_x.h, x_cr)
    V.wait_ge(s_v.h, v_we)
    crawf = craw[:].rearrange("p a b -> p (a b)")
    if DBG:
        s_o.inc(SY.dma_start(out=o_craw, in_=crawf).wait_op(s_x.h, x_cr, "sem-ge"), 16)
    V.tensor_scalar(emask[:], crawf, 0.0, None, ALU.is_ge)
    v_em2 = s_v.inc(V.drain())
    TE.wait_ge(s_v.h, v_em2)
    for q in range(4):
        s_t.inc(TE.matmul(pb[q][0:1, :], sb_ones[0:16, :], emask[:, 512 * q:512 * (q + 1)],
                          start=True, stop=True))
    V.wait_ge(s_t.h, s_t.n)
    for q in range(4):
        V.tensor_copy(eX1[0:1, 512 * q:512 * (q + 1)], pb[q][0:1, :])
    V.drain()
    cumsum(eX2[0:1, :], eX1[0:1, :])
    V.drain()
    v_col = s_v.inc(V.tensor_tensor(eX2[0:1, :], eX2[0:1, :], eX1[0:1, :], ALU.subtract))
    TE.wait_ge(s_v.h, v_col)
    for q in range(4):
        TE.matmul(pb[4 + (q % 2)][0:16, :], sb_triu16[:], emask[:, 512 * q:512 * (q + 1)],
                  start=True, stop=False)
        t_p = s_t.inc(TE.matmul(pb[4 + (q % 2)][0:16, :], sb_onesrow[0:1, 0:16],
                                eX2[0:1, 512 * q:512 * (q + 1)], start=False, stop=True))
        V.wait_ge(s_t.h, t_p)
        v_ep = s_v.inc(V.tensor_copy(eX1[:, 512 * q:512 * (q + 1)], pb[4 + (q % 2)][0:16, :]))
        TE.wait_ge(s_v.h, v_ep)
    V.tensor_tensor(eX1[:], eX1[:], emask[:], ALU.add)
    V.drain()
    V.tensor_scalar(eX2[:], eX1[:], float(CAP), None, ALU.is_le)
    V.drain()
    V.tensor_tensor(eX2[:], eX2[:], emask[:], ALU.mult)
    V.drain()
    eU8 = eX1[:].bitcast(U8)[:, 0:2048]
    V.tensor_copy(eU8, eX2[:])
    V.drain()
    v_ec = s_v.inc(V.select(ecand[:], eU8, crawf, sb_m116[:]))
    if DBG:
        s_o.inc(SY.dma_start(out=o_ecand, in_=ecand[:]).wait_op(s_v.h, v_ec, "sem-ge"), 16)
        s_o.inc(SY.dma_start(out=o_egr, in_=eX2[:]).wait_op(s_v.h, v_ec, "sem-ge"), 16)
    if _os.environ.get("K_E6") == "3":
        SY.wait_ge(s_st.h, s_st.n)
        SY.wait_ge(s_x.h, s_x.n)
        SY.wait_ge(s_v.h, s_v.n)
        SY.wait_ge(s_g.h, s_g.n)
        SY.wait_ge(s_cc.h, s_cc.n)
        return nc
    GP.load_library(library_config.sparse_gather)
    GP.wait_ge(s_v.h, v_ec)

    def spg(in_ap, out_ap, nf_cell, fill_ap):
        g_ = s_g.inc(GP.sparse_gather(out=out_ap, in_=in_ap, num_found=nf_cell))
        V.wait_ge(s_g.h, g_)
        v1 = s_v.inc(V.tensor_copy(nf_sf[:], nf_cell))
        TE.wait_ge(s_v.h, v1)
        t1 = s_t.inc(TE.matmul(pb[6][0:16, 0:1], sb_onesrow[0:1, 0:16], nf_sf[:], start=True, stop=True))
        V.wait_ge(s_t.h, t1)
        V.tensor_copy(nfb[:], pb[6][0:16, 0:1])
        V.drain()
        V.tensor_tensor(evalid[:], sb_iotasl[:], nfb[:].to_broadcast([16, 256]), ALU.is_ge)
        V.drain()
        v2 = s_v.inc(V.copy_predicated(out_ap, evalid[:], fill_ap))
        GP.wait_ge(s_v.h, v2)
        return v2

    for q in range(4):
        spg(ecand[:, 512 * q:512 * (q + 1)],
            (spA if q < 2 else spB)[:, 256 * (q % 2):256 * (q % 2 + 1)],
            nf_s[0:1, q:q + 1], sb_m116[:, 0:256])
    spg(spA[:], spC[:, 0:256], nf_s[0:1, 4:5], sb_m116[:, 0:256])
    spg(spB[:], spC[:, 256:512], nf_s[0:1, 5:6], sb_m116[:, 0:256])
    v_cl0 = spg(spC[:], clist[:], nf_sb[:], sb_z16[:])
    GP.load_library(library_config.mlp)
    SY.wait_ge(s_v.h, v_cl0)
    for k in range(8):
        s_x.inc(SY.dma_start(out=clrep[16 * k:16 * (k + 1), :], in_=clist[:]), 16)
    x_clr = s_x.n

    if _os.environ.get("K_E6") == "5":
        SY.wait_ge(s_st.h, s_st.n)
        SY.wait_ge(s_x.h, s_x.n)
        SY.wait_ge(s_v.h, s_v.n)
        SY.wait_ge(s_g.h, s_g.n)
        SY.wait_ge(s_cc.h, s_cc.n)
        return nc
    if DBG:
        GP.wait_ge(s_o.h, s_o.n)
    GP.wait_ge(s_cc.h, cc_y)
    GP.wait_ge(s_x.h, x_clr)
    TE.wait_ge(s_v.h, v_gi)
    relu_done = {}
    st_ct = []
    gath_y = []
    mm1_last = {}
    for tg in range(8):
        gpar = tg % 2
        if tg >= 2:
            GP.wait_ge(s_t.h, mm1_last[tg - 2])
        g_y = s_g.inc(GP.dma_gather(
            yfm2[:, gpar], y_all, clrep[:, 32 * tg:32 * (tg + 1)], 512, 512, D,
            elem_step=D, transpose=True), 16)
        gath_y.append(g_y)
        TE.wait_ge(s_g.h, max(g_y, w_ffn))
        for ic in range(16):
            ps = pb[ic % 2]
            if (tg, ic - 2) in relu_done:
                TE.wait_ge(s_s.h, relu_done[(tg, ic - 2)])
            elif tg > 0 and ic < 2:
                TE.wait_ge(s_s.h, relu_done[(tg - 1, 14 + ic)])
            t_i = None
            for dc in range(8):
                inst = TE.matmul(ps[:], w1t[:, dc, 128 * ic:128 * (ic + 1)], yfm2[:, gpar, dc, :],
                                 start=(dc == 0), stop=(dc == 7))
                if dc == 7:
                    t_i = s_t.inc(inst)
            if ic == 15:
                mm1_last[tg] = t_i
            SC.wait_ge(s_t.h, t_i)
            relu_done[(tg, ic)] = s_s.inc(
                SC.activation(out=inter[:, ic, :], in_=ps[:], func=AF.Relu, bias=0.0, scale=1.0))
        TE.wait_ge(s_s.h, relu_done[(tg, 15)])
        for sub in range(4):
            cpar = sub % 2
            for hh in range(2):
                ps = pb[4 + 2 * cpar + hh]
                t_o = None
                for ic in range(16):
                    inst = TE.matmul(ps[:], inter[:, ic, 128 * sub:128 * (sub + 1)],
                                     w2t[:, ic, 512 * hh:512 * (hh + 1)],
                                     start=(ic == 0), stop=(ic == 15))
                    if ic == 15:
                        t_o = s_t.inc(inst)
                V.wait_ge(s_t.h, t_o)
                if len(st_ct) >= 2:
                    V.wait_ge(s_st.h, st_ct[-2])
                v_o = s_v.inc(V.tensor_copy(ctile[:, cpar, 512 * hh:512 * (hh + 1)], ps[:]))
                TE.wait_ge(s_v.h, v_o)
            row0 = 512 * tg + 128 * sub
            st_ct.append(s_st.inc(SY.dma_start(
                out=contrib_in[row0:row0 + 128, :],
                in_=ctile[:, cpar, :]).wait_op(s_v.h, s_v.n, "sem-ge"), 16))
    cc_con = None
    for k in range(4):
        GP.wait_ge(s_st.h, st_ct[4 * (2 * k + 1) + 3])
        cc_con = s_cc.inc(GP.collective_compute(
            "AllGather", ALU.bypass, replica_groups=[list(range(NC))],
            ins=[contrib_in[1024 * k:1024 * (k + 1), :]],
            outs=[contrib_all[8192 * k:8192 * (k + 1), :]]))

    if upto < 7:
        SY.wait_ge(s_st.h, s_st.n)
        SY.wait_ge(s_x.h, s_x.n)
        SY.wait_ge(s_v.h, s_v.n)
        SY.wait_ge(s_g.h, s_g.n)
        SY.wait_ge(s_cc.h, s_cc.n)
        return nc
    GP.wait_ge(s_cc.h, cc_con)
    GP.wait_ge(s_x.h, x_gil)
    fin = []
    o_base = s_o.n
    for g in range(8):
        gpar = g % 2
        if g >= 2:
            GP.wait_ge(s_v.h, fin[g - 2])
        g_c1 = s_g.inc(GP.dma_gather(
            c1b[:, gpar], contrib_all, g1il[:, 32 * g:32 * (g + 1)], 512, 512, D,
            elem_step=D, transpose=False), 16)
        g_c2 = s_g.inc(GP.dma_gather(
            c2b[:, gpar], contrib_all, g2il[:, 32 * g:32 * (g + 1)], 512, 512, D,
            elem_step=D, transpose=False), 16)
        ld = GP.dma_start(out=cxg[:, gpar], in_=x1_hbm[512 * g:512 * (g + 1), :].rearrange("(a p) d -> p a d", p=128))
        if g >= 2:
            ld.wait_op(s_v.h, fin[g - 2], "sem-ge")
        x_t = s_x.inc(ld, 16)
        we1g = we_tm[:, 0, 4 * g:4 * (g + 1)].unsqueeze(2).to_broadcast([128, 4, D])
        we2g = we_tm[:, 1, 4 * g:4 * (g + 1)].unsqueeze(2).to_broadcast([128, 4, D])
        V.wait_ge(s_g.h, g_c2)
        V.wait_ge(s_x.h, x_t)
        if g >= 2:
            V.wait_ge(s_o.h, o_base + 16 * (g - 1))
        V.tensor_tensor(o1g[:, gpar], c1b[:, gpar], we1g, ALU.mult)
        V.tensor_tensor(o2g[:], c2b[:, gpar], we2g, ALU.mult)
        V.drain()
        V.tensor_tensor(o1g[:, gpar], o1g[:, gpar], cxg[:, gpar], ALU.add)
        V.drain()
        v_f = s_v.inc(V.tensor_tensor(o1g[:, gpar], o1g[:, gpar], o2g[:], ALU.add))
        fin.append(v_f)
        s_o.inc(SY.dma_start(out=out_d[512 * g:512 * (g + 1), :].rearrange("(a p) d -> p a d", p=128),
                             in_=o1g[:, gpar]).wait_op(s_v.h, v_f, "sem-ge"), 16)
    if DBG:
        SY.wait_ge(s_v.h, s_v.n)
        SY.wait_ge(s_g.h, s_g.n)
        s_o.inc(SY.dma_start(out=o_clist, in_=clist[:]), 16)
        s_o.inc(SY.dma_start(out=o_dots, in_=dots_em[:]), 16)
        s_o.inc(SY.dma_start(out=o_pack, in_=packB[:]), 16)
        s_o.inc(SY.dma_start(out=o_gb, in_=gB[:].rearrange("a b c -> a (b c)")), 16)
        s_o.inc(SY.dma_start(out=o_we, in_=we_tm[:].rearrange("a b c -> a (b c)")), 16)
        s_o.inc(SY.dma_start(out=o_con, in_=contrib_in[0:128, :]), 16)
        s_o.inc(SY.dma_start(out=o_yrow, in_=y_all[0:8, :]), 16)
        s_o.inc(SY.dma_start(out=o_sb[:, 0:128], in_=sB1[:]), 16)
        s_o.inc(SY.dma_start(out=o_sb[:, 128:256], in_=sB2[:]), 16)
        s_o.inc(SY.dma_start(out=o_tot[:, 0:8], in_=totB[:]), 16)
        s_o.inc(SY.dma_start(out=o_tot[:, 8:16], in_=pfxT[:]), 16)
        s_o.inc(SY.dma_start(out=o_x1, in_=x1_hbm[2048:2176, :]), 16)
        s_o.inc(SY.dma_start(out=o_arow, in_=a_row[:]), 16)
        s_o.inc(SY.dma_start(out=o_nfs, in_=nf_s[:]), 16)
        s_o.inc(SY.dma_start(out=o_nff[0:1, 0:1], in_=nf_sb[:]), 16)
    SY.wait_ge(s_o.h, s_o.n)
    return nc


def _host_consts():
    c = {}
    c["iden"] = np.eye(128, dtype=np.float32)
    c["onescol"] = np.ones((128, 1), np.float32)
    c["onesrow"] = np.ones((1, 128), np.float32)
    c["triu16"] = np.triu(np.ones((16, 16), np.float32), 1)
    c["triu32"] = np.triu(np.ones((32, 32), np.float32), 1)
    c["iota8"] = np.tile(np.arange(8, dtype=np.float32), (128, 1))
    c["iota8m"] = np.tile(np.arange(8, dtype=np.float32) - 1024.0, (128, 1))
    c["iotaEB"] = np.ascontiguousarray(
        np.tile(np.arange(8, dtype=np.float32)[None, :, None], (32, 1, 128)).reshape(32, -1))
    idxid = np.zeros((16, 256), np.int16)
    for i_ in range(4096):
        idxid[i_ % 16, i_ // 16] = i_
    c["idxid"] = np.tile(idxid, (8, 1))
    iotasl = np.zeros((16, 256), np.float32)
    for j in range(4096):
        iotasl[j % 16, j // 16] = j
    c["iotasl"] = iotasl
    c["m1c16"] = np.full((16, 2048), -1, np.int16)
    c["m1c32"] = np.full((32, 128), -1, np.float32)
    c["z16"] = np.zeros((16, 256), np.int16)
    return c


def kernel(L0_attn_x, L0_attn_w_qkv, L0_attn_w_o, L0_ffn_expert_embeddings,
           L0_ffn_first_linear, L0_ffn_second_linear):
    x = np.ascontiguousarray(np.asarray(L0_attn_x, dtype=np.float32))
    wqkv = np.asarray(L0_attn_w_qkv, dtype=np.float32)
    wo = np.asarray(L0_attn_w_o, dtype=np.float32)
    emb = np.asarray(L0_ffn_expert_embeddings, dtype=np.float32)
    w1 = np.asarray(L0_ffn_first_linear, dtype=np.float32)
    w2 = np.asarray(L0_ffn_second_linear, dtype=np.float32)

    wvT = np.ascontiguousarray(wqkv[2 * D:3 * D, :].T)
    woT = np.ascontiguousarray(wo.T)
    embT = np.ascontiguousarray(emb.T)
    consts = _host_consts()

    import os as _os
    _upto = int(_os.environ.get("K_UPTO", 99))
    if _CACHE.get("upto") != _upto:
        _CACHE.clear()
        _CACHE["upto"] = _upto
        _CACHE["nc"] = build(_upto)
        mybir.codegen_inst_isa_subclasses(_CACHE["nc"])
    nc = _CACHE["nc"]

    xr = x.reshape(T, D)
    in_maps = []
    for c in range(NC):
        iotaG = (4096 * c + 128 * np.arange(32)[:, None] +
                 np.arange(128)[None, :]).astype(np.float32)
        maskc = np.zeros((E, 32), np.float32)
        maskc[:c, :] = 1.0
        idx8 = np.full((128, 1), -1, np.int16)
        for k in range(8):
            for j in range(8):
                idx8[16 * k + j, 0] = 8 * j + c
        m = dict(
            xs=np.ascontiguousarray(xr[TPC * c:TPC * (c + 1)]),
            wvT=wvT, woT=woT, embT=embT,
            w1T=np.ascontiguousarray(w1[c].T), w2T=np.ascontiguousarray(w2[c].T),
            iotaG=np.ascontiguousarray(iotaG), maskc32=maskc, idx8=idx8,
            **consts,
        )
        in_maps.append(m)
    res = run_bass_kernel_spmd(nc, in_maps, core_ids=list(range(NC)))
    out = np.concatenate([np.asarray(r["out"]) for r in res.results], axis=0)
    return out.reshape(B, S, D)


# version 09
# speedup vs baseline: 1.3518x; 1.0112x over previous
import numpy as np
import concourse.bass as bass
import concourse.mybir as mybir
from concourse.bass_utils import run_bass_kernel_spmd
from concourse import library_config

F32 = mybir.dt.float32
BF16 = mybir.dt.bfloat16
I16 = mybir.dt.int16
U8 = mybir.dt.uint8
U32 = mybir.dt.uint32
AF = mybir.ActivationFunctionType
ALU = mybir.AluOpType
AX = mybir.AxisListType

NC = 8
B, S, D = 16, 2048, 1024
T = B * S
TPC = T // NC
BPC = B // NC
E, I = 8, 2048
CAP = 4096
NT = TPC // 128
EPS = 1e-5
KB = 1024

_CACHE = {}


class Sem:
    def __init__(self, nc, name):
        self.h = nc.alloc_semaphore(name)
        self.n = 0

    def inc(self, inst, k=1):
        inst.then_inc(self.h, k)
        self.n += k
        return self.n


def build(upto=99):
    import os as _os
    S3T = int(_os.environ.get("K_S3T", NT))
    nc = bass.Bass("TRN2", target_bir_lowering=False, num_devices=NC)

    def din(name, shape, dt):
        return nc.dram_tensor(name, shape, dt, kind="ExternalInput").ap()

    xs_d = din("xs", [TPC, D], F32)
    wvT_d = din("wvT", [D, D], F32)
    woT_d = din("woT", [D, D], F32)
    embT_d = din("embT", [D, E], F32)
    w1T_d = din("w1T", [D, I], F32)
    w2T_d = din("w2T", [I, D], F32)
    iden_d = din("iden", [128, 128], F32)
    ones_d = din("onescol", [128, 1], F32)
    onesrow_d = din("onesrow", [1, 128], F32)
    triu16_d = din("triu16", [16, 16], F32)
    triu32_d = din("triu32", [32, 32], F32)
    maskc_d = din("maskc32", [E, 32], F32)
    iota8_d = din("iota8", [128, 8], F32)
    iota8m_d = din("iota8m", [128, 8], F32)
    iotaE_d = din("iotaEB", [32, 8 * 128], F32)
    iotaG_d = din("iotaG", [32, 128], F32)
    idxid_d = din("idxid", [128, 256], I16)
    idx8_d = din("idx8", [128, 1], I16)
    iotasl_d = din("iotasl", [16, 256], F32)
    m1_16_d = din("m1c16", [16, 2048], I16)
    m1_32_d = din("m1c32", [32, 128], F32)
    z16_d = din("z16", [16, 256], I16)

    out_d = nc.dram_tensor("out", [TPC, D], F32, kind="ExternalOutput").ap()
    DBG = bool(_os.environ.get("K_DBG"))
    if DBG:
        o_clist = nc.dram_tensor("o_clist", [16, 256], I16, kind="ExternalOutput").ap()
        o_dots = nc.dram_tensor("o_dots", [8, TPC], F32, kind="ExternalOutput").ap()
        o_pack = nc.dram_tensor("o_pack", [96, 128], F32, kind="ExternalOutput").ap()
        o_gb = nc.dram_tensor("o_gb", [32, 2 * 128], F32, kind="ExternalOutput").ap()
        o_we = nc.dram_tensor("o_we", [128, 2 * NT], F32, kind="ExternalOutput").ap()
        o_con = nc.dram_tensor("o_con", [128, D], BF16, kind="ExternalOutput").ap()
        o_yrow = nc.dram_tensor("o_yrow", [8, D], BF16, kind="ExternalOutput").ap()
        o_sb = nc.dram_tensor("o_sb", [32, 2 * 128], F32, kind="ExternalOutput").ap()
        o_tot = nc.dram_tensor("o_tot", [32, 2 * 8], F32, kind="ExternalOutput").ap()
        o_craw = nc.dram_tensor("o_craw", [16, 2048], I16, kind="ExternalOutput").ap()
        o_ecand = nc.dram_tensor("o_ecand", [16, 2048], I16, kind="ExternalOutput").ap()
        o_nfs = nc.dram_tensor("o_nfs", [1, 8], U32, kind="ExternalOutput").ap()
        o_nff = nc.dram_tensor("o_nff", [1, 4], U32, kind="ExternalOutput").ap()
        o_grank = nc.dram_tensor("o_grank", [32, 8 * 128], F32, kind="ExternalOutput").ap()
        o_csum = nc.dram_tensor("o_csum", [32, 8 * 128], F32, kind="ExternalOutput").ap()
        o_egr = nc.dram_tensor("o_egr", [16, 2048], F32, kind="ExternalOutput").ap()
        o_x1 = nc.dram_tensor("o_x1", [128, D], F32, kind="ExternalOutput").ap()
        o_arow = nc.dram_tensor("o_arow", [1, BPC * D], F32, kind="ExternalOutput").ap()

    def dint(name, shape, dt, shared=False):
        return nc.dram_tensor(name, shape, dt, kind="Internal",
                              addr_space="Shared" if shared else "Local").ap()

    x1_hbm = dint("x1_hbm", [TPC, D], F32)
    yhi_hbm = dint("yhi_hbm", [TPC, D], BF16)
    ylo_hbm = dint("ylo_hbm", [TPC, D], BF16)
    r_bounce = dint("r_bounce", [BPC * D], F32)
    t_bounce = dint("t_bounce", [BPC * D], F32)
    cnt_in = dint("cnt_in", [1, E], F32)
    cnt_all = dint("cnt_all", [NC, E], F32, shared=True)
    cand_in = dint("cand_in", [E, TPC], I16)
    cand_all = dint("cand_all", [NC * E, TPC], I16, shared=True)
    cand_mine = dint("cand_mine", [NC * TPC], I16)
    y_all = dint("y_all", [T, D], BF16, shared=True)
    contrib_in = dint("contrib_in", [CAP, D], BF16)
    contrib_all = dint("contrib_all", [E * CAP, D], BF16, shared=True)
    g1_dram = dint("g1_dram", [TPC], I16)
    g2_dram = dint("g2_dram", [TPC], I16)

    A = nc.alloc_sbuf_tensor
    sb_iden = A("c_iden", [128, 128], F32)
    sb_ones = A("c_ones", [128, 1], F32)
    sb_onesrow = A("c_onesrow", [1, 128], F32)
    sb_triu16 = A("c_triu16", [16, 16], F32)
    sb_triu32 = A("c_triu32", [32, 32], F32)
    sb_maskc = A("c_maskc", [E, 32], F32)
    sb_iota8 = A("c_iota8", [128, 8], F32)
    sb_iota8m = A("c_iota8m", [128, 8], F32)
    sb_iotaE = A("c_iotaE", [32, 8, 128], F32)
    sb_iotaG = A("c_iotaG", [32, 128], F32)
    sb_idxid = A("c_idxid", [128, 256], I16)
    sb_idx8 = A("c_idx8", [128, 1], I16)
    sb_iotasl = A("c_iotasl", [16, 256], F32)
    sb_m116 = A("c_m116", [16, 2048], I16)
    sb_m132 = A("c_m132", [32, 128], F32)
    canI2 = A("r_canI2", [128, 32, 8], I16)
    g_i16 = A("r_gi16", [16, 2, 256], I16)
    sb_z16 = A("c_z16", [16, 256], I16)
    sb_embT = A("c_embT", [128, 8, E], F32)
    sb_ehi = A("c_ehi", [128, 8, E], BF16)
    sb_elo = A("c_elo", [128, 8, E], BF16)
    sb_elof = A("c_elof", [128, 8, E], F32)

    sc_y = A("s_y", [128, D], F32)
    sc_y2 = A("s_y2", [128, D], F32)
    sc_hi = A("s_hi", [128, D], BF16)
    sc_lo = A("s_lo", [128, D], BF16)
    sc_lo2 = A("s_lo2", [128, D], BF16)
    sc_lo3 = A("s_lo3", [128, D], BF16)
    sxC = A("s_sxC", [128, NT], F32)
    sx2C = A("s_sx2C", [128, NT], F32)
    mC = A("s_mC", [128, NT], F32)
    varC = A("s_varC", [128, NT], F32)
    rsC = A("s_rsC", [128, NT], F32)
    nbC = A("s_nbC", [128, NT], F32)
    r3a = A("s_r3a", [128, NT, 8], F32)
    r3b = A("s_r3b", [128, NT, 8], F32)
    mx1C = A("s_mx1C", [128, NT], F32)
    mx2C = A("s_mx2C", [128, NT], F32)
    r_row = A("s_rrow", [1, BPC * D], F32)
    t_sb = A("s_t", [BPC, D], F32)
    a_sb = A("s_a", [BPC, D], F32)
    a_row = A("s_arow", [1, BPC * D], F32)
    rfm = A("s_rfm", [128, 8, BPC], F32)
    tfm = A("s_tfm", [128, 8, BPC], F32)
    wv_tile = A("s_wv", [128, 2, D], F32)
    dots_tm = A("s_dotstm", [128, NT, 8], F32)
    mx8 = A("r_mx8", [128, 8], F32)
    m1r = A("r_m1r", [128, 8], F32)
    m2r = A("r_m2r", [128, 8], F32)
    m1t = A("r_m1t", [128, 8], F32)
    m2t = A("r_m2t", [128, 8], F32)
    cc8 = A("r_cc8", [128, 8], F32)
    packA = A("r_packA", [128, 96], F32)
    packB = A("r_packB", [96, 128], F32)
    ix2B0 = A("r_ix2B0", [32, 128], F32)
    d12B0 = A("r_d12B0", [32, 128], F32)
    totB = A("r_totB", [32, 8], F32)
    pfxT = A("r_pfxT", [32, 8], F32)
    cntsb = A("r_cnt", [1, 8], F32)
    cntall_sb = A("r_cntall", [8, 8], F32)
    sB1 = A("r_sB1", [32, 128], F32)
    sB2 = A("r_sB2", [32, 128], F32)
    w1B = A("r_w1B", [32, 128], F32)
    w2B = A("r_w2B", [32, 128], F32)
    weB = A("r_weB", [32, 2, 128], F32)
    gB = A("r_gB", [32, 2, 128], F32)
    we_tm = A("r_wetm", [128, 2, NT], F32)
    clist = A("e_clist", [16, 256], I16)
    spA = A("e_spA", [16, 512], I16)
    spB = A("e_spB", [16, 512], I16)
    spC = A("e_spC", [16, 512], I16)
    nf_s = A("e_nfs", [1, 8], U32)
    nf_sf = A("e_nfsf", [1, 1], F32)
    nf_sb = A("e_nf", [1, 1], U32)
    nf_f = A("e_nff", [1, 1], F32)
    nfb = A("e_nfb", [16, 1], F32)
    evalid = A("e_valid", [16, 256], U8)
    clrep = A("e_clrep", [128, 256], I16)
    g1il = A("e_g1il", [128, 256], I16)
    g2il = A("e_g2il", [128, 256], I16)
    rowg = A("e_rowg", [128, TPC], I16)
    sb_idg = A("s_idg", [128, 64], I16)

    ARENA = 120 * KB
    _guard = A("arena_guard", [128, ARENA], U8)
    abase = nc.lookup_mloc(_guard).addr

    def AT(name, shape, dt, off):
        n = int(np.prod(shape[1:])) * mybir.dt.size(dt)
        assert off + n <= ARENA, f"{name}: {off}+{n} > {ARENA}"
        return nc.alloc_sbuf_tensor_at(name, shape, dt, offset=abase + off)

    xt2 = AT("p_xt2", [128, 2, D], F32, 0)
    attn_bc = AT("p_attnbc", [128, BPC, D], F32, 8 * KB)
    yfm_hi = AT("p_yfmhi", [128, 2, 8, 512], BF16, 16 * KB)
    yfm_lo = AT("p_yfmlo", [128, 2, 8, 512], BF16, 32 * KB)
    dots_em = AT("p_dotsem", [8, TPC], F32, 64 * KB)
    m1B = AT("p_m1B", [32, 8, 128], F32, 80 * KB)
    m2B = AT("p_m2B", [32, 8, 128], F32, 84 * KB)
    onesB = AT("p_onesB", [32, 8, 128], F32, 88 * KB)
    csumB = AT("p_csumB", [32, 8, 128], F32, 92 * KB)
    grankB = AT("p_grankB", [32, 8, 128], F32, 96 * KB)
    survB = AT("p_survB", [32, 8, 128], F32, 100 * KB)
    scrB = AT("p_scrB", [32, 8, 128], F32, 104 * KB)
    candB = AT("p_candB", [32, 8, 128], F32, 108 * KB)
    w1t = AT("p_w1t", [128, 8, I], BF16, 0)
    w2t = AT("p_w2t", [128, 16, D], BF16, 32 * KB)
    craw = AT("p_craw", [16, 8, 256], I16, 64 * KB)
    ecand = AT("p_ecand", [16, 2048], I16, 68 * KB)
    emask = AT("p_emask", [16, 2048], F32, 72 * KB)
    eX1 = AT("p_eX1", [16, 2048], F32, 80 * KB)
    eX2 = AT("p_eX2", [16, 2048], F32, 88 * KB)
    yfm2 = AT("p_yfm2", [128, 2, 8, 512], BF16, 96 * KB)
    inter = AT("p_inter", [128, 16, 512], BF16, 64 * KB)
    ctile = AT("p_ctile", [128, 2, D], BF16, 80 * KB)
    c1b = AT("p_c1b", [128, 2, 4, D], BF16, 0)
    c2b = AT("p_c2b", [128, 2, 4, D], BF16, 16 * KB)
    cxg = AT("p_cxg", [128, 2, 4, D], F32, 32 * KB)
    o1g = AT("p_o1g", [128, 2, 4, D], F32, 64 * KB)
    o2g = AT("p_o2g", [128, 4, D], F32, 96 * KB)

    pb = [nc.alloc_psum_tensor(f"pb{i}", [128, 512], F32) for i in range(8)]

    s_x = Sem(nc, "s_x")
    s_v = Sem(nc, "s_v")
    s_s = Sem(nc, "s_s")
    s_t = Sem(nc, "s_t")
    s_g = Sem(nc, "s_g")
    s_st = Sem(nc, "s_st")
    s_o = Sem(nc, "s_o")
    s_cc = Sem(nc, "s_cc")

    V, SC, TE, GP, SY = nc.vector, nc.scalar, nc.tensor, nc.gpsimd, nc.sync

    def cumsum(out_ap, in_ap):
        return V.tensor_tensor_scan(out=out_ap, data0=in_ap, data1=in_ap,
                                    initial=0.0, op0=ALU.add, op1=ALU.bypass)

    for dst, src in [
        (sb_iden[:], iden_d), (sb_ones[:], ones_d), (sb_onesrow[:], onesrow_d),
        (sb_triu16[:], triu16_d), (sb_triu32[:], triu32_d), (sb_maskc[:], maskc_d),
        (sb_iota8[:], iota8_d), (sb_iota8m[:], iota8m_d), (sb_iotasl[:], iotasl_d),
        (sb_iotaE[:], iotaE_d.rearrange("a (b c) -> a b c", b=8)),
        (sb_iotaG[:], iotaG_d),
        (sb_idxid[:], idxid_d), (sb_idx8[:], idx8_d),
        (sb_m116[:], m1_16_d),
        (sb_m132[:], m1_32_d),
        (sb_z16[:], z16_d),
        (sb_embT[:], embT_d.rearrange("(dc p) e -> p dc e", p=128)),
    ]:
        s_x.inc(SY.dma_start(out=dst, in_=src), 16)
    X0 = s_x.n
    if _os.environ.get("K_AGEARLY"):
        GP.wait_ge(s_x.h, X0)
        s_cc.inc(GP.collective_compute(
            "AllGather", ALU.bypass, replica_groups=[list(range(NC))],
            ins=[cnt_in], outs=[cnt_all]))
    V.wait_ge(s_x.h, X0)
    V.tensor_copy(sb_ehi[:], sb_embT[:])
    V.drain()
    V.tensor_copy(sb_elof[:], sb_ehi[:])
    V.drain()
    V.tensor_tensor(sb_elo[:], sb_embT[:], sb_elof[:], ALU.subtract)
    v_emb = s_v.inc(V.drain())

    if upto < 1:
        SY.wait_ge(s_st.h, s_st.n)
        SY.wait_ge(s_x.h, s_x.n)
        SY.wait_ge(s_v.h, s_v.n)
        SY.wait_ge(s_g.h, s_g.n)
        SY.wait_ge(s_cc.h, s_cc.n)
        return nc
    passA_done = []
    for t in range(NT):
        par = t % 2
        ld = SY.dma_start(out=xt2[:, par, :], in_=xs_d[128 * t:128 * (t + 1), :])
        if t >= 2:
            ld.wait_op(s_v.h, passA_done[t - 2], "sem-ge")
        x_t = s_x.inc(ld, 16)
        xt = xt2[:, par, :]
        SC.wait_ge(s_x.h, x_t)
        sq_t = s_s.inc(SC.activation(out=sc_y2[:], in_=xt, func=AF.Square, accum_out=sx2C[:, t:t + 1]))
        V.wait_ge(s_s.h, sq_t)
        passA_done.append(s_v.inc(V.reduce_sum(sxC[:, t:t + 1], xt, axis=AX.X)))
    V.wait_ge(s_s.h, s_s.n)
    V.drain()
    V.tensor_scalar_mul(mC[:], sxC[:], 1.0 / D)
    V.drain()
    V.tensor_tensor(varC[:], mC[:], mC[:], ALU.mult)
    V.drain()
    V.scalar_tensor_tensor(varC[:], sx2C[:], 1.0 / D, varC[:], ALU.mult, ALU.subtract)
    V.drain()
    v_var = s_v.inc(V.tensor_scalar_add(varC[:], varC[:], EPS))
    SC.wait_ge(s_v.h, v_var)
    sqr = s_s.inc(SC.activation(out=rsC[:], in_=varC[:], func=AF.Sqrt, bias=0.0, scale=1.0))
    V.wait_ge(s_s.h, sqr)
    V.reciprocal(rsC[:], rsC[:])
    V.drain()
    V.tensor_tensor(nbC[:], mC[:], rsC[:], ALU.mult)
    V.drain()
    v_nb = s_v.inc(V.tensor_scalar_mul(nbC[:], nbC[:], -1.0))
    SC.wait_ge(s_v.h, v_nb)
    t_cons = []
    for t in range(NT):
        b = t // 16
        par = t % 2
        first = (t % 16 == 0)
        last = (t % 16 == 15)
        ld = SY.dma_start(out=xt2[:, par, :], in_=xs_d[128 * t:128 * (t + 1), :])
        if t >= 2:
            ld.wait_op(s_s.h, t_cons[t - 2][0], "sem-ge")
        else:
            ld.wait_op(s_v.h, passA_done[-1], "sem-ge")
        x_t = s_x.inc(ld, 16)
        xt = xt2[:, par, :]
        SC.wait_ge(s_x.h, x_t)
        if t >= 2:
            SC.wait_ge(s_t.h, t_cons[t - 2][1])
        yslot = sc_y[:] if par == 0 else sc_y2[:]
        y_t = s_s.inc(SC.activation(out=yslot, in_=xt, func=AF.Identity,
                                    bias=nbC[:, t:t + 1], scale=rsC[:, t:t + 1]))
        TE.wait_ge(s_s.h, y_t)
        TE.matmul(pb[2 * b][0:1, :], sb_ones[:], yslot[:, 0:512] if False else (sc_y[:, 0:512] if par == 0 else sc_y2[:, 0:512]), start=first, stop=last)
        t_r = s_t.inc(TE.matmul(pb[2 * b + 1][0:1, :], sb_ones[:], sc_y[:, 512:1024] if par == 0 else sc_y2[:, 512:1024], start=first, stop=last))
        t_cons.append((y_t, t_r))

    V.wait_ge(s_t.h, s_t.n)
    for b in range(BPC):
        V.tensor_copy(r_row[0:1, D * b:D * b + 512], pb[2 * b][0:1, :])
        V.tensor_copy(r_row[0:1, D * b + 512:D * (b + 1)], pb[2 * b + 1][0:1, :])
    v_r = s_v.inc(V.drain())

    def matvec_fm(src_rows, bounce, fm_tile, w_dram, ps_lo, ps_hi, prod_sem_val):
        SY.wait_ge(s_v.h, prod_sem_val)
        s_st.inc(SY.dma_start(out=bounce.rearrange("(o a) -> o a", o=1), in_=src_rows), 16)
        SY.wait_ge(s_st.h, s_st.n)
        for b_ in range(BPC):
            s_x.inc(SY.dma_start(out=fm_tile[:, :, b_],
                                 in_=bounce.rearrange("(b k p) -> b p k", p=128, k=8)[b_],
                                 allow_slow_non_contiguous=True), 16)
        t_base = s_t.n
        x_base = s_x.n
        for dc in range(8):
            ld = SY.dma_start(out=wv_tile[:, dc % 2, :], in_=w_dram[128 * dc:128 * (dc + 1), :])
            if dc >= 2:
                ld.wait_op(s_t.h, t_base + dc - 1, "sem-ge")
            s_x.inc(ld, 16)
        for dc in range(8):
            TE.wait_ge(s_x.h, x_base + 16 * (dc + 1))
            TE.matmul(ps_lo[0:BPC, :], fm_tile[:, dc, :], wv_tile[:, dc % 2, 0:512],
                      start=(dc == 0), stop=(dc == 7))
            s_t.inc(TE.matmul(ps_hi[0:BPC, :], fm_tile[:, dc, :], wv_tile[:, dc % 2, 512:1024],
                              start=(dc == 0), stop=(dc == 7)))

    matvec_fm(r_row[:], r_bounce, rfm, wvT_d, pb[4], pb[5], v_r)
    V.wait_ge(s_t.h, s_t.n)
    V.tensor_copy(t_sb[:, 0:512], pb[4][0:BPC, :])
    V.tensor_copy(t_sb[:, 512:1024], pb[5][0:BPC, :])
    v_t = s_v.inc(V.drain())
    matvec_fm(t_sb[:], t_bounce, tfm, woT_d, pb[6], pb[7], v_t)
    V.wait_ge(s_t.h, s_t.n)
    V.tensor_copy(a_sb[:, 0:512], pb[6][0:BPC, :])
    V.tensor_copy(a_sb[:, 512:1024], pb[7][0:BPC, :])
    v_a = s_v.inc(V.drain())
    SY.wait_ge(s_v.h, v_a)
    s_st.inc(SY.dma_start(out=t_bounce.rearrange("(b d) -> b d", b=BPC), in_=a_sb[:]), 16)
    SY.wait_ge(s_st.h, s_st.n)
    x_ar = s_x.inc(SY.dma_start(out=a_row[:], in_=t_bounce.rearrange("(o a) -> o a", o=1)), 16)
    TE.wait_ge(s_x.h, x_ar)
    for b in range(BPC):
        TE.matmul(pb[0][:], sb_onesrow[:], a_row[0:1, D * b:D * b + 512], start=True, stop=True)
        t_bc = s_t.inc(TE.matmul(pb[1][:], sb_onesrow[:], a_row[0:1, D * b + 512:D * b + 1024], start=True, stop=True))
        V.wait_ge(s_t.h, t_bc)
        V.tensor_copy(attn_bc[:, b, 0:512], pb[0][:])
        v_bc = s_v.inc(V.tensor_copy(attn_bc[:, b, 512:1024], pb[1][:]))
        TE.wait_ge(s_v.h, v_bc)
    v_abc = s_v.inc(V.drain())

    if upto < 3:
        SY.wait_ge(s_st.h, s_st.n)
        SY.wait_ge(s_x.h, s_x.n)
        SY.wait_ge(s_v.h, s_v.n)
        SY.wait_ge(s_g.h, s_g.n)
        SY.wait_ge(s_cc.h, s_cc.n)
        return nc
    V.wait_ge(s_v.h, v_abc)
    pA = []
    for t in range(S3T):
        par = t % 2
        b = t // 16
        ld = SY.dma_start(out=xt2[:, par, :], in_=xs_d[128 * t:128 * (t + 1), :])
        if t >= 2:
            ld.wait_op(s_st.h, pA[t - 2], "sem-ge")
        x_t = s_x.inc(ld, 16)
        xt = xt2[:, par, :]
        V.wait_ge(s_x.h, x_t)
        v_x1 = s_v.inc(V.tensor_tensor(xt, xt, attn_bc[:, b, :], ALU.add))
        SC.wait_ge(s_v.h, v_x1)
        sq_t = s_s.inc(SC.activation(out=sc_y2[:], in_=xt, func=AF.Square, accum_out=sx2C[:, t:t + 1]))
        V.wait_ge(s_s.h, sq_t)
        s_v.inc(V.reduce_sum(sxC[:, t:t + 1], xt, axis=AX.X))
        pA.append(s_st.inc(SY.dma_start(out=x1_hbm[128 * t:128 * (t + 1), :],
                                        in_=xt).wait_op(s_v.h, s_v.n, "sem-ge"), 16))
    V.wait_ge(s_s.h, s_s.n)
    V.drain()
    V.tensor_scalar_mul(mC[:], sxC[:], 1.0 / D)
    V.drain()
    V.tensor_tensor(varC[:], mC[:], mC[:], ALU.mult)
    V.drain()
    V.scalar_tensor_tensor(varC[:], sx2C[:], 1.0 / D, varC[:], ALU.mult, ALU.subtract)
    V.drain()
    v_var = s_v.inc(V.tensor_scalar_add(varC[:], varC[:], EPS))
    SC.wait_ge(s_v.h, v_var)
    sqr = s_s.inc(SC.activation(out=rsC[:], in_=varC[:], func=AF.Sqrt, bias=0.0, scale=1.0))
    V.wait_ge(s_s.h, sqr)
    V.reciprocal(rsC[:], rsC[:])
    V.drain()
    V.tensor_tensor(nbC[:], mC[:], rsC[:], ALU.mult)
    V.drain()
    v_nb = s_v.inc(V.tensor_scalar_mul(nbC[:], nbC[:], -1.0))
    SC.wait_ge(s_v.h, v_nb)
    tile_done = []
    for t in range(S3T):
        par = t % 2
        ld = SY.dma_start(out=xt2[:, par, :], in_=x1_hbm[128 * t:128 * (t + 1), :])
        ld.wait_op(s_st.h, max(pA[-1], tile_done[t - 2]) if t >= 2 else pA[-1], "sem-ge")
        x_t = s_x.inc(ld, 16)
        xt = xt2[:, par, :]
        SC.wait_ge(s_x.h, x_t)
        if t >= 2:
            SC.wait_ge(s_st.h, tile_done[t - 2])
        yslot = sc_y[:] if par == 0 else sc_y2[:]
        hslot = sc_hi[:, 0:512].bitcast(BF16) if False else (sc_hi[:] if par == 0 else sc_lo[:])
        SC.activation(out=yslot, in_=xt, func=AF.Identity,
                      bias=nbC[:, t:t + 1], scale=rsC[:, t:t + 1])
        hi_t = s_s.inc(SC.activation(out=hslot, in_=xt, func=AF.Identity,
                                     bias=nbC[:, t:t + 1], scale=rsC[:, t:t + 1]))
        V.wait_ge(s_s.h, hi_t)
        lslot = sc_lo2[:] if par == 0 else sc_lo3[:]
        v_lo = s_v.inc(V.tensor_tensor(lslot, yslot, hslot, ALU.subtract))
        s_st.inc(SY.dma_start(out=yhi_hbm[128 * t:128 * (t + 1), :], in_=hslot).wait_op(s_s.h, hi_t, "sem-ge"), 16)
        st_t = s_st.inc(SY.dma_start(out=ylo_hbm[128 * t:128 * (t + 1), :], in_=lslot).wait_op(s_v.h, v_lo, "sem-ge"), 16)
        tile_done.append(st_t)
    ST3 = s_st.n

    NOCC = bool(_os.environ.get("K_NOCC"))
    GP.load_library(library_config.mlp)

    def y_chunk(k):
        GP.wait_ge(s_st.h, tile_done[8 * k + 7])
        return s_cc.inc(GP.collective_compute(
            "AllGather", ALU.bypass, replica_groups=[list(range(NC))],
            ins=[yhi_hbm[1024 * k:1024 * (k + 1), :]],
            outs=[y_all[8192 * k:8192 * (k + 1), :]]))

    cc_y = 0
    if not NOCC:
        y_chunk(0)
        cc_y = y_chunk(1)

    if upto < 4:
        SY.wait_ge(s_st.h, s_st.n)
        SY.wait_ge(s_x.h, s_x.n)
        SY.wait_ge(s_v.h, s_v.n)
        SY.wait_ge(s_g.h, s_g.n)
        SY.wait_ge(s_cc.h, s_cc.n)
        return nc
    V.wait_ge(s_v.h, v_emb)
    dots_mm = []
    for g in range(0 if _os.environ.get("K_NODG") else 8):
        gp_ = g % 2
        GP.wait_ge(s_st.h, tile_done[4 * g + 3])
        if g >= 2:
            GP.wait_ge(s_t.h, dots_mm[g - 2])
        s_g.inc(GP.dma_gather(
            yfm_hi[:, gp_], yhi_hbm, sb_idxid[:, 32 * g:32 * (g + 1)], 512, 512, D,
            elem_step=D, transpose=True), 16)
        g_lo = s_g.inc(GP.dma_gather(
            yfm_lo[:, gp_], ylo_hbm, sb_idxid[:, 32 * g:32 * (g + 1)], 512, 512, D,
            elem_step=D, transpose=True), 16)
        TE.wait_ge(s_g.h, g_lo)
        ps = pb[2 + gp_]
        if g >= 2:
            TE.wait_ge(s_v.h, s_v.n)
        n_mm = 0
        t_d = None
        for (yt, et) in ((yfm_hi, sb_ehi), (yfm_hi, sb_elo), (yfm_lo, sb_ehi), (yfm_lo, sb_elo)):
            for dc in range(8):
                inst = TE.matmul(ps[0:8, :], et[:, dc, :], yt[:, gp_, dc, :],
                                 start=(n_mm == 0), stop=(n_mm == 31))
                n_mm += 1
                if n_mm == 32:
                    t_d = s_t.inc(inst)
        dots_mm.append(t_d)
        V.wait_ge(s_t.h, t_d)
        s_v.inc(V.tensor_copy(dots_em[:, 512 * g:512 * (g + 1)], ps[0:8, :]))

    if dots_mm:
        GP.wait_ge(s_t.h, dots_mm[-1])
    w_ffn = s_g.n
    if not _os.environ.get("K_NOW"):
        for icg in range(8):
            s_g.inc(GP.dma_start(out=w1t[:, icg, :],
                                 in_=w1T_d.rearrange("(dc p) i -> p dc i", p=128)[:, icg, :]), 16)
        for icg in range(16):
            s_g.inc(GP.dma_start(out=w2t[:, icg, :], in_=w2T_d[128 * icg:128 * (icg + 1), :]), 16)
        w_ffn = s_g.n

    if upto < 5:
        SY.wait_ge(s_st.h, s_st.n)
        SY.wait_ge(s_x.h, s_x.n)
        SY.wait_ge(s_v.h, s_v.n)
        SY.wait_ge(s_g.h, s_g.n)
        SY.wait_ge(s_cc.h, s_cc.n)
        return nc
    V.wait_ge(s_v.h, s_v.n)
    for t in range(NT):
        ps = pb[4 + t % 2]
        t_tr = s_t.inc(TE.transpose(ps[0:128, 0:8], dots_em[:, 128 * t:128 * (t + 1)], sb_iden[0:8, 0:8]))
        V.wait_ge(s_t.h, t_tr)
        v_tr = s_v.inc(V.tensor_copy(dots_tm[:, t, :], ps[0:128, 0:8]))
        TE.wait_ge(s_v.h, v_tr)
    dots3 = dots_tm[:]
    iota3 = sb_iota8[:].unsqueeze(1).to_broadcast([128, NT, 8])
    iotam3 = sb_iota8m[:].unsqueeze(1).to_broadcast([128, NT, 8])
    V.wait_ge(s_v.h, s_v.n)
    V.drain()
    V.reduce_max(mx1C[:], dots3, axis=AX.X)
    V.drain()
    V.tensor_tensor(r3a[:], dots3, mx1C[:].unsqueeze(2).to_broadcast([128, NT, 8]), ALU.is_equal)
    V.drain()
    V.tensor_tensor(r3b[:], r3a[:], iotam3, ALU.mult)
    V.drain()
    V.tensor_reduce(packA[:, 0:32], r3b[:], axis=AX.X, op=ALU.min)
    V.drain()
    V.tensor_scalar_add(packA[:, 0:32], packA[:, 0:32], 1024.0)
    V.drain()
    V.tensor_tensor(r3a[:], iota3, packA[:, 0:32].unsqueeze(2).to_broadcast([128, NT, 8]), ALU.is_equal)
    V.drain()
    V.scalar_tensor_tensor(r3b[:], r3a[:], -1e30, dots3, ALU.mult, ALU.add)
    V.drain()
    V.reduce_max(mx2C[:], r3b[:], axis=AX.X)
    V.drain()
    V.tensor_tensor(r3a[:], r3b[:], mx2C[:].unsqueeze(2).to_broadcast([128, NT, 8]), ALU.is_equal)
    V.drain()
    V.tensor_tensor(r3b[:], r3a[:], iotam3, ALU.mult)
    V.drain()
    V.tensor_reduce(packA[:, 32:64], r3b[:], axis=AX.X, op=ALU.min)
    V.drain()
    V.tensor_scalar_add(packA[:, 32:64], packA[:, 32:64], 1024.0)
    V.drain()
    V.tensor_tensor(packA[:, 64:96], mx1C[:], mx2C[:], ALU.subtract)
    V.drain()
    v_pack = s_v.inc(V.drain())
    TE.wait_ge(s_v.h, v_pack)
    t_pk = s_t.inc(TE.transpose(pb[6][0:96, 0:128], packA[:, 0:96], sb_iden[:]))
    V.wait_ge(s_t.h, t_pk)
    v_pk = s_v.inc(V.tensor_copy(packB[:], pb[6][0:96, 0:128]))
    SY.wait_ge(s_v.h, v_pk)
    s_x.inc(SY.dma_start(out=ix2B0[:], in_=packB[32:64, :]), 16)
    s_x.inc(SY.dma_start(out=d12B0[:], in_=packB[64:96, :]), 16)
    x_pk = s_x.n
    ix1B = packB[0:32, :]
    ix2B = ix2B0[:]
    d12B = d12B0[:]
    SC.wait_ge(s_x.h, x_pk)
    V.wait_ge(s_x.h, x_pk)
    SC.wait_ge(s_v.h, v_pk)
    SC.activation(out=w1B[:], in_=d12B, func=AF.Sigmoid, bias=0.0, scale=1.0)
    s_s.inc(SC.activation(out=w2B[:], in_=d12B, func=AF.Sigmoid, bias=0.0, scale=-1.0))
    V.wait_ge(s_v.h, v_pk)
    V.tensor_tensor(m1B[:], ix1B.unsqueeze(1).to_broadcast([32, 8, 128]), sb_iotaE[:], ALU.is_equal)
    V.tensor_tensor(m2B[:], ix2B.unsqueeze(1).to_broadcast([32, 8, 128]), sb_iotaE[:], ALU.is_equal)
    V.drain()
    V.tensor_tensor(onesB[:], m1B[:], m2B[:], ALU.add)
    V.drain()
    for e in range(E):
        cumsum(csumB[:, e, :], onesB[:, e, :])
        V.drain()
    V.tensor_copy(totB[:], csumB[:, :, 127])
    v_tot = s_v.inc(V.drain())
    TE.wait_ge(s_v.h, v_tot)
    t_cnt = s_t.inc(TE.matmul(pb[7][0:1, 0:8], sb_ones[0:32, :], totB[:], start=True, stop=True))
    V.wait_ge(s_t.h, t_cnt)
    v_cnt = s_v.inc(V.tensor_copy(cntsb[:], pb[7][0:1, 0:8]))
    SY.wait_ge(s_v.h, v_cnt)
    s_st.inc(SY.dma_start(out=cnt_in, in_=cntsb[:]), 16)
    GP.wait_ge(s_st.h, s_st.n)
    cc_cnt = s_cc.inc(GP.collective_compute(
        "AllGather", ALU.bypass, replica_groups=[list(range(NC))],
        ins=[cnt_in], outs=[cnt_all]))
    V.tensor_scalar(survB[:], onesB[:], 0.5, None, ALU.is_gt)
    v_su = s_v.inc(V.drain())
    sU8 = AT("p_sU8", [32, 8, 128], U8, 112 * KB)
    V.tensor_copy(sU8[:], survB[:])
    V.drain()
    V.select(candB[:], sU8[:], sb_iotaG[:].unsqueeze(1).to_broadcast([32, 8, 128]),
             sb_m132[:].unsqueeze(1).to_broadcast([32, 8, 128]))
    V.drain()
    cand4 = survB[:].rearrange("p a (b c) -> p b a c", b=8, c=16)
    v_cand = s_v.inc(V.tensor_copy(cand4, candB[:]))
    TE.wait_ge(s_v.h, v_cand)
    t_ct = None
    for fc in range(8):
        t_ct = s_t.inc(TE.transpose(
            pb[0][0:128, 32 * fc:32 * (fc + 1)],
            survB[:, fc, :], sb_iden[0:32, 0:32]))
    V.wait_ge(s_t.h, t_ct)
    v_ci = s_v.inc(V.tensor_copy(canI2[:].rearrange("p a b -> p (a b)"),
                                 pb[0][0:128, 0:256].rearrange("p (a b) -> p b a", a=8)))
    SY.wait_ge(s_v.h, v_ci)
    s_st.inc(SY.dma_start(out=cand_in.rearrange("e (q c) -> (e q) c", q=16),
                          in_=canI2[:].rearrange("p a b -> p (a b)")), 16)
    GP.wait_ge(s_st.h, s_st.n)
    cc_cand = s_cc.inc(GP.collective_compute(
        "AllGather", ALU.bypass, replica_groups=[list(range(NC))],
        ins=[cand_in], outs=[cand_all]))
    cc_bar1 = s_cc.inc(GP.collective_compute(
        "AllGather", ALU.bypass, replica_groups=[list(range(NC))],
        ins=[cnt_in], outs=[cnt_all]))
    if not NOCC:
        y_chunk(2)
        cc_y = y_chunk(3)
    cc_bar2 = s_cc.inc(GP.collective_compute(
        "AllGather", ALU.bypass, replica_groups=[list(range(NC))],
        ins=[cnt_in], outs=[cnt_all]))
    SY.wait_ge(s_cc.h, cc_cand)
    x_cnt = s_x.inc(SY.dma_start(out=cntall_sb[:], in_=cnt_all), 16)
    TE.wait_ge(s_x.h, x_cnt)
    TE.wait_ge(s_v.h, v_cnt)
    TE.matmul(pb[7][0:32, 0:8], sb_triu32[:], totB[:], start=True, stop=False)
    t_pfx = s_t.inc(TE.matmul(pb[7][0:32, 0:8], sb_maskc[:], cntall_sb[:], start=False, stop=True))
    V.wait_ge(s_t.h, t_pfx)
    v_pfx = s_v.inc(V.tensor_copy(pfxT[:], pb[7][0:32, 0:8]))
    V.wait_ge(s_v.h, v_pfx)
    V.tensor_tensor(grankB[:], csumB[:], pfxT[:].unsqueeze(2).to_broadcast([32, 8, 128]), ALU.add)
    v_gr_tap = s_v.inc(V.drain())
    if DBG:
        s_o.inc(SY.dma_start(out=o_grank, in_=grankB[:].rearrange("a b c -> a (b c)")).wait_op(s_v.h, v_gr_tap, "sem-ge"), 16)
        s_o.inc(SY.dma_start(out=o_csum, in_=csumB[:].rearrange("a b c -> a (b c)")).wait_op(s_v.h, v_gr_tap, "sem-ge"), 16)
    V.tensor_scalar(survB[:], grankB[:], float(CAP), None, ALU.is_le)
    V.drain()
    V.tensor_tensor(survB[:], survB[:], onesB[:], ALU.mult)
    V.drain()
    def sum_e(dst, msk, val):
        V.tensor_tensor(scrB[:], msk, val, ALU.mult)
        V.drain()
        V.tensor_tensor(dst, scrB[:, 0, :], scrB[:, 1, :], ALU.add)
        for e_ in range(2, E):
            V.drain()
            V.tensor_tensor(dst, dst, scrB[:, e_, :], ALU.add)
        V.drain()

    sum_e(sB1[:], m1B[:], grankB[:])
    sum_e(sB2[:], m2B[:], grankB[:])
    V.tensor_scalar(sB1[:], sB1[:], -1.0, None, ALU.add)
    V.tensor_scalar(sB2[:], sB2[:], -1.0, None, ALU.add)
    V.drain()
    V.tensor_scalar(sB1[:], sB1[:], float(CAP - 1), None, ALU.min)
    V.tensor_scalar(sB2[:], sB2[:], float(CAP - 1), None, ALU.min)
    V.drain()
    for k, (sBk, ixk) in enumerate(((sB1, ix1B), (sB2, ix2B))):
        V.tensor_scalar(scrB[:, 0, :], sBk[:], 1024.0 - 0.5, None, ALU.is_ge)
        V.tensor_scalar(scrB[:, 1, :], sBk[:], 2048.0 - 0.5, None, ALU.is_ge)
        V.tensor_scalar(scrB[:, 2, :], sBk[:], 3072.0 - 0.5, None, ALU.is_ge)
        V.drain()
        V.tensor_tensor(scrB[:, 0, :], scrB[:, 0, :], scrB[:, 1, :], ALU.add)
        V.drain()
        V.tensor_tensor(scrB[:, 0, :], scrB[:, 0, :], scrB[:, 2, :], ALU.add)
        V.drain()
        V.scalar_tensor_tensor(scrB[:, 2, :], scrB[:, 0, :], 7168.0, sBk[:], ALU.mult, ALU.add)
        V.drain()
        V.scalar_tensor_tensor(gB[:, k, :], ixk, 1024.0, scrB[:, 2, :], ALU.mult, ALU.add)
        V.drain()
    v_g = s_v.inc(V.drain())
    sum_e(sB1[:], m1B[:], survB[:])
    sum_e(sB2[:], m2B[:], survB[:])
    V.wait_ge(s_s.h, s_s.n)
    V.tensor_tensor(weB[:, 0, :], w1B[:], sB1[:], ALU.mult)
    V.tensor_tensor(weB[:, 1, :], w2B[:], sB2[:], ALU.mult)
    v_we = s_v.inc(V.drain())
    TE.wait_ge(s_v.h, v_we)
    for k in range(2):
        t_we = s_t.inc(TE.transpose(pb[6][0:128, 0:32], weB[:, k, :], sb_iden[0:32, 0:32]))
        V.wait_ge(s_t.h, t_we)
        v_wt = s_v.inc(V.tensor_copy(we_tm[:, k, :], pb[6][0:128, 0:32]))
        TE.wait_ge(s_v.h, v_wt)
    TE.wait_ge(s_v.h, v_g)
    t_gt = None
    for k in range(2):
        for fc in range(8):
            t_gt = s_t.inc(TE.transpose(
                pb[1][0:16, 256 * k + 32 * fc:256 * k + 32 * (fc + 1)],
                gB[:, k, 16 * fc:16 * (fc + 1)], sb_iden[0:32, 0:32]))
    V.wait_ge(s_t.h, t_gt)
    V.tensor_copy(g_i16[:, 0, :], pb[1][0:16, 0:256].rearrange("p (a b) -> p b a", a=8))
    v_gi = s_v.inc(V.tensor_copy(g_i16[:, 1, :], pb[1][0:16, 256:512].rearrange("p (a b) -> p b a", a=8)))
    SY.wait_ge(s_v.h, v_gi)
    for k in range(8):
        s_x.inc(SY.dma_start(out=g1il[16 * k:16 * (k + 1), :], in_=g_i16[:, 0, :]), 16)
        s_x.inc(SY.dma_start(out=g2il[16 * k:16 * (k + 1), :], in_=g_i16[:, 1, :]), 16)
    x_gil = s_x.n

    if upto < 6:
        SY.wait_ge(s_st.h, s_st.n)
        SY.wait_ge(s_x.h, s_x.n)
        SY.wait_ge(s_v.h, s_v.n)
        SY.wait_ge(s_g.h, s_g.n)
        SY.wait_ge(s_cc.h, s_cc.n)
        return nc
    GP.wait_ge(s_cc.h, cc_bar1)
    g_rows = s_g.inc(GP.dma_gather(
        rowg[:].rearrange("p (c n) -> p c n", c=1), cand_all, sb_idx8[:], 8, 8, TPC,
        elem_step=TPC, transpose=False), 16)
    SY.wait_ge(s_g.h, g_rows)
    st_cm = s_st.inc(SY.dma_start(out=cand_mine.rearrange("(r n) -> r n", r=8), in_=rowg[0:8, :]), 16)
    if _os.environ.get("K_E6") == "1":
        SY.wait_ge(s_st.h, s_st.n)
        SY.wait_ge(s_x.h, s_x.n)
        SY.wait_ge(s_v.h, s_v.n)
        SY.wait_ge(s_g.h, s_g.n)
        SY.wait_ge(s_cc.h, s_cc.n)
        return nc
    SY.wait_ge(s_st.h, st_cm)
    x_cr = s_x.inc(SY.dma_start(out=craw[:],
                                in_=cand_mine.rearrange("(blk q c) -> q blk c", q=16, c=256)
                                ).wait_op(s_v.h, v_we, "sem-ge"), 16)
    if _os.environ.get("K_E6") == "2":
        SY.wait_ge(s_st.h, s_st.n)
        SY.wait_ge(s_x.h, s_x.n)
        SY.wait_ge(s_v.h, s_v.n)
        SY.wait_ge(s_g.h, s_g.n)
        SY.wait_ge(s_cc.h, s_cc.n)
        return nc
    V.wait_ge(s_x.h, x_cr)
    V.wait_ge(s_v.h, v_we)
    crawf = craw[:].rearrange("p a b -> p (a b)")
    if DBG:
        s_o.inc(SY.dma_start(out=o_craw, in_=crawf).wait_op(s_x.h, x_cr, "sem-ge"), 16)
    V.tensor_scalar(emask[:], crawf, 0.0, None, ALU.is_ge)
    v_em2 = s_v.inc(V.drain())
    TE.wait_ge(s_v.h, v_em2)
    for q in range(4):
        s_t.inc(TE.matmul(pb[q][0:1, :], sb_ones[0:16, :], emask[:, 512 * q:512 * (q + 1)],
                          start=True, stop=True))
    V.wait_ge(s_t.h, s_t.n)
    for q in range(4):
        V.tensor_copy(eX1[0:1, 512 * q:512 * (q + 1)], pb[q][0:1, :])
    V.drain()
    cumsum(eX2[0:1, :], eX1[0:1, :])
    V.drain()
    v_col = s_v.inc(V.tensor_tensor(eX2[0:1, :], eX2[0:1, :], eX1[0:1, :], ALU.subtract))
    TE.wait_ge(s_v.h, v_col)
    for q in range(4):
        TE.matmul(pb[4 + (q % 2)][0:16, :], sb_triu16[:], emask[:, 512 * q:512 * (q + 1)],
                  start=True, stop=False)
        t_p = s_t.inc(TE.matmul(pb[4 + (q % 2)][0:16, :], sb_onesrow[0:1, 0:16],
                                eX2[0:1, 512 * q:512 * (q + 1)], start=False, stop=True))
        V.wait_ge(s_t.h, t_p)
        v_ep = s_v.inc(V.tensor_copy(eX1[:, 512 * q:512 * (q + 1)], pb[4 + (q % 2)][0:16, :]))
        TE.wait_ge(s_v.h, v_ep)
    V.tensor_tensor(eX1[:], eX1[:], emask[:], ALU.add)
    V.drain()
    V.tensor_scalar(eX2[:], eX1[:], float(CAP), None, ALU.is_le)
    V.drain()
    V.tensor_tensor(eX2[:], eX2[:], emask[:], ALU.mult)
    V.drain()
    eU8 = eX1[:].bitcast(U8)[:, 0:2048]
    V.tensor_copy(eU8, eX2[:])
    V.drain()
    v_ec = s_v.inc(V.select(ecand[:], eU8, crawf, sb_m116[:]))
    if DBG:
        s_o.inc(SY.dma_start(out=o_ecand, in_=ecand[:]).wait_op(s_v.h, v_ec, "sem-ge"), 16)
        s_o.inc(SY.dma_start(out=o_egr, in_=eX2[:]).wait_op(s_v.h, v_ec, "sem-ge"), 16)
    if _os.environ.get("K_E6") == "3":
        SY.wait_ge(s_st.h, s_st.n)
        SY.wait_ge(s_x.h, s_x.n)
        SY.wait_ge(s_v.h, s_v.n)
        SY.wait_ge(s_g.h, s_g.n)
        SY.wait_ge(s_cc.h, s_cc.n)
        return nc
    GP.load_library(library_config.sparse_gather)
    GP.wait_ge(s_v.h, v_ec)

    def spg(in_ap, out_ap, nf_cell, fill_ap):
        g_ = s_g.inc(GP.sparse_gather(out=out_ap, in_=in_ap, num_found=nf_cell))
        V.wait_ge(s_g.h, g_)
        v1 = s_v.inc(V.tensor_copy(nf_sf[:], nf_cell))
        TE.wait_ge(s_v.h, v1)
        t1 = s_t.inc(TE.matmul(pb[6][0:16, 0:1], sb_onesrow[0:1, 0:16], nf_sf[:], start=True, stop=True))
        V.wait_ge(s_t.h, t1)
        V.tensor_copy(nfb[:], pb[6][0:16, 0:1])
        V.drain()
        V.tensor_tensor(evalid[:], sb_iotasl[:], nfb[:].to_broadcast([16, 256]), ALU.is_ge)
        V.drain()
        v2 = s_v.inc(V.copy_predicated(out_ap, evalid[:], fill_ap))
        GP.wait_ge(s_v.h, v2)
        return v2

    for q in range(4):
        spg(ecand[:, 512 * q:512 * (q + 1)],
            (spA if q < 2 else spB)[:, 256 * (q % 2):256 * (q % 2 + 1)],
            nf_s[0:1, q:q + 1], sb_m116[:, 0:256])
    spg(spA[:], spC[:, 0:256], nf_s[0:1, 4:5], sb_m116[:, 0:256])
    spg(spB[:], spC[:, 256:512], nf_s[0:1, 5:6], sb_m116[:, 0:256])
    v_cl0 = spg(spC[:], clist[:], nf_sb[:], sb_z16[:])
    GP.load_library(library_config.mlp)
    clf = AT("e_clf", [16, 256], F32, 72 * KB)
    cl_i = AT("e_cli", [16, 256], F32, 73 * KB)
    cl_t = AT("e_clt", [16, 256], F32, 74 * KB)
    cl_k = AT("e_clk", [16, 256], F32, 75 * KB)
    cl_s = AT("e_cls", [16, 256], F32, 76 * KB)
    V.drain()
    V.tensor_copy(clf[:], clist[:])
    V.drain()
    V.tensor_scalar(cl_i[:], clf[:], 4095.5, None, ALU.is_ge)
    V.drain()
    for j in range(2, 8):
        V.tensor_scalar(cl_t[:], clf[:], 4096.0 * j - 0.5, None, ALU.is_ge)
        V.drain()
        V.tensor_tensor(cl_i[:], cl_i[:], cl_t[:], ALU.add)
        V.drain()
    V.scalar_tensor_tensor(cl_t[:], cl_i[:], -4096.0, clf[:], ALU.mult, ALU.add)
    V.drain()
    V.tensor_scalar(cl_k[:], cl_t[:], 1023.5, None, ALU.is_ge)
    V.drain()
    for thr in (2047.5, 3071.5):
        V.tensor_scalar(cl_s[:], cl_t[:], thr, None, ALU.is_ge)
        V.drain()
        V.tensor_tensor(cl_k[:], cl_k[:], cl_s[:], ALU.add)
        V.drain()
    V.scalar_tensor_tensor(cl_s[:], cl_i[:], -3072.0, clf[:], ALU.mult, ALU.add)
    V.drain()
    V.scalar_tensor_tensor(clf[:], cl_k[:], 7168.0, cl_s[:], ALU.mult, ALU.add)
    V.drain()
    v_cl0 = s_v.inc(V.tensor_copy(clist[:], clf[:]))
    SY.wait_ge(s_v.h, v_cl0)
    for k in range(8):
        s_x.inc(SY.dma_start(out=clrep[16 * k:16 * (k + 1), :], in_=clist[:]), 16)
    x_clr = s_x.n

    if _os.environ.get("K_E6") == "5":
        SY.wait_ge(s_st.h, s_st.n)
        SY.wait_ge(s_x.h, s_x.n)
        SY.wait_ge(s_v.h, s_v.n)
        SY.wait_ge(s_g.h, s_g.n)
        SY.wait_ge(s_cc.h, s_cc.n)
        return nc
    if DBG:
        GP.wait_ge(s_o.h, s_o.n)
    GP.wait_ge(s_cc.h, cc_bar2)
    GP.wait_ge(s_x.h, x_clr)
    TE.wait_ge(s_v.h, v_gi)
    relu_done = {}
    st_ct = []
    gath_y = []
    mm1_last = {}
    for tg in range(8):
        gpar = tg % 2
        if tg >= 2:
            GP.wait_ge(s_t.h, mm1_last[tg - 2])
        g_y = s_g.inc(GP.dma_gather(
            yfm2[:, gpar], y_all, clrep[:, 32 * tg:32 * (tg + 1)], 512, 512, D,
            elem_step=D, transpose=True), 16)
        gath_y.append(g_y)
        TE.wait_ge(s_g.h, max(g_y, w_ffn))
        for ic in range(16):
            ps = pb[ic % 2]
            if (tg, ic - 2) in relu_done:
                TE.wait_ge(s_s.h, relu_done[(tg, ic - 2)])
            elif tg > 0 and ic < 2:
                TE.wait_ge(s_s.h, relu_done[(tg - 1, 14 + ic)])
            t_i = None
            for dc in range(8):
                inst = TE.matmul(ps[:], w1t[:, dc, 128 * ic:128 * (ic + 1)], yfm2[:, gpar, dc, :],
                                 start=(dc == 0), stop=(dc == 7))
                if dc == 7:
                    t_i = s_t.inc(inst)
            if ic == 15:
                mm1_last[tg] = t_i
            SC.wait_ge(s_t.h, t_i)
            relu_done[(tg, ic)] = s_s.inc(
                SC.activation(out=inter[:, ic, :], in_=ps[:], func=AF.Relu, bias=0.0, scale=1.0))
        TE.wait_ge(s_s.h, relu_done[(tg, 15)])
        for sub in range(4):
            cpar = sub % 2
            for hh in range(2):
                ps = pb[4 + 2 * cpar + hh]
                t_o = None
                for ic in range(16):
                    inst = TE.matmul(ps[:], inter[:, ic, 128 * sub:128 * (sub + 1)],
                                     w2t[:, ic, 512 * hh:512 * (hh + 1)],
                                     start=(ic == 0), stop=(ic == 15))
                    if ic == 15:
                        t_o = s_t.inc(inst)
                V.wait_ge(s_t.h, t_o)
                if len(st_ct) >= 2:
                    V.wait_ge(s_st.h, st_ct[-2])
                v_o = s_v.inc(V.tensor_copy(ctile[:, cpar, 512 * hh:512 * (hh + 1)], ps[:]))
                TE.wait_ge(s_v.h, v_o)
            row0 = 512 * tg + 128 * sub
            st_ct.append(s_st.inc(SY.dma_start(
                out=contrib_in[row0:row0 + 128, :],
                in_=ctile[:, cpar, :]).wait_op(s_v.h, s_v.n, "sem-ge"), 16))
    cc_con = None
    for k in range(4):
        GP.wait_ge(s_st.h, st_ct[4 * (2 * k + 1) + 3])
        cc_con = s_cc.inc(GP.collective_compute(
            "AllGather", ALU.bypass, replica_groups=[list(range(NC))],
            ins=[contrib_in[1024 * k:1024 * (k + 1), :]],
            outs=[contrib_all[8192 * k:8192 * (k + 1), :]]))
    cc_con = s_cc.inc(GP.collective_compute(
        "AllGather", ALU.bypass, replica_groups=[list(range(NC))],
        ins=[cnt_in], outs=[cnt_all]))

    if upto < 7:
        SY.wait_ge(s_st.h, s_st.n)
        SY.wait_ge(s_x.h, s_x.n)
        SY.wait_ge(s_v.h, s_v.n)
        SY.wait_ge(s_g.h, s_g.n)
        SY.wait_ge(s_cc.h, s_cc.n)
        return nc
    GP.wait_ge(s_cc.h, cc_con)
    GP.wait_ge(s_x.h, x_gil)
    fin = []
    o_base = s_o.n
    for g in range(8):
        gpar = g % 2
        if g >= 2:
            GP.wait_ge(s_v.h, fin[g - 2])
        g_c1 = s_g.inc(GP.dma_gather(
            c1b[:, gpar], contrib_all, g1il[:, 32 * g:32 * (g + 1)], 512, 512, D,
            elem_step=D, transpose=False), 16)
        g_c2 = s_g.inc(GP.dma_gather(
            c2b[:, gpar], contrib_all, g2il[:, 32 * g:32 * (g + 1)], 512, 512, D,
            elem_step=D, transpose=False), 16)
        ld = GP.dma_start(out=cxg[:, gpar], in_=x1_hbm[512 * g:512 * (g + 1), :].rearrange("(a p) d -> p a d", p=128))
        if g >= 2:
            ld.wait_op(s_v.h, fin[g - 2], "sem-ge")
        x_t = s_x.inc(ld, 16)
        we1g = we_tm[:, 0, 4 * g:4 * (g + 1)].unsqueeze(2).to_broadcast([128, 4, D])
        we2g = we_tm[:, 1, 4 * g:4 * (g + 1)].unsqueeze(2).to_broadcast([128, 4, D])
        V.wait_ge(s_g.h, g_c2)
        V.wait_ge(s_x.h, x_t)
        if g >= 2:
            V.wait_ge(s_o.h, o_base + 16 * (g - 1))
        V.tensor_tensor(o1g[:, gpar], c1b[:, gpar], we1g, ALU.mult)
        V.tensor_tensor(o2g[:], c2b[:, gpar], we2g, ALU.mult)
        V.drain()
        V.tensor_tensor(o1g[:, gpar], o1g[:, gpar], cxg[:, gpar], ALU.add)
        V.drain()
        v_f = s_v.inc(V.tensor_tensor(o1g[:, gpar], o1g[:, gpar], o2g[:], ALU.add))
        fin.append(v_f)
        s_o.inc(SY.dma_start(out=out_d[512 * g:512 * (g + 1), :].rearrange("(a p) d -> p a d", p=128),
                             in_=o1g[:, gpar]).wait_op(s_v.h, v_f, "sem-ge"), 16)
    if DBG:
        SY.wait_ge(s_v.h, s_v.n)
        SY.wait_ge(s_g.h, s_g.n)
        s_o.inc(SY.dma_start(out=o_clist, in_=clist[:]), 16)
        s_o.inc(SY.dma_start(out=o_dots, in_=dots_em[:]), 16)
        s_o.inc(SY.dma_start(out=o_pack, in_=packB[:]), 16)
        s_o.inc(SY.dma_start(out=o_gb, in_=gB[:].rearrange("a b c -> a (b c)")), 16)
        s_o.inc(SY.dma_start(out=o_we, in_=we_tm[:].rearrange("a b c -> a (b c)")), 16)
        s_o.inc(SY.dma_start(out=o_con, in_=contrib_in[0:128, :]), 16)
        s_o.inc(SY.dma_start(out=o_yrow, in_=y_all[0:8, :]), 16)
        s_o.inc(SY.dma_start(out=o_sb[:, 0:128], in_=sB1[:]), 16)
        s_o.inc(SY.dma_start(out=o_sb[:, 128:256], in_=sB2[:]), 16)
        s_o.inc(SY.dma_start(out=o_tot[:, 0:8], in_=totB[:]), 16)
        s_o.inc(SY.dma_start(out=o_tot[:, 8:16], in_=pfxT[:]), 16)
        s_o.inc(SY.dma_start(out=o_x1, in_=x1_hbm[2048:2176, :]), 16)
        s_o.inc(SY.dma_start(out=o_arow, in_=a_row[:]), 16)
        s_o.inc(SY.dma_start(out=o_nfs, in_=nf_s[:]), 16)
        s_o.inc(SY.dma_start(out=o_nff[0:1, 0:1], in_=nf_sb[:]), 16)
    SY.wait_ge(s_o.h, s_o.n)
    return nc


def _host_consts():
    c = {}
    c["iden"] = np.eye(128, dtype=np.float32)
    c["onescol"] = np.ones((128, 1), np.float32)
    c["onesrow"] = np.ones((1, 128), np.float32)
    c["triu16"] = np.triu(np.ones((16, 16), np.float32), 1)
    c["triu32"] = np.triu(np.ones((32, 32), np.float32), 1)
    c["iota8"] = np.tile(np.arange(8, dtype=np.float32), (128, 1))
    c["iota8m"] = np.tile(np.arange(8, dtype=np.float32) - 1024.0, (128, 1))
    c["iotaEB"] = np.ascontiguousarray(
        np.tile(np.arange(8, dtype=np.float32)[None, :, None], (32, 1, 128)).reshape(32, -1))
    idxid = np.zeros((16, 256), np.int16)
    for i_ in range(4096):
        idxid[i_ % 16, i_ // 16] = i_
    c["idxid"] = np.tile(idxid, (8, 1))
    iotasl = np.zeros((16, 256), np.float32)
    for j in range(4096):
        iotasl[j % 16, j // 16] = j
    c["iotasl"] = iotasl
    c["m1c16"] = np.full((16, 2048), -1, np.int16)
    c["m1c32"] = np.full((32, 128), -1, np.float32)
    c["z16"] = np.zeros((16, 256), np.int16)
    return c


def kernel(L0_attn_x, L0_attn_w_qkv, L0_attn_w_o, L0_ffn_expert_embeddings,
           L0_ffn_first_linear, L0_ffn_second_linear):
    x = np.ascontiguousarray(np.asarray(L0_attn_x, dtype=np.float32))
    wqkv = np.asarray(L0_attn_w_qkv, dtype=np.float32)
    wo = np.asarray(L0_attn_w_o, dtype=np.float32)
    emb = np.asarray(L0_ffn_expert_embeddings, dtype=np.float32)
    w1 = np.asarray(L0_ffn_first_linear, dtype=np.float32)
    w2 = np.asarray(L0_ffn_second_linear, dtype=np.float32)

    wvT = np.ascontiguousarray(wqkv[2 * D:3 * D, :].T)
    woT = np.ascontiguousarray(wo.T)
    embT = np.ascontiguousarray(emb.T)
    consts = _host_consts()

    import os as _os
    _upto = int(_os.environ.get("K_UPTO", 99))
    if _CACHE.get("upto") != _upto:
        _CACHE.clear()
        _CACHE["upto"] = _upto
        _CACHE["nc"] = build(_upto)
        mybir.codegen_inst_isa_subclasses(_CACHE["nc"])
    nc = _CACHE["nc"]

    xr = x.reshape(T, D)
    in_maps = []
    for c in range(NC):
        iotaG = (4096 * c + 128 * np.arange(32)[:, None] +
                 np.arange(128)[None, :]).astype(np.float32)
        maskc = np.zeros((E, 32), np.float32)
        maskc[:c, :] = 1.0
        idx8 = np.full((128, 1), -1, np.int16)
        for k in range(8):
            for j in range(8):
                idx8[16 * k + j, 0] = 8 * j + c
        m = dict(
            xs=np.ascontiguousarray(xr[TPC * c:TPC * (c + 1)]),
            wvT=wvT, woT=woT, embT=embT,
            w1T=np.ascontiguousarray(w1[c].T), w2T=np.ascontiguousarray(w2[c].T),
            iotaG=np.ascontiguousarray(iotaG), maskc32=maskc, idx8=idx8,
            **consts,
        )
        in_maps.append(m)
    res = run_bass_kernel_spmd(nc, in_maps, core_ids=list(range(NC)))
    out = np.concatenate([np.asarray(r["out"]) for r in res.results], axis=0)
    return out.reshape(B, S, D)


# version 10 (final)
# speedup vs baseline: 1.3595x; 1.0057x over previous
import numpy as np
import concourse.bass as bass
import concourse.mybir as mybir
from concourse.bass_utils import run_bass_kernel_spmd
from concourse import library_config

F32 = mybir.dt.float32
BF16 = mybir.dt.bfloat16
I16 = mybir.dt.int16
U8 = mybir.dt.uint8
U32 = mybir.dt.uint32
AF = mybir.ActivationFunctionType
ALU = mybir.AluOpType
AX = mybir.AxisListType

NC = 8
B, S, D = 16, 2048, 1024
T = B * S
TPC = T // NC
BPC = B // NC
E, I = 8, 2048
CAP = 4096
NT = TPC // 128
EPS = 1e-5
KB = 1024

_CACHE = {}


class Sem:
    def __init__(self, nc, name):
        self.h = nc.alloc_semaphore(name)
        self.n = 0

    def inc(self, inst, k=1):
        inst.then_inc(self.h, k)
        self.n += k
        return self.n


def build(upto=99):
    import os as _os
    S3T = int(_os.environ.get("K_S3T", NT))
    nc = bass.Bass("TRN2", target_bir_lowering=False, num_devices=NC)

    def din(name, shape, dt):
        return nc.dram_tensor(name, shape, dt, kind="ExternalInput").ap()

    xs_d = din("xs", [TPC, D], F32)
    wvT_d = din("wvT", [D, D], F32)
    woT_d = din("woT", [D, D], F32)
    embT_d = din("embT", [D, E], F32)
    w1T_d = din("w1T", [D, I], F32)
    w2T_d = din("w2T", [I, D], F32)
    iden_d = din("iden", [128, 128], F32)
    ones_d = din("onescol", [128, 1], F32)
    onesrow_d = din("onesrow", [1, 128], F32)
    triu16_d = din("triu16", [16, 16], F32)
    triu32_d = din("triu32", [32, 32], F32)
    maskc_d = din("maskc32", [E, 32], F32)
    iota8_d = din("iota8", [128, 8], F32)
    iota8m_d = din("iota8m", [128, 8], F32)
    iotaE_d = din("iotaEB", [32, 8 * 128], F32)
    iotaG_d = din("iotaG", [32, 128], F32)
    idxid_d = din("idxid", [128, 256], I16)
    idx8_d = din("idx8", [128, 1], I16)
    iotasl_d = din("iotasl", [16, 256], F32)
    m1_16_d = din("m1c16", [16, 2048], I16)
    m1_32_d = din("m1c32", [32, 128], F32)
    z16_d = din("z16", [16, 256], I16)

    out_d = nc.dram_tensor("out", [TPC, D], F32, kind="ExternalOutput").ap()
    DBG = bool(_os.environ.get("K_DBG"))
    if DBG:
        o_clist = nc.dram_tensor("o_clist", [16, 256], I16, kind="ExternalOutput").ap()
        o_dots = nc.dram_tensor("o_dots", [8, TPC], F32, kind="ExternalOutput").ap()
        o_pack = nc.dram_tensor("o_pack", [96, 128], F32, kind="ExternalOutput").ap()
        o_gb = nc.dram_tensor("o_gb", [32, 2 * 128], F32, kind="ExternalOutput").ap()
        o_we = nc.dram_tensor("o_we", [128, 2 * NT], F32, kind="ExternalOutput").ap()
        o_con = nc.dram_tensor("o_con", [128, D], BF16, kind="ExternalOutput").ap()
        o_yrow = nc.dram_tensor("o_yrow", [8, D], BF16, kind="ExternalOutput").ap()
        o_sb = nc.dram_tensor("o_sb", [32, 2 * 128], F32, kind="ExternalOutput").ap()
        o_tot = nc.dram_tensor("o_tot", [32, 2 * 8], F32, kind="ExternalOutput").ap()
        o_craw = nc.dram_tensor("o_craw", [16, 2048], I16, kind="ExternalOutput").ap()
        o_ecand = nc.dram_tensor("o_ecand", [16, 2048], I16, kind="ExternalOutput").ap()
        o_nfs = nc.dram_tensor("o_nfs", [1, 8], U32, kind="ExternalOutput").ap()
        o_nff = nc.dram_tensor("o_nff", [1, 4], U32, kind="ExternalOutput").ap()
        o_grank = nc.dram_tensor("o_grank", [32, 8 * 128], F32, kind="ExternalOutput").ap()
        o_csum = nc.dram_tensor("o_csum", [32, 8 * 128], F32, kind="ExternalOutput").ap()
        o_egr = nc.dram_tensor("o_egr", [16, 2048], F32, kind="ExternalOutput").ap()
        o_x1 = nc.dram_tensor("o_x1", [128, D], F32, kind="ExternalOutput").ap()
        o_arow = nc.dram_tensor("o_arow", [1, BPC * D], F32, kind="ExternalOutput").ap()

    def dint(name, shape, dt, shared=False):
        return nc.dram_tensor(name, shape, dt, kind="Internal",
                              addr_space="Shared" if shared else "Local").ap()

    x1_hbm = dint("x1_hbm", [TPC, D], F32)
    yhi_hbm = dint("yhi_hbm", [TPC, D], BF16)
    ylo_hbm = dint("ylo_hbm", [TPC, D], BF16)
    r_bounce = dint("r_bounce", [BPC * D], F32)
    t_bounce = dint("t_bounce", [BPC * D], F32)
    cnt_in = dint("cnt_in", [1, E], F32)
    cnt_all = dint("cnt_all", [NC, E], F32, shared=True)
    cand_in = dint("cand_in", [E, TPC], I16)
    cand_all = dint("cand_all", [NC * E, TPC], I16, shared=True)
    cand_mine = dint("cand_mine", [NC * TPC], I16)
    y_all = dint("y_all", [T, D], BF16, shared=True)
    contrib_in = dint("contrib_in", [CAP, D], BF16)
    contrib_all = dint("contrib_all", [E * CAP, D], BF16, shared=True)
    g1_dram = dint("g1_dram", [TPC], I16)
    g2_dram = dint("g2_dram", [TPC], I16)

    A = nc.alloc_sbuf_tensor
    sb_iden = A("c_iden", [128, 128], F32)
    sb_ones = A("c_ones", [128, 1], F32)
    sb_onesrow = A("c_onesrow", [1, 128], F32)
    sb_triu16 = A("c_triu16", [16, 16], F32)
    sb_triu32 = A("c_triu32", [32, 32], F32)
    sb_maskc = A("c_maskc", [E, 32], F32)
    sb_iota8 = A("c_iota8", [128, 8], F32)
    sb_iota8m = A("c_iota8m", [128, 8], F32)
    sb_iotaE = A("c_iotaE", [32, 8, 128], F32)
    sb_iotaG = A("c_iotaG", [32, 128], F32)
    sb_idxid = A("c_idxid", [128, 256], I16)
    sb_idx8 = A("c_idx8", [128, 1], I16)
    sb_iotasl = A("c_iotasl", [16, 256], F32)
    sb_m116 = A("c_m116", [16, 2048], I16)
    sb_m132 = A("c_m132", [32, 128], F32)
    canI2 = A("r_canI2", [128, 32, 8], I16)
    g_i16 = A("r_gi16", [16, 2, 256], I16)
    sb_z16 = A("c_z16", [16, 256], I16)
    sb_embT = A("c_embT", [128, 8, E], F32)
    sb_ehi = A("c_ehi", [128, 8, E], BF16)
    sb_elo = A("c_elo", [128, 8, E], BF16)
    sb_elof = A("c_elof", [128, 8, E], F32)

    sc_y = A("s_y", [128, D], F32)
    sc_y2 = A("s_y2", [128, D], F32)
    sc_hi = A("s_hi", [128, D], BF16)
    sc_lo = A("s_lo", [128, D], BF16)
    sc_lo2 = A("s_lo2", [128, D], BF16)
    sc_lo3 = A("s_lo3", [128, D], BF16)
    sxC = A("s_sxC", [128, NT], F32)
    sx2C = A("s_sx2C", [128, NT], F32)
    mC = A("s_mC", [128, NT], F32)
    varC = A("s_varC", [128, NT], F32)
    rsC = A("s_rsC", [128, NT], F32)
    nbC = A("s_nbC", [128, NT], F32)
    r3a = A("s_r3a", [128, NT, 8], F32)
    r3b = A("s_r3b", [128, NT, 8], F32)
    mx1C = A("s_mx1C", [128, NT], F32)
    mx2C = A("s_mx2C", [128, NT], F32)
    r_row = A("s_rrow", [1, BPC * D], F32)
    t_sb = A("s_t", [BPC, D], F32)
    a_sb = A("s_a", [BPC, D], F32)
    a_row = A("s_arow", [1, BPC * D], F32)
    rfm = A("s_rfm", [128, 8, BPC], F32)
    tfm = A("s_tfm", [128, 8, BPC], F32)
    wv_tile = A("s_wv", [128, 2, D], F32)
    dots_tm = A("s_dotstm", [128, NT, 8], F32)
    mx8 = A("r_mx8", [128, 8], F32)
    m1r = A("r_m1r", [128, 8], F32)
    m2r = A("r_m2r", [128, 8], F32)
    m1t = A("r_m1t", [128, 8], F32)
    m2t = A("r_m2t", [128, 8], F32)
    cc8 = A("r_cc8", [128, 8], F32)
    packA = A("r_packA", [128, 96], F32)
    packB = A("r_packB", [96, 128], F32)
    ix2B0 = A("r_ix2B0", [32, 128], F32)
    d12B0 = A("r_d12B0", [32, 128], F32)
    totB = A("r_totB", [32, 8], F32)
    pfxT = A("r_pfxT", [32, 8], F32)
    cntsb = A("r_cnt", [1, 8], F32)
    cntall_sb = A("r_cntall", [8, 8], F32)
    sB1 = A("r_sB1", [32, 128], F32)
    sB2 = A("r_sB2", [32, 128], F32)
    w1B = A("r_w1B", [32, 128], F32)
    w2B = A("r_w2B", [32, 128], F32)
    weB = A("r_weB", [32, 2, 128], F32)
    gB = A("r_gB", [32, 2, 128], F32)
    we_tm = A("r_wetm", [128, 2, NT], F32)
    clist = A("e_clist", [16, 256], I16)
    spA = A("e_spA", [16, 512], I16)
    spB = A("e_spB", [16, 512], I16)
    spC = A("e_spC", [16, 512], I16)
    nf_s = A("e_nfs", [1, 8], U32)
    nf_sf = A("e_nfsf", [1, 1], F32)
    nf_sb = A("e_nf", [1, 1], U32)
    nf_f = A("e_nff", [1, 1], F32)
    nfb = A("e_nfb", [16, 1], F32)
    evalid = A("e_valid", [16, 256], U8)
    clrep = A("e_clrep", [128, 256], I16)
    g1il = A("e_g1il", [128, 256], I16)
    g2il = A("e_g2il", [128, 256], I16)
    rowg = A("e_rowg", [128, TPC], I16)
    sb_idg = A("s_idg", [128, 64], I16)

    ARENA = 120 * KB
    _guard = A("arena_guard", [128, ARENA], U8)
    abase = nc.lookup_mloc(_guard).addr

    def AT(name, shape, dt, off):
        n = int(np.prod(shape[1:])) * mybir.dt.size(dt)
        assert off + n <= ARENA, f"{name}: {off}+{n} > {ARENA}"
        return nc.alloc_sbuf_tensor_at(name, shape, dt, offset=abase + off)

    xt2 = AT("p_xt2", [128, 2, D], F32, 0)
    attn_bc = AT("p_attnbc", [128, BPC, D], F32, 8 * KB)
    yfm_hi = AT("p_yfmhi", [128, 2, 8, 512], BF16, 16 * KB)
    yfm_lo = AT("p_yfmlo", [128, 2, 8, 512], BF16, 32 * KB)
    dots_em = AT("p_dotsem", [8, TPC], F32, 64 * KB)
    m1B = AT("p_m1B", [32, 8, 128], F32, 80 * KB)
    m2B = AT("p_m2B", [32, 8, 128], F32, 84 * KB)
    onesB = AT("p_onesB", [32, 8, 128], F32, 88 * KB)
    csumB = AT("p_csumB", [32, 8, 128], F32, 92 * KB)
    grankB = AT("p_grankB", [32, 8, 128], F32, 96 * KB)
    survB = AT("p_survB", [32, 8, 128], F32, 100 * KB)
    scrB = AT("p_scrB", [32, 8, 128], F32, 104 * KB)
    candB = AT("p_candB", [32, 8, 128], F32, 108 * KB)
    w1t = AT("p_w1t", [128, 8, I], BF16, 0)
    w2t = AT("p_w2t", [128, 16, D], BF16, 32 * KB)
    craw = AT("p_craw", [16, 8, 256], I16, 64 * KB)
    ecand = AT("p_ecand", [16, 2048], I16, 68 * KB)
    emask = AT("p_emask", [16, 2048], F32, 72 * KB)
    eX1 = AT("p_eX1", [16, 2048], F32, 80 * KB)
    eX2 = AT("p_eX2", [16, 2048], F32, 88 * KB)
    yfm2 = AT("p_yfm2", [128, 2, 8, 512], BF16, 96 * KB)
    inter = AT("p_inter", [128, 16, 512], BF16, 64 * KB)
    ctile = AT("p_ctile", [128, 2, D], BF16, 80 * KB)
    c1b = AT("p_c1b", [128, 2, 4, D], BF16, 0)
    c2b = AT("p_c2b", [128, 2, 4, D], BF16, 16 * KB)
    cxg = AT("p_cxg", [128, 2, 4, D], F32, 32 * KB)
    o1g = AT("p_o1g", [128, 2, 4, D], F32, 64 * KB)
    o2g = AT("p_o2g", [128, 4, D], F32, 96 * KB)

    pb = [nc.alloc_psum_tensor(f"pb{i}", [128, 512], F32) for i in range(8)]

    s_x = Sem(nc, "s_x")
    s_v = Sem(nc, "s_v")
    s_s = Sem(nc, "s_s")
    s_t = Sem(nc, "s_t")
    s_g = Sem(nc, "s_g")
    s_st = Sem(nc, "s_st")
    s_o = Sem(nc, "s_o")
    s_cc = Sem(nc, "s_cc")

    V, SC, TE, GP, SY = nc.vector, nc.scalar, nc.tensor, nc.gpsimd, nc.sync

    def cumsum(out_ap, in_ap):
        return V.tensor_tensor_scan(out=out_ap, data0=in_ap, data1=in_ap,
                                    initial=0.0, op0=ALU.add, op1=ALU.bypass)

    for dst, src in [
        (sb_iden[:], iden_d), (sb_ones[:], ones_d), (sb_onesrow[:], onesrow_d),
        (sb_triu16[:], triu16_d), (sb_triu32[:], triu32_d), (sb_maskc[:], maskc_d),
        (sb_iota8[:], iota8_d), (sb_iota8m[:], iota8m_d), (sb_iotasl[:], iotasl_d),
        (sb_iotaE[:], iotaE_d.rearrange("a (b c) -> a b c", b=8)),
        (sb_iotaG[:], iotaG_d),
        (sb_idxid[:], idxid_d), (sb_idx8[:], idx8_d),
        (sb_m116[:], m1_16_d),
        (sb_m132[:], m1_32_d),
        (sb_z16[:], z16_d),
        (sb_embT[:], embT_d.rearrange("(dc p) e -> p dc e", p=128)),
    ]:
        s_x.inc(SY.dma_start(out=dst, in_=src), 16)
    X0 = s_x.n
    if _os.environ.get("K_AGEARLY"):
        GP.wait_ge(s_x.h, X0)
        s_cc.inc(GP.collective_compute(
            "AllGather", ALU.bypass, replica_groups=[list(range(NC))],
            ins=[cnt_in], outs=[cnt_all]))
    V.wait_ge(s_x.h, X0)
    V.tensor_copy(sb_ehi[:], sb_embT[:])
    V.drain()
    V.tensor_copy(sb_elof[:], sb_ehi[:])
    V.drain()
    V.tensor_tensor(sb_elo[:], sb_embT[:], sb_elof[:], ALU.subtract)
    v_emb = s_v.inc(V.drain())

    if upto < 1:
        SY.wait_ge(s_st.h, s_st.n)
        SY.wait_ge(s_x.h, s_x.n)
        SY.wait_ge(s_v.h, s_v.n)
        SY.wait_ge(s_g.h, s_g.n)
        SY.wait_ge(s_cc.h, s_cc.n)
        return nc
    passA_done = []
    for t in range(NT):
        par = t % 2
        ld = SY.dma_start(out=xt2[:, par, :], in_=xs_d[128 * t:128 * (t + 1), :])
        if t >= 2:
            ld.wait_op(s_v.h, passA_done[t - 2], "sem-ge")
        x_t = s_x.inc(ld, 16)
        xt = xt2[:, par, :]
        SC.wait_ge(s_x.h, x_t)
        sq_t = s_s.inc(SC.activation(out=sc_y2[:], in_=xt, func=AF.Square, accum_out=sx2C[:, t:t + 1]))
        V.wait_ge(s_s.h, sq_t)
        passA_done.append(s_v.inc(V.reduce_sum(sxC[:, t:t + 1], xt, axis=AX.X)))
    V.wait_ge(s_s.h, s_s.n)
    V.drain()
    V.tensor_scalar_mul(mC[:], sxC[:], 1.0 / D)
    V.drain()
    V.tensor_tensor(varC[:], mC[:], mC[:], ALU.mult)
    V.drain()
    V.scalar_tensor_tensor(varC[:], sx2C[:], 1.0 / D, varC[:], ALU.mult, ALU.subtract)
    V.drain()
    v_var = s_v.inc(V.tensor_scalar_add(varC[:], varC[:], EPS))
    SC.wait_ge(s_v.h, v_var)
    sqr = s_s.inc(SC.activation(out=rsC[:], in_=varC[:], func=AF.Sqrt, bias=0.0, scale=1.0))
    V.wait_ge(s_s.h, sqr)
    V.reciprocal(rsC[:], rsC[:])
    V.drain()
    V.tensor_tensor(nbC[:], mC[:], rsC[:], ALU.mult)
    V.drain()
    v_nb = s_v.inc(V.tensor_scalar_mul(nbC[:], nbC[:], -1.0))
    SC.wait_ge(s_v.h, v_nb)
    t_cons = []
    for t in range(NT):
        b = t // 16
        par = t % 2
        first = (t % 16 == 0)
        last = (t % 16 == 15)
        ld = SY.dma_start(out=xt2[:, par, :], in_=xs_d[128 * t:128 * (t + 1), :])
        if t >= 2:
            ld.wait_op(s_s.h, t_cons[t - 2][0], "sem-ge")
        else:
            ld.wait_op(s_v.h, passA_done[-1], "sem-ge")
        x_t = s_x.inc(ld, 16)
        xt = xt2[:, par, :]
        SC.wait_ge(s_x.h, x_t)
        if t >= 2:
            SC.wait_ge(s_t.h, t_cons[t - 2][1])
        yslot = sc_y[:] if par == 0 else sc_y2[:]
        y_t = s_s.inc(SC.activation(out=yslot, in_=xt, func=AF.Identity,
                                    bias=nbC[:, t:t + 1], scale=rsC[:, t:t + 1]))
        TE.wait_ge(s_s.h, y_t)
        TE.matmul(pb[2 * b][0:1, :], sb_ones[:], yslot[:, 0:512] if False else (sc_y[:, 0:512] if par == 0 else sc_y2[:, 0:512]), start=first, stop=last)
        t_r = s_t.inc(TE.matmul(pb[2 * b + 1][0:1, :], sb_ones[:], sc_y[:, 512:1024] if par == 0 else sc_y2[:, 512:1024], start=first, stop=last))
        t_cons.append((y_t, t_r))

    V.wait_ge(s_t.h, s_t.n)
    for b in range(BPC):
        V.tensor_copy(r_row[0:1, D * b:D * b + 512], pb[2 * b][0:1, :])
        V.tensor_copy(r_row[0:1, D * b + 512:D * (b + 1)], pb[2 * b + 1][0:1, :])
    v_r = s_v.inc(V.drain())

    def matvec_fm(src_rows, bounce, fm_tile, w_dram, ps_lo, ps_hi, prod_sem_val):
        SY.wait_ge(s_v.h, prod_sem_val)
        s_st.inc(SY.dma_start(out=bounce.rearrange("(o a) -> o a", o=1), in_=src_rows), 16)
        SY.wait_ge(s_st.h, s_st.n)
        for b_ in range(BPC):
            s_x.inc(SY.dma_start(out=fm_tile[:, :, b_],
                                 in_=bounce.rearrange("(b k p) -> b p k", p=128, k=8)[b_],
                                 allow_slow_non_contiguous=True), 16)
        t_base = s_t.n
        x_base = s_x.n
        for dc in range(8):
            ld = SY.dma_start(out=wv_tile[:, dc % 2, :], in_=w_dram[128 * dc:128 * (dc + 1), :])
            if dc >= 2:
                ld.wait_op(s_t.h, t_base + dc - 1, "sem-ge")
            s_x.inc(ld, 16)
        for dc in range(8):
            TE.wait_ge(s_x.h, x_base + 16 * (dc + 1))
            TE.matmul(ps_lo[0:BPC, :], fm_tile[:, dc, :], wv_tile[:, dc % 2, 0:512],
                      start=(dc == 0), stop=(dc == 7))
            s_t.inc(TE.matmul(ps_hi[0:BPC, :], fm_tile[:, dc, :], wv_tile[:, dc % 2, 512:1024],
                              start=(dc == 0), stop=(dc == 7)))

    matvec_fm(r_row[:], r_bounce, rfm, wvT_d, pb[4], pb[5], v_r)
    V.wait_ge(s_t.h, s_t.n)
    V.tensor_copy(t_sb[:, 0:512], pb[4][0:BPC, :])
    V.tensor_copy(t_sb[:, 512:1024], pb[5][0:BPC, :])
    v_t = s_v.inc(V.drain())
    matvec_fm(t_sb[:], t_bounce, tfm, woT_d, pb[6], pb[7], v_t)
    V.wait_ge(s_t.h, s_t.n)
    V.tensor_copy(a_sb[:, 0:512], pb[6][0:BPC, :])
    V.tensor_copy(a_sb[:, 512:1024], pb[7][0:BPC, :])
    v_a = s_v.inc(V.drain())
    SY.wait_ge(s_v.h, v_a)
    s_st.inc(SY.dma_start(out=t_bounce.rearrange("(b d) -> b d", b=BPC), in_=a_sb[:]), 16)
    SY.wait_ge(s_st.h, s_st.n)
    x_ar = s_x.inc(SY.dma_start(out=a_row[:], in_=t_bounce.rearrange("(o a) -> o a", o=1)), 16)
    TE.wait_ge(s_x.h, x_ar)
    for b in range(BPC):
        TE.matmul(pb[0][:], sb_onesrow[:], a_row[0:1, D * b:D * b + 512], start=True, stop=True)
        t_bc = s_t.inc(TE.matmul(pb[1][:], sb_onesrow[:], a_row[0:1, D * b + 512:D * b + 1024], start=True, stop=True))
        V.wait_ge(s_t.h, t_bc)
        V.tensor_copy(attn_bc[:, b, 0:512], pb[0][:])
        v_bc = s_v.inc(V.tensor_copy(attn_bc[:, b, 512:1024], pb[1][:]))
        TE.wait_ge(s_v.h, v_bc)
    v_abc = s_v.inc(V.drain())

    if upto < 3:
        SY.wait_ge(s_st.h, s_st.n)
        SY.wait_ge(s_x.h, s_x.n)
        SY.wait_ge(s_v.h, s_v.n)
        SY.wait_ge(s_g.h, s_g.n)
        SY.wait_ge(s_cc.h, s_cc.n)
        return nc
    V.wait_ge(s_v.h, v_abc)
    pA = []
    for t in range(S3T):
        par = t % 2
        b = t // 16
        ld = SY.dma_start(out=xt2[:, par, :], in_=xs_d[128 * t:128 * (t + 1), :])
        if t >= 2:
            ld.wait_op(s_st.h, pA[t - 2], "sem-ge")
        x_t = s_x.inc(ld, 16)
        xt = xt2[:, par, :]
        V.wait_ge(s_x.h, x_t)
        v_x1 = s_v.inc(V.tensor_tensor(xt, xt, attn_bc[:, b, :], ALU.add))
        SC.wait_ge(s_v.h, v_x1)
        sq_t = s_s.inc(SC.activation(out=sc_y2[:], in_=xt, func=AF.Square, accum_out=sx2C[:, t:t + 1]))
        V.wait_ge(s_s.h, sq_t)
        s_v.inc(V.reduce_sum(sxC[:, t:t + 1], xt, axis=AX.X))
        pA.append(s_st.inc(SY.dma_start(out=x1_hbm[128 * t:128 * (t + 1), :],
                                        in_=xt).wait_op(s_v.h, s_v.n, "sem-ge"), 16))
    V.wait_ge(s_s.h, s_s.n)
    V.drain()
    V.tensor_scalar_mul(mC[:], sxC[:], 1.0 / D)
    V.drain()
    V.tensor_tensor(varC[:], mC[:], mC[:], ALU.mult)
    V.drain()
    V.scalar_tensor_tensor(varC[:], sx2C[:], 1.0 / D, varC[:], ALU.mult, ALU.subtract)
    V.drain()
    v_var = s_v.inc(V.tensor_scalar_add(varC[:], varC[:], EPS))
    SC.wait_ge(s_v.h, v_var)
    sqr = s_s.inc(SC.activation(out=rsC[:], in_=varC[:], func=AF.Sqrt, bias=0.0, scale=1.0))
    V.wait_ge(s_s.h, sqr)
    V.reciprocal(rsC[:], rsC[:])
    V.drain()
    V.tensor_tensor(nbC[:], mC[:], rsC[:], ALU.mult)
    V.drain()
    v_nb = s_v.inc(V.tensor_scalar_mul(nbC[:], nbC[:], -1.0))
    SC.wait_ge(s_v.h, v_nb)
    tile_done = []
    for t in range(S3T):
        par = t % 2
        ld = SY.dma_start(out=xt2[:, par, :], in_=x1_hbm[128 * t:128 * (t + 1), :])
        ld.wait_op(s_st.h, max(pA[-1], tile_done[t - 2]) if t >= 2 else pA[-1], "sem-ge")
        x_t = s_x.inc(ld, 16)
        xt = xt2[:, par, :]
        SC.wait_ge(s_x.h, x_t)
        if t >= 2:
            SC.wait_ge(s_st.h, tile_done[t - 2])
        yslot = sc_y[:] if par == 0 else sc_y2[:]
        hslot = sc_hi[:, 0:512].bitcast(BF16) if False else (sc_hi[:] if par == 0 else sc_lo[:])
        SC.activation(out=yslot, in_=xt, func=AF.Identity,
                      bias=nbC[:, t:t + 1], scale=rsC[:, t:t + 1])
        hi_t = s_s.inc(SC.activation(out=hslot, in_=xt, func=AF.Identity,
                                     bias=nbC[:, t:t + 1], scale=rsC[:, t:t + 1]))
        V.wait_ge(s_s.h, hi_t)
        lslot = sc_lo2[:] if par == 0 else sc_lo3[:]
        v_lo = s_v.inc(V.tensor_tensor(lslot, yslot, hslot, ALU.subtract))
        s_st.inc(SY.dma_start(out=yhi_hbm[128 * t:128 * (t + 1), :], in_=hslot).wait_op(s_s.h, hi_t, "sem-ge"), 16)
        st_t = s_st.inc(SY.dma_start(out=ylo_hbm[128 * t:128 * (t + 1), :], in_=lslot).wait_op(s_v.h, v_lo, "sem-ge"), 16)
        tile_done.append(st_t)
    ST3 = s_st.n

    NOCC = bool(_os.environ.get("K_NOCC"))
    GP.load_library(library_config.mlp)

    def y_chunk(k):
        GP.wait_ge(s_st.h, tile_done[8 * k + 7])
        return s_cc.inc(GP.collective_compute(
            "AllGather", ALU.bypass, replica_groups=[list(range(NC))],
            ins=[yhi_hbm[1024 * k:1024 * (k + 1), :]],
            outs=[y_all[8192 * k:8192 * (k + 1), :]]))

    cc_y = 0
    if not NOCC:
        y_chunk(0)
        cc_y = y_chunk(1)

    if upto < 4:
        SY.wait_ge(s_st.h, s_st.n)
        SY.wait_ge(s_x.h, s_x.n)
        SY.wait_ge(s_v.h, s_v.n)
        SY.wait_ge(s_g.h, s_g.n)
        SY.wait_ge(s_cc.h, s_cc.n)
        return nc
    V.wait_ge(s_v.h, v_emb)
    dots_mm = []
    for g in range(0 if _os.environ.get("K_NODG") else 8):
        gp_ = g % 2
        GP.wait_ge(s_st.h, tile_done[4 * g + 3])
        if g >= 2:
            GP.wait_ge(s_t.h, dots_mm[g - 2])
        s_g.inc(GP.dma_gather(
            yfm_hi[:, gp_], yhi_hbm, sb_idxid[:, 32 * g:32 * (g + 1)], 512, 512, D,
            elem_step=D, transpose=True), 16)
        g_lo = s_g.inc(GP.dma_gather(
            yfm_lo[:, gp_], ylo_hbm, sb_idxid[:, 32 * g:32 * (g + 1)], 512, 512, D,
            elem_step=D, transpose=True), 16)
        TE.wait_ge(s_g.h, g_lo)
        ps = pb[g]
        n_mm = 0
        t_d = None
        for (yt, et) in ((yfm_hi, sb_ehi), (yfm_hi, sb_elo), (yfm_lo, sb_ehi), (yfm_lo, sb_elo)):
            for dc in range(8):
                inst = TE.matmul(ps[0:8, :], et[:, dc, :], yt[:, gp_, dc, :],
                                 start=(n_mm == 0), stop=(n_mm == 31))
                n_mm += 1
                if n_mm == 32:
                    t_d = s_t.inc(inst)
        dots_mm.append(t_d)
        V.wait_ge(s_t.h, t_d)
        s_v.inc(V.tensor_copy(dots_em[:, 512 * g:512 * (g + 1)], ps[0:8, :]))

    if dots_mm:
        GP.wait_ge(s_t.h, dots_mm[-1])
    w_ffn = s_g.n
    if not _os.environ.get("K_NOW"):
        for icg in range(8):
            s_g.inc(GP.dma_start(out=w1t[:, icg, :],
                                 in_=w1T_d.rearrange("(dc p) i -> p dc i", p=128)[:, icg, :]), 16)
        for icg in range(16):
            s_g.inc(GP.dma_start(out=w2t[:, icg, :], in_=w2T_d[128 * icg:128 * (icg + 1), :]), 16)
        w_ffn = s_g.n

    if upto < 5:
        SY.wait_ge(s_st.h, s_st.n)
        SY.wait_ge(s_x.h, s_x.n)
        SY.wait_ge(s_v.h, s_v.n)
        SY.wait_ge(s_g.h, s_g.n)
        SY.wait_ge(s_cc.h, s_cc.n)
        return nc
    V.wait_ge(s_v.h, s_v.n)
    TE.wait_ge(s_v.h, s_v.n)
    for t in range(NT):
        ps = pb[4 + t % 2]
        t_tr = s_t.inc(TE.transpose(ps[0:128, 0:8], dots_em[:, 128 * t:128 * (t + 1)], sb_iden[0:8, 0:8]))
        V.wait_ge(s_t.h, t_tr)
        v_tr = s_v.inc(V.tensor_copy(dots_tm[:, t, :], ps[0:128, 0:8]))
        TE.wait_ge(s_v.h, v_tr)
    dots3 = dots_tm[:]
    iota3 = sb_iota8[:].unsqueeze(1).to_broadcast([128, NT, 8])
    iotam3 = sb_iota8m[:].unsqueeze(1).to_broadcast([128, NT, 8])
    V.wait_ge(s_v.h, s_v.n)
    V.drain()
    V.reduce_max(mx1C[:], dots3, axis=AX.X)
    V.drain()
    V.tensor_tensor(r3a[:], dots3, mx1C[:].unsqueeze(2).to_broadcast([128, NT, 8]), ALU.is_equal)
    V.drain()
    V.tensor_tensor(r3b[:], r3a[:], iotam3, ALU.mult)
    V.drain()
    V.tensor_reduce(packA[:, 0:32], r3b[:], axis=AX.X, op=ALU.min)
    V.drain()
    V.tensor_scalar_add(packA[:, 0:32], packA[:, 0:32], 1024.0)
    V.drain()
    V.tensor_tensor(r3a[:], iota3, packA[:, 0:32].unsqueeze(2).to_broadcast([128, NT, 8]), ALU.is_equal)
    V.drain()
    V.scalar_tensor_tensor(r3b[:], r3a[:], -1e30, dots3, ALU.mult, ALU.add)
    V.drain()
    V.reduce_max(mx2C[:], r3b[:], axis=AX.X)
    V.drain()
    V.tensor_tensor(r3a[:], r3b[:], mx2C[:].unsqueeze(2).to_broadcast([128, NT, 8]), ALU.is_equal)
    V.drain()
    V.tensor_tensor(r3b[:], r3a[:], iotam3, ALU.mult)
    V.drain()
    V.tensor_reduce(packA[:, 32:64], r3b[:], axis=AX.X, op=ALU.min)
    V.drain()
    V.tensor_scalar_add(packA[:, 32:64], packA[:, 32:64], 1024.0)
    V.drain()
    V.tensor_tensor(packA[:, 64:96], mx1C[:], mx2C[:], ALU.subtract)
    V.drain()
    v_pack = s_v.inc(V.drain())
    TE.wait_ge(s_v.h, v_pack)
    t_pk = s_t.inc(TE.transpose(pb[6][0:96, 0:128], packA[:, 0:96], sb_iden[:]))
    V.wait_ge(s_t.h, t_pk)
    v_pk = s_v.inc(V.tensor_copy(packB[:], pb[6][0:96, 0:128]))
    SY.wait_ge(s_v.h, v_pk)
    s_x.inc(SY.dma_start(out=ix2B0[:], in_=packB[32:64, :]), 16)
    s_x.inc(SY.dma_start(out=d12B0[:], in_=packB[64:96, :]), 16)
    x_pk = s_x.n
    ix1B = packB[0:32, :]
    ix2B = ix2B0[:]
    d12B = d12B0[:]
    SC.wait_ge(s_x.h, x_pk)
    V.wait_ge(s_x.h, x_pk)
    SC.wait_ge(s_v.h, v_pk)
    SC.activation(out=w1B[:], in_=d12B, func=AF.Sigmoid, bias=0.0, scale=1.0)
    s_s.inc(SC.activation(out=w2B[:], in_=d12B, func=AF.Sigmoid, bias=0.0, scale=-1.0))
    V.wait_ge(s_v.h, v_pk)
    V.tensor_tensor(m1B[:], ix1B.unsqueeze(1).to_broadcast([32, 8, 128]), sb_iotaE[:], ALU.is_equal)
    V.tensor_tensor(m2B[:], ix2B.unsqueeze(1).to_broadcast([32, 8, 128]), sb_iotaE[:], ALU.is_equal)
    V.drain()
    V.tensor_tensor(onesB[:], m1B[:], m2B[:], ALU.add)
    V.drain()
    for e in range(E):
        cumsum(csumB[:, e, :], onesB[:, e, :])
        V.drain()
    V.tensor_copy(totB[:], csumB[:, :, 127])
    v_tot = s_v.inc(V.drain())
    TE.wait_ge(s_v.h, v_tot)
    t_cnt = s_t.inc(TE.matmul(pb[7][0:1, 0:8], sb_ones[0:32, :], totB[:], start=True, stop=True))
    V.wait_ge(s_t.h, t_cnt)
    v_cnt = s_v.inc(V.tensor_copy(cntsb[:], pb[7][0:1, 0:8]))
    SY.wait_ge(s_v.h, v_cnt)
    s_st.inc(SY.dma_start(out=cnt_in, in_=cntsb[:]), 16)
    GP.wait_ge(s_st.h, s_st.n)
    cc_cnt = s_cc.inc(GP.collective_compute(
        "AllGather", ALU.bypass, replica_groups=[list(range(NC))],
        ins=[cnt_in], outs=[cnt_all]))
    V.tensor_scalar(survB[:], onesB[:], 0.5, None, ALU.is_gt)
    v_su = s_v.inc(V.drain())
    sU8 = AT("p_sU8", [32, 8, 128], U8, 112 * KB)
    V.tensor_copy(sU8[:], survB[:])
    V.drain()
    V.select(candB[:], sU8[:], sb_iotaG[:].unsqueeze(1).to_broadcast([32, 8, 128]),
             sb_m132[:].unsqueeze(1).to_broadcast([32, 8, 128]))
    V.drain()
    cand4 = survB[:].rearrange("p a (b c) -> p b a c", b=8, c=16)
    v_cand = s_v.inc(V.tensor_copy(cand4, candB[:]))
    TE.wait_ge(s_v.h, v_cand)
    t_ct = None
    for fc in range(8):
        t_ct = s_t.inc(TE.transpose(
            pb[0][0:128, 32 * fc:32 * (fc + 1)],
            survB[:, fc, :], sb_iden[0:32, 0:32]))
    V.wait_ge(s_t.h, t_ct)
    v_ci = s_v.inc(V.tensor_copy(canI2[:].rearrange("p a b -> p (a b)"),
                                 pb[0][0:128, 0:256].rearrange("p (a b) -> p b a", a=8)))
    SY.wait_ge(s_v.h, v_ci)
    s_st.inc(SY.dma_start(out=cand_in.rearrange("e (q c) -> (e q) c", q=16),
                          in_=canI2[:].rearrange("p a b -> p (a b)")), 16)
    GP.wait_ge(s_st.h, s_st.n)
    cc_cand = s_cc.inc(GP.collective_compute(
        "AllGather", ALU.bypass, replica_groups=[list(range(NC))],
        ins=[cand_in], outs=[cand_all]))
    cc_bar1 = s_cc.inc(GP.collective_compute(
        "AllGather", ALU.bypass, replica_groups=[list(range(NC))],
        ins=[cnt_in], outs=[cnt_all]))
    if not NOCC:
        y_chunk(2)
        cc_y = y_chunk(3)
    cc_bar2 = s_cc.inc(GP.collective_compute(
        "AllGather", ALU.bypass, replica_groups=[list(range(NC))],
        ins=[cnt_in], outs=[cnt_all]))
    SY.wait_ge(s_cc.h, cc_cand)
    x_cnt = s_x.inc(SY.dma_start(out=cntall_sb[:], in_=cnt_all), 16)
    TE.wait_ge(s_x.h, x_cnt)
    TE.wait_ge(s_v.h, v_cnt)
    TE.matmul(pb[7][0:32, 0:8], sb_triu32[:], totB[:], start=True, stop=False)
    t_pfx = s_t.inc(TE.matmul(pb[7][0:32, 0:8], sb_maskc[:], cntall_sb[:], start=False, stop=True))
    V.wait_ge(s_t.h, t_pfx)
    v_pfx = s_v.inc(V.tensor_copy(pfxT[:], pb[7][0:32, 0:8]))
    V.wait_ge(s_v.h, v_pfx)
    V.tensor_tensor(grankB[:], csumB[:], pfxT[:].unsqueeze(2).to_broadcast([32, 8, 128]), ALU.add)
    v_gr_tap = s_v.inc(V.drain())
    if DBG:
        s_o.inc(SY.dma_start(out=o_grank, in_=grankB[:].rearrange("a b c -> a (b c)")).wait_op(s_v.h, v_gr_tap, "sem-ge"), 16)
        s_o.inc(SY.dma_start(out=o_csum, in_=csumB[:].rearrange("a b c -> a (b c)")).wait_op(s_v.h, v_gr_tap, "sem-ge"), 16)
    V.tensor_scalar(survB[:], grankB[:], float(CAP), None, ALU.is_le)
    V.drain()
    V.tensor_tensor(survB[:], survB[:], onesB[:], ALU.mult)
    V.drain()
    def sum_e(dst, msk, val):
        V.tensor_tensor(scrB[:], msk, val, ALU.mult)
        V.drain()
        V.tensor_tensor(dst, scrB[:, 0, :], scrB[:, 1, :], ALU.add)
        for e_ in range(2, E):
            V.drain()
            V.tensor_tensor(dst, dst, scrB[:, e_, :], ALU.add)
        V.drain()

    sum_e(sB1[:], m1B[:], grankB[:])
    sum_e(sB2[:], m2B[:], grankB[:])
    V.tensor_scalar(sB1[:], sB1[:], -1.0, None, ALU.add)
    V.tensor_scalar(sB2[:], sB2[:], -1.0, None, ALU.add)
    V.drain()
    V.tensor_scalar(sB1[:], sB1[:], float(CAP - 1), None, ALU.min)
    V.tensor_scalar(sB2[:], sB2[:], float(CAP - 1), None, ALU.min)
    V.drain()
    for k, (sBk, ixk) in enumerate(((sB1, ix1B), (sB2, ix2B))):
        V.tensor_scalar(scrB[:, 0, :], sBk[:], 1024.0 - 0.5, None, ALU.is_ge)
        V.tensor_scalar(scrB[:, 1, :], sBk[:], 2048.0 - 0.5, None, ALU.is_ge)
        V.tensor_scalar(scrB[:, 2, :], sBk[:], 3072.0 - 0.5, None, ALU.is_ge)
        V.drain()
        V.tensor_tensor(scrB[:, 0, :], scrB[:, 0, :], scrB[:, 1, :], ALU.add)
        V.drain()
        V.tensor_tensor(scrB[:, 0, :], scrB[:, 0, :], scrB[:, 2, :], ALU.add)
        V.drain()
        V.scalar_tensor_tensor(scrB[:, 2, :], scrB[:, 0, :], 7168.0, sBk[:], ALU.mult, ALU.add)
        V.drain()
        V.scalar_tensor_tensor(gB[:, k, :], ixk, 1024.0, scrB[:, 2, :], ALU.mult, ALU.add)
        V.drain()
    v_g = s_v.inc(V.drain())
    sum_e(sB1[:], m1B[:], survB[:])
    sum_e(sB2[:], m2B[:], survB[:])
    V.wait_ge(s_s.h, s_s.n)
    V.tensor_tensor(weB[:, 0, :], w1B[:], sB1[:], ALU.mult)
    V.tensor_tensor(weB[:, 1, :], w2B[:], sB2[:], ALU.mult)
    v_we = s_v.inc(V.drain())
    TE.wait_ge(s_v.h, v_we)
    for k in range(2):
        t_we = s_t.inc(TE.transpose(pb[6][0:128, 0:32], weB[:, k, :], sb_iden[0:32, 0:32]))
        V.wait_ge(s_t.h, t_we)
        v_wt = s_v.inc(V.tensor_copy(we_tm[:, k, :], pb[6][0:128, 0:32]))
        TE.wait_ge(s_v.h, v_wt)
    TE.wait_ge(s_v.h, v_g)
    t_gt = None
    for k in range(2):
        for fc in range(8):
            t_gt = s_t.inc(TE.transpose(
                pb[1][0:16, 256 * k + 32 * fc:256 * k + 32 * (fc + 1)],
                gB[:, k, 16 * fc:16 * (fc + 1)], sb_iden[0:32, 0:32]))
    V.wait_ge(s_t.h, t_gt)
    V.tensor_copy(g_i16[:, 0, :], pb[1][0:16, 0:256].rearrange("p (a b) -> p b a", a=8))
    v_gi = s_v.inc(V.tensor_copy(g_i16[:, 1, :], pb[1][0:16, 256:512].rearrange("p (a b) -> p b a", a=8)))
    SY.wait_ge(s_v.h, v_gi)
    for k in range(8):
        s_x.inc(SY.dma_start(out=g1il[16 * k:16 * (k + 1), :], in_=g_i16[:, 0, :]), 16)
        s_x.inc(SY.dma_start(out=g2il[16 * k:16 * (k + 1), :], in_=g_i16[:, 1, :]), 16)
    x_gil = s_x.n

    if upto < 6:
        SY.wait_ge(s_st.h, s_st.n)
        SY.wait_ge(s_x.h, s_x.n)
        SY.wait_ge(s_v.h, s_v.n)
        SY.wait_ge(s_g.h, s_g.n)
        SY.wait_ge(s_cc.h, s_cc.n)
        return nc
    GP.wait_ge(s_cc.h, cc_bar1)
    g_rows = s_g.inc(GP.dma_gather(
        rowg[:].rearrange("p (c n) -> p c n", c=1), cand_all, sb_idx8[:], 8, 8, TPC,
        elem_step=TPC, transpose=False), 16)
    SY.wait_ge(s_g.h, g_rows)
    st_cm = s_st.inc(SY.dma_start(out=cand_mine.rearrange("(r n) -> r n", r=8), in_=rowg[0:8, :]), 16)
    if _os.environ.get("K_E6") == "1":
        SY.wait_ge(s_st.h, s_st.n)
        SY.wait_ge(s_x.h, s_x.n)
        SY.wait_ge(s_v.h, s_v.n)
        SY.wait_ge(s_g.h, s_g.n)
        SY.wait_ge(s_cc.h, s_cc.n)
        return nc
    SY.wait_ge(s_st.h, st_cm)
    x_cr = s_x.inc(SY.dma_start(out=craw[:],
                                in_=cand_mine.rearrange("(blk q c) -> q blk c", q=16, c=256)
                                ).wait_op(s_v.h, v_we, "sem-ge"), 16)
    if _os.environ.get("K_E6") == "2":
        SY.wait_ge(s_st.h, s_st.n)
        SY.wait_ge(s_x.h, s_x.n)
        SY.wait_ge(s_v.h, s_v.n)
        SY.wait_ge(s_g.h, s_g.n)
        SY.wait_ge(s_cc.h, s_cc.n)
        return nc
    V.wait_ge(s_x.h, x_cr)
    V.wait_ge(s_v.h, v_we)
    crawf = craw[:].rearrange("p a b -> p (a b)")
    if DBG:
        s_o.inc(SY.dma_start(out=o_craw, in_=crawf).wait_op(s_x.h, x_cr, "sem-ge"), 16)
    V.tensor_scalar(emask[:], crawf, 0.0, None, ALU.is_ge)
    v_em2 = s_v.inc(V.drain())
    TE.wait_ge(s_v.h, v_em2)
    for q in range(4):
        s_t.inc(TE.matmul(pb[q][0:1, :], sb_ones[0:16, :], emask[:, 512 * q:512 * (q + 1)],
                          start=True, stop=True))
    V.wait_ge(s_t.h, s_t.n)
    for q in range(4):
        V.tensor_copy(eX1[0:1, 512 * q:512 * (q + 1)], pb[q][0:1, :])
    V.drain()
    cumsum(eX2[0:1, :], eX1[0:1, :])
    V.drain()
    v_col = s_v.inc(V.tensor_tensor(eX2[0:1, :], eX2[0:1, :], eX1[0:1, :], ALU.subtract))
    TE.wait_ge(s_v.h, v_col)
    for q in range(4):
        TE.matmul(pb[4 + (q % 2)][0:16, :], sb_triu16[:], emask[:, 512 * q:512 * (q + 1)],
                  start=True, stop=False)
        t_p = s_t.inc(TE.matmul(pb[4 + (q % 2)][0:16, :], sb_onesrow[0:1, 0:16],
                                eX2[0:1, 512 * q:512 * (q + 1)], start=False, stop=True))
        V.wait_ge(s_t.h, t_p)
        v_ep = s_v.inc(V.tensor_copy(eX1[:, 512 * q:512 * (q + 1)], pb[4 + (q % 2)][0:16, :]))
        TE.wait_ge(s_v.h, v_ep)
    V.tensor_tensor(eX1[:], eX1[:], emask[:], ALU.add)
    V.drain()
    V.tensor_scalar(eX2[:], eX1[:], float(CAP), None, ALU.is_le)
    V.drain()
    V.tensor_tensor(eX2[:], eX2[:], emask[:], ALU.mult)
    V.drain()
    eU8 = eX1[:].bitcast(U8)[:, 0:2048]
    V.tensor_copy(eU8, eX2[:])
    V.drain()
    v_ec = s_v.inc(V.select(ecand[:], eU8, crawf, sb_m116[:]))
    if DBG:
        s_o.inc(SY.dma_start(out=o_ecand, in_=ecand[:]).wait_op(s_v.h, v_ec, "sem-ge"), 16)
        s_o.inc(SY.dma_start(out=o_egr, in_=eX2[:]).wait_op(s_v.h, v_ec, "sem-ge"), 16)
    if _os.environ.get("K_E6") == "3":
        SY.wait_ge(s_st.h, s_st.n)
        SY.wait_ge(s_x.h, s_x.n)
        SY.wait_ge(s_v.h, s_v.n)
        SY.wait_ge(s_g.h, s_g.n)
        SY.wait_ge(s_cc.h, s_cc.n)
        return nc
    GP.load_library(library_config.sparse_gather)
    GP.wait_ge(s_v.h, v_ec)

    def spg(in_ap, out_ap, nf_cell, fill_ap):
        g_ = s_g.inc(GP.sparse_gather(out=out_ap, in_=in_ap, num_found=nf_cell))
        V.wait_ge(s_g.h, g_)
        v1 = s_v.inc(V.tensor_copy(nf_sf[:], nf_cell))
        TE.wait_ge(s_v.h, v1)
        t1 = s_t.inc(TE.matmul(pb[6][0:16, 0:1], sb_onesrow[0:1, 0:16], nf_sf[:], start=True, stop=True))
        V.wait_ge(s_t.h, t1)
        V.tensor_copy(nfb[:], pb[6][0:16, 0:1])
        V.drain()
        V.tensor_tensor(evalid[:], sb_iotasl[:], nfb[:].to_broadcast([16, 256]), ALU.is_ge)
        V.drain()
        v2 = s_v.inc(V.copy_predicated(out_ap, evalid[:], fill_ap))
        GP.wait_ge(s_v.h, v2)
        return v2

    for q in range(4):
        spg(ecand[:, 512 * q:512 * (q + 1)],
            (spA if q < 2 else spB)[:, 256 * (q % 2):256 * (q % 2 + 1)],
            nf_s[0:1, q:q + 1], sb_m116[:, 0:256])
    spg(spA[:], spC[:, 0:256], nf_s[0:1, 4:5], sb_m116[:, 0:256])
    spg(spB[:], spC[:, 256:512], nf_s[0:1, 5:6], sb_m116[:, 0:256])
    v_cl0 = spg(spC[:], clist[:], nf_sb[:], sb_z16[:])
    GP.load_library(library_config.mlp)
    clf = AT("e_clf", [16, 256], F32, 72 * KB)
    cl_i = AT("e_cli", [16, 256], F32, 73 * KB)
    cl_t = AT("e_clt", [16, 256], F32, 74 * KB)
    cl_k = AT("e_clk", [16, 256], F32, 75 * KB)
    cl_s = AT("e_cls", [16, 256], F32, 76 * KB)
    V.drain()
    V.tensor_copy(clf[:], clist[:])
    V.drain()
    V.tensor_scalar(cl_i[:], clf[:], 4095.5, None, ALU.is_ge)
    V.drain()
    for j in range(2, 8):
        V.tensor_scalar(cl_t[:], clf[:], 4096.0 * j - 0.5, None, ALU.is_ge)
        V.drain()
        V.tensor_tensor(cl_i[:], cl_i[:], cl_t[:], ALU.add)
        V.drain()
    V.scalar_tensor_tensor(cl_t[:], cl_i[:], -4096.0, clf[:], ALU.mult, ALU.add)
    V.drain()
    V.tensor_scalar(cl_k[:], cl_t[:], 1023.5, None, ALU.is_ge)
    V.drain()
    for thr in (2047.5, 3071.5):
        V.tensor_scalar(cl_s[:], cl_t[:], thr, None, ALU.is_ge)
        V.drain()
        V.tensor_tensor(cl_k[:], cl_k[:], cl_s[:], ALU.add)
        V.drain()
    V.scalar_tensor_tensor(cl_s[:], cl_i[:], -3072.0, clf[:], ALU.mult, ALU.add)
    V.drain()
    V.scalar_tensor_tensor(clf[:], cl_k[:], 7168.0, cl_s[:], ALU.mult, ALU.add)
    V.drain()
    v_cl0 = s_v.inc(V.tensor_copy(clist[:], clf[:]))
    SY.wait_ge(s_v.h, v_cl0)
    for k in range(8):
        s_x.inc(SY.dma_start(out=clrep[16 * k:16 * (k + 1), :], in_=clist[:]), 16)
    x_clr = s_x.n

    if _os.environ.get("K_E6") == "5":
        SY.wait_ge(s_st.h, s_st.n)
        SY.wait_ge(s_x.h, s_x.n)
        SY.wait_ge(s_v.h, s_v.n)
        SY.wait_ge(s_g.h, s_g.n)
        SY.wait_ge(s_cc.h, s_cc.n)
        return nc
    if DBG:
        GP.wait_ge(s_o.h, s_o.n)
    GP.wait_ge(s_cc.h, cc_bar2)
    GP.wait_ge(s_x.h, x_clr)
    TE.wait_ge(s_v.h, v_gi)
    relu_done = {}
    st_ct = []
    gath_y = []
    mm1_last = {}
    for tg in range(8):
        gpar = tg % 2
        if tg >= 2:
            GP.wait_ge(s_t.h, mm1_last[tg - 2])
        g_y = s_g.inc(GP.dma_gather(
            yfm2[:, gpar], y_all, clrep[:, 32 * tg:32 * (tg + 1)], 512, 512, D,
            elem_step=D, transpose=True), 16)
        gath_y.append(g_y)
        TE.wait_ge(s_g.h, max(g_y, w_ffn))
        for ic in range(16):
            ps = pb[ic % 2]
            if (tg, ic - 2) in relu_done:
                TE.wait_ge(s_s.h, relu_done[(tg, ic - 2)])
            elif tg > 0 and ic < 2:
                TE.wait_ge(s_s.h, relu_done[(tg - 1, 14 + ic)])
            t_i = None
            for dc in range(8):
                inst = TE.matmul(ps[:], w1t[:, dc, 128 * ic:128 * (ic + 1)], yfm2[:, gpar, dc, :],
                                 start=(dc == 0), stop=(dc == 7))
                if dc == 7:
                    t_i = s_t.inc(inst)
            if ic == 15:
                mm1_last[tg] = t_i
            SC.wait_ge(s_t.h, t_i)
            relu_done[(tg, ic)] = s_s.inc(
                SC.activation(out=inter[:, ic, :], in_=ps[:], func=AF.Relu, bias=0.0, scale=1.0))
        TE.wait_ge(s_s.h, relu_done[(tg, 15)])
        for sub in range(4):
            cpar = sub % 2
            for hh in range(2):
                ps = pb[4 + 2 * cpar + hh]
                t_o = None
                for ic in range(16):
                    inst = TE.matmul(ps[:], inter[:, ic, 128 * sub:128 * (sub + 1)],
                                     w2t[:, ic, 512 * hh:512 * (hh + 1)],
                                     start=(ic == 0), stop=(ic == 15))
                    if ic == 15:
                        t_o = s_t.inc(inst)
                V.wait_ge(s_t.h, t_o)
                if len(st_ct) >= 2:
                    V.wait_ge(s_st.h, st_ct[-2])
                v_o = s_v.inc(V.tensor_copy(ctile[:, cpar, 512 * hh:512 * (hh + 1)], ps[:]))
                TE.wait_ge(s_v.h, v_o)
            row0 = 512 * tg + 128 * sub
            st_ct.append(s_st.inc(SY.dma_start(
                out=contrib_in[row0:row0 + 128, :],
                in_=ctile[:, cpar, :]).wait_op(s_v.h, s_v.n, "sem-ge"), 16))
    cc_con = None
    for k in range(4):
        GP.wait_ge(s_st.h, st_ct[4 * (2 * k + 1) + 3])
        cc_con = s_cc.inc(GP.collective_compute(
            "AllGather", ALU.bypass, replica_groups=[list(range(NC))],
            ins=[contrib_in[1024 * k:1024 * (k + 1), :]],
            outs=[contrib_all[8192 * k:8192 * (k + 1), :]]))
    cc_con = s_cc.inc(GP.collective_compute(
        "AllGather", ALU.bypass, replica_groups=[list(range(NC))],
        ins=[cnt_in], outs=[cnt_all]))

    if upto < 7:
        SY.wait_ge(s_st.h, s_st.n)
        SY.wait_ge(s_x.h, s_x.n)
        SY.wait_ge(s_v.h, s_v.n)
        SY.wait_ge(s_g.h, s_g.n)
        SY.wait_ge(s_cc.h, s_cc.n)
        return nc
    GP.wait_ge(s_cc.h, cc_con)
    GP.wait_ge(s_x.h, x_gil)
    fin = []
    o_base = s_o.n
    for g in range(8):
        gpar = g % 2
        if g >= 2:
            GP.wait_ge(s_v.h, fin[g - 2])
        g_c1 = s_g.inc(GP.dma_gather(
            c1b[:, gpar], contrib_all, g1il[:, 32 * g:32 * (g + 1)], 512, 512, D,
            elem_step=D, transpose=False), 16)
        g_c2 = s_g.inc(GP.dma_gather(
            c2b[:, gpar], contrib_all, g2il[:, 32 * g:32 * (g + 1)], 512, 512, D,
            elem_step=D, transpose=False), 16)
        ld = GP.dma_start(out=cxg[:, gpar], in_=x1_hbm[512 * g:512 * (g + 1), :].rearrange("(a p) d -> p a d", p=128))
        if g >= 2:
            ld.wait_op(s_v.h, fin[g - 2], "sem-ge")
        x_t = s_x.inc(ld, 16)
        we1g = we_tm[:, 0, 4 * g:4 * (g + 1)].unsqueeze(2).to_broadcast([128, 4, D])
        we2g = we_tm[:, 1, 4 * g:4 * (g + 1)].unsqueeze(2).to_broadcast([128, 4, D])
        V.wait_ge(s_g.h, g_c2)
        V.wait_ge(s_x.h, x_t)
        if g >= 2:
            V.wait_ge(s_o.h, o_base + 16 * (g - 1))
        V.tensor_tensor(o1g[:, gpar], c1b[:, gpar], we1g, ALU.mult)
        V.tensor_tensor(o2g[:], c2b[:, gpar], we2g, ALU.mult)
        V.drain()
        V.tensor_tensor(o1g[:, gpar], o1g[:, gpar], cxg[:, gpar], ALU.add)
        V.drain()
        v_f = s_v.inc(V.tensor_tensor(o1g[:, gpar], o1g[:, gpar], o2g[:], ALU.add))
        fin.append(v_f)
        s_o.inc(SY.dma_start(out=out_d[512 * g:512 * (g + 1), :].rearrange("(a p) d -> p a d", p=128),
                             in_=o1g[:, gpar]).wait_op(s_v.h, v_f, "sem-ge"), 16)
    if DBG:
        SY.wait_ge(s_v.h, s_v.n)
        SY.wait_ge(s_g.h, s_g.n)
        s_o.inc(SY.dma_start(out=o_clist, in_=clist[:]), 16)
        s_o.inc(SY.dma_start(out=o_dots, in_=dots_em[:]), 16)
        s_o.inc(SY.dma_start(out=o_pack, in_=packB[:]), 16)
        s_o.inc(SY.dma_start(out=o_gb, in_=gB[:].rearrange("a b c -> a (b c)")), 16)
        s_o.inc(SY.dma_start(out=o_we, in_=we_tm[:].rearrange("a b c -> a (b c)")), 16)
        s_o.inc(SY.dma_start(out=o_con, in_=contrib_in[0:128, :]), 16)
        s_o.inc(SY.dma_start(out=o_yrow, in_=y_all[0:8, :]), 16)
        s_o.inc(SY.dma_start(out=o_sb[:, 0:128], in_=sB1[:]), 16)
        s_o.inc(SY.dma_start(out=o_sb[:, 128:256], in_=sB2[:]), 16)
        s_o.inc(SY.dma_start(out=o_tot[:, 0:8], in_=totB[:]), 16)
        s_o.inc(SY.dma_start(out=o_tot[:, 8:16], in_=pfxT[:]), 16)
        s_o.inc(SY.dma_start(out=o_x1, in_=x1_hbm[2048:2176, :]), 16)
        s_o.inc(SY.dma_start(out=o_arow, in_=a_row[:]), 16)
        s_o.inc(SY.dma_start(out=o_nfs, in_=nf_s[:]), 16)
        s_o.inc(SY.dma_start(out=o_nff[0:1, 0:1], in_=nf_sb[:]), 16)
    SY.wait_ge(s_o.h, s_o.n)
    return nc


def _host_consts():
    c = {}
    c["iden"] = np.eye(128, dtype=np.float32)
    c["onescol"] = np.ones((128, 1), np.float32)
    c["onesrow"] = np.ones((1, 128), np.float32)
    c["triu16"] = np.triu(np.ones((16, 16), np.float32), 1)
    c["triu32"] = np.triu(np.ones((32, 32), np.float32), 1)
    c["iota8"] = np.tile(np.arange(8, dtype=np.float32), (128, 1))
    c["iota8m"] = np.tile(np.arange(8, dtype=np.float32) - 1024.0, (128, 1))
    c["iotaEB"] = np.ascontiguousarray(
        np.tile(np.arange(8, dtype=np.float32)[None, :, None], (32, 1, 128)).reshape(32, -1))
    idxid = np.zeros((16, 256), np.int16)
    for i_ in range(4096):
        idxid[i_ % 16, i_ // 16] = i_
    c["idxid"] = np.tile(idxid, (8, 1))
    iotasl = np.zeros((16, 256), np.float32)
    for j in range(4096):
        iotasl[j % 16, j // 16] = j
    c["iotasl"] = iotasl
    c["m1c16"] = np.full((16, 2048), -1, np.int16)
    c["m1c32"] = np.full((32, 128), -1, np.float32)
    c["z16"] = np.zeros((16, 256), np.int16)
    return c


def kernel(L0_attn_x, L0_attn_w_qkv, L0_attn_w_o, L0_ffn_expert_embeddings,
           L0_ffn_first_linear, L0_ffn_second_linear):
    x = np.ascontiguousarray(np.asarray(L0_attn_x, dtype=np.float32))
    wqkv = np.asarray(L0_attn_w_qkv, dtype=np.float32)
    wo = np.asarray(L0_attn_w_o, dtype=np.float32)
    emb = np.asarray(L0_ffn_expert_embeddings, dtype=np.float32)
    w1 = np.asarray(L0_ffn_first_linear, dtype=np.float32)
    w2 = np.asarray(L0_ffn_second_linear, dtype=np.float32)

    wvT = np.ascontiguousarray(wqkv[2 * D:3 * D, :].T)
    woT = np.ascontiguousarray(wo.T)
    embT = np.ascontiguousarray(emb.T)
    consts = _host_consts()

    import os as _os
    _upto = int(_os.environ.get("K_UPTO", 99))
    if _CACHE.get("upto") != _upto:
        _CACHE.clear()
        _CACHE["upto"] = _upto
        _CACHE["nc"] = build(_upto)
        mybir.codegen_inst_isa_subclasses(_CACHE["nc"])
    nc = _CACHE["nc"]

    xr = x.reshape(T, D)
    in_maps = []
    for c in range(NC):
        iotaG = (4096 * c + 128 * np.arange(32)[:, None] +
                 np.arange(128)[None, :]).astype(np.float32)
        maskc = np.zeros((E, 32), np.float32)
        maskc[:c, :] = 1.0
        idx8 = np.full((128, 1), -1, np.int16)
        for k in range(8):
            for j in range(8):
                idx8[16 * k + j, 0] = 8 * j + c
        m = dict(
            xs=np.ascontiguousarray(xr[TPC * c:TPC * (c + 1)]),
            wvT=wvT, woT=woT, embT=embT,
            w1T=np.ascontiguousarray(w1[c].T), w2T=np.ascontiguousarray(w2[c].T),
            iotaG=np.ascontiguousarray(iotaG), maskc32=maskc, idx8=idx8,
            **consts,
        )
        in_maps.append(m)
    res = run_bass_kernel_spmd(nc, in_maps, core_ids=list(range(NC)))
    out = np.concatenate([np.asarray(r["out"]) for r in res.results], axis=0)
    return out.reshape(B, S, D)
